# version 09
# baseline (speedup 1.0000x reference)
import numpy as np
import ml_dtypes
import concourse.bass as bass
import concourse.mybir as mybir
from concourse.bass_utils import run_bass_kernel_spmd

F32 = mybir.dt.float32
BF16 = mybir.dt.bfloat16
AF = mybir.ActivationFunctionType
ALU = mybir.AluOpType
AXL = mybir.AxisListType
BF = ml_dtypes.bfloat16

D = 1024
T = 2048
NSEQ = 2
NT = 16
WIN = 512
HID = 2730
EPS = 1e-5
NEG_BIG = -1.0e9
CH = 32
NH_RW = 16


def build_constants():
    c = {}
    c["c_I128"] = np.eye(128, dtype=BF)
    c["c_I128f"] = np.eye(128, dtype=np.float32)
    c["c_onesr"] = np.ones((128, 128), dtype=np.float32)
    i = np.arange(128)
    c["c_tri_q"] = np.where(i[None, :] > i[:, None], NEG_BIG, 0.0).astype(BF)
    c["c_tri_p"] = np.where(i[None, :] < i[:, None], NEG_BIG, 0.0).astype(BF)
    dh = 64
    half_idx = np.arange(0, dh, 2, dtype=np.float64)
    inv_freq = 1.0 / (10000.0 ** (half_idx / dh))
    t = np.arange(T, dtype=np.float64)
    freqs = np.concatenate([t[:, None] * inv_freq[None, :]] * 2, axis=-1)
    base = (half_idx + 0.4 * dh) / (1.4 * dh)
    power = (t - T // 2) / float(WIN // 2)
    scale = np.concatenate([base[None, :] ** power[:, None]] * 2, axis=-1)
    sign = np.concatenate([-np.ones(dh // 2), np.ones(dh // 2)])[None, :]
    QK_SCALE = 8.0
    c["c_rot"] = np.stack([
        QK_SCALE * np.cos(freqs) * scale, QK_SCALE * sign * np.sin(freqs) * scale,
        np.cos(freqs) / scale, sign * np.sin(freqs) / scale,
    ], axis=1).astype(np.float32)
    j = np.arange(CH)
    Lin = (j[None, :] <= j[:, None]).astype(np.float64)
    Lex = (j[None, :] < j[:, None]).astype(np.float64)
    ref = (j[None, :] <= 15).astype(np.float64)

    def blockdiag(blk):
        m = np.zeros((128, 128))
        for q in range(4):
            m[q * CH:(q + 1) * CH, q * CH:(q + 1) * CH] = blk.T
        return m.astype(np.float32)

    c["c_Lin"] = blockdiag(-(Lin - ref))
    c["c_Lex"] = blockdiag(-(Lex - ref))
    c["c_Lend"] = blockdiag(-(Lin - 1.0))
    c["c_Lex0"] = blockdiag(-Lex)
    m = np.zeros((128, 128))
    for q in range(4):
        m[q * CH:(q + 1) * CH, q * CH:(q + 1) * CH] = 1.0
    c["c_blko"] = m.astype(np.float32)
    jj = np.arange(CH)
    c["c_triu01"] = np.tile((jj[:, None] < jj[None, :]).astype(np.float32),
                            (1, NH_RW)).astype(BF)
    c["c_diag01"] = np.tile(np.eye(CH, dtype=np.float32), (1, NH_RW)).astype(BF)
    c["c_scal"] = np.tile(np.array([0.0, 1.0, -1.0, 1e-5, 0.5, -0.5, 8.0, 1e-12],
                           dtype=np.float32)[None, :], (128, 1))
    return c


CONSTS = build_constants()


NLAYERS_FFN = 2


def build_program(dumps=(), phases=("A", "B", "C"), bq=4, bstop="full", bng=4, bnc=4):
    dumps = set(dumps)
    nc = bass.Bass("TRN2", target_bir_lowering=False)
    guards = []

    def keep(g):
        t = g.__enter__()
        guards.append(g)
        return t

    def sbuf(name, shape, dtype):
        return keep(nc.sbuf_tensor(name, list(shape), dtype))

    din, dout = {}, {}

    def dram_in(name, shape, dtype=F32):
        din[name] = nc.dram_tensor(name, list(shape), dtype, kind="ExternalInput").ap()
        return din[name]

    def dram_out(name, shape, dtype=F32):
        dout[name] = nc.dram_tensor(name, list(shape), dtype, kind="ExternalOutput").ap()
        return dout[name]

    xin = dram_in("xin", [NSEQ, T, D])
    xout = dram_out("xout", [NSEQ, T, D])
    xscr = nc.dram_tensor("xscr", [T, D], F32, kind="Internal").ap()

    W = {}
    for name, shape in [
        ("attn_mu_x", [D]), ("attn_W_x1", [D, 160]), ("attn_W_x2", [D, 5, 32]),
        ("attn_x_bias", [5, D]), ("attn_W_r", [D, 512]), ("attn_W_w1", [D, 64]),
        ("attn_W_w2", [64, 512]), ("attn_b_w2", [512]), ("attn_W_k", [D, 512]),
        ("attn_W_v", [D, D]), ("attn_W_g", [D, D]), ("attn_u", [16, 32]),
        ("attn_gn_w", [D]), ("attn_gn_b", [D]), ("attn_W_o", [D, D]),
        ("ffn_wqkv", [2, D, 1536]), ("ffn_q_scale", [2, 64]), ("ffn_k_scale", [2, 64]),
        ("ffn_wo", [2, 512, D]), ("ffn_ln_g", [2, D]), ("ffn_ln_b", [2, D]),
        ("ffn_w1", [2, D, 2 * HID]), ("ffn_b1", [2, 2 * HID]),
        ("ffn_w2", [2, HID, D]), ("ffn_b2", [2, D]),
    ]:
        W[name] = dram_in(name, shape)
    C = {}
    for name, arr in CONSTS.items():
        C[name] = dram_in(name, list(arr.shape), BF16 if arr.dtype == BF else F32)

    sb = {}
    sb["s"] = sbuf("s_res", [128, NT, D], F32)
    ARENA_HW = 57 * 1024
    arena = sbuf("arena", [128, ARENA_HW], BF16)

    def av(off_bytes, shape, dtype=BF16):
        es = 2 if dtype == BF16 else 4
        assert off_bytes % 64 == 0
        n = int(np.prod(shape))
        base = arena[:].bitcast(dtype) if dtype != BF16 else arena[:]
        flat = base[:, off_bytes // es: off_bytes // es + n]
        if len(shape) == 1:
            assert shape[0] == n
            return flat
        pat = " ".join(f"d{i}" for i in range(len(shape)))
        kw = {f"d{i}": shape[i] for i in range(len(shape))}
        return flat.rearrange(f"p ({pat}) -> p {pat}", **kw)

    AT = {}
    o = 0
    def _alloc(tag, shape, dtype=BF16):
        nonlocal o
        es = 2 if dtype == BF16 else 4
        n = int(np.prod(shape)) * es
        AT[tag] = (o, shape, dtype)
        o += (n + 63) // 64 * 64
    _alloc("h_fm", [8, 513]); _alloc("sbf", [4, D])
    _alloc("q_tm", [4, 512]); _alloc("k_tm", [4, 512])
    _alloc("qr_tm", [4, 512]); _alloc("kr_tm", [4, 512])
    _alloc("v_tm", [2, 4, 512]); _alloc("q_fm", [4, 512]); _alloc("k_fm", [2, 4, 512])
    _alloc("pexp", [4, 640]); _alloc("pt", [4, 640])
    _alloc("o_tm", [4, 512]); _alloc("o_fm", [4, 4, 128])
    _alloc("wqkv", [8, 1536]); _alloc("wo", [4, D])
    ATT_END = o
    assert ATT_END <= 2 * ARENA_HW, ATT_END

    MT = {}
    o = 0
    def _malloc(tag, shape, dtype=BF16):
        nonlocal o
        es = 2 if dtype == BF16 else 4
        n = int(np.prod(shape)) * es
        MT[tag] = (o, shape, dtype)
        o += (n + 63) // 64 * 64
    _malloc("y_fm", [8, 1025])
    _malloc("u_fm", [22, 1024])
    _malloc("w1t", [2, 8, 128])
    _malloc("w2res", [22, 1024])
    _malloc("p12", [D]); _malloc("b1blk", [128])
    MLP_END = o
    assert MLP_END <= 2 * ARENA_HW, MLP_END

    def at(tag):
        off, shape, dt = AT[tag]
        return av(off, shape, dt)

    BT = {}
    o = 0
    def _balloc(tag, shape, dtype=BF16):
        nonlocal o
        es = 2 if dtype == BF16 else 4
        n = int(np.prod(shape)) * es
        BT[tag] = (o, shape, dtype)
        o += (n + 63) // 64 * 64
    _balloc("xln_fm", [8, 513])
    _balloc("delta", [8, 512]); _balloc("xi", [8, 512])
    _balloc("low", [2, 512]); _balloc("lora", [512]); _balloc("ww2aug", [512])
    _balloc("r_tm", [4, 512]); _balloc("k_tm2", [4, 512]); _balloc("khat", [4, 512])
    _balloc("Fr", [512]); _balloc("Fk", [512]); _balloc("Fd", [512])
    _balloc("v_tm2", [4, D]); _balloc("g_tm", [4, D])
    _balloc("rt_fm", [4, 128]); _balloc("kt_fm", [4, 128])
    _balloc("r0t_fm", [4, 128]); _balloc("Fr0", [512])
    _balloc("rr_fm", [4, 128]); _balloc("ku_fm", [4, 128])
    _balloc("B1", [512]); _balloc("B2", [512])
    _balloc("dcols", [4, 4], F32); _balloc("S0", [4, 64], F32)
    _balloc("S0b", [4, 64])
    _balloc("Stmp", [4, 64], F32)
    _balloc("t1sg_fm", [2, 8, 128])
    _balloc("wsl", [2, 8, D])
    _balloc("Lc", [5, 128], F32)
    _balloc("masks", [2, 512])
    RWK_END = o
    assert RWK_END <= 2 * ARENA_HW, RWK_END

    def bt(tag):
        if tag == "xln_bf":
            off, _shape, dt = BT["v_tm2"]
            return av(off, [4, D], BF16)
        off, shape, dt = BT[tag]
        return av(off, shape, dt)

    def mt(tag):
        off, shape, dt = MT[tag]
        if tag == "b1blk":
            return arena[0:44, off // 2: off // 2 + 128]
        return av(off, shape, dt)

    def srow_seg(which, j, a, b):
        p = 0 if which == "s" else 64
        return av(MT["u_fm"][0], [8320], F32)[p:p + 1, 2080 * j + a: 2080 * j + b]

    sb["I"] = sbuf("I128", [128, 128], BF16)
    sb["If"] = sbuf("I128f", [128, 128], F32)
    sb["onesr"] = sbuf("onesr", [128, 128], F32)
    sb["tri_pq"] = sbuf("tri_pq", [128, 2, 128], BF16)
    sb["rot"] = sbuf("rot", [128, 4, 4, 64], F32)
    sb["xp"] = sbuf("xpair", [128, 2, D], F32)
    sb["scr"] = sbuf("scr", [128, 512], F32)
    sb["scr2"] = sbuf("scr2", [128, 512], F32)
    sb["st4"] = sbuf("st4", [128, NT, 4], F32)
    sb["mcols"] = sbuf("mcols", [128, NT, 4], F32)
    sb["ssq"] = sbuf("ssq", [128, 4, 8], F32)
    sb["ssk"] = sbuf("ssk", [128, 4, 8], F32)
    sb["qsb"] = sbuf("qsb", [128, 2, 64], F32)
    sb["qsrow"] = sbuf("qsrow", [1, 2, 64], F32)
    sb["rz"] = sbuf("rz", [128, 2, 8], F32)
    sb["mx"] = sbuf("mx", [128, 4], F32)
    sb["halo"] = sbuf("halo", [128, 8, 1], BF16)
    sb["gg"] = sbuf("gg", [128, 512], BF16)
    sb["cpcol"] = sbuf("cpcol", [128, 2, 44], F32)
    sb["lbcol"] = sbuf("lbcol", [128, 8], BF16)
    sb["lgcol"] = sbuf("lgcol", [128, 8], F32)
    sb["hrow"] = sbuf("hrow", [1, 4], F32)
    sb["lnrow"] = sbuf("lnrow", [1, D], F32)
    sb["wx2t"] = sbuf("wx2t", [128, 2, 8, 128], BF16)
    sb["bcols"] = sbuf("bcols", [128, 70], F32)
    sb["halo_b"] = sbuf("halo_b", [128, 8, 1], BF16)
    sb["cpool"] = sbuf("cpool", [128, 8], F32)
    for _ci, _cv in enumerate([0.0, 1.0, -1.0, 1e-5, 0.5, -0.5, 8.0, 1e-12]):
        nc.const_aps.register(F32, _cv, sb["cpool"][:, _ci:_ci + 1])

    ps = keep(nc.psum_tensor("ps", [128, 4096], F32))

    def pb(bi, off=0, width=512):
        return ps[:, 512 * bi + off: 512 * bi + off + width]

    block = keep(nc.Block())

    sems, count = {}, {}
    for nm in (["pe", "dve", "act", "pool"]
               + [f"qs{i}" for i in range(4)] + [f"qg{i}" for i in range(4)]):
        sems[nm] = keep(nc.semaphore("sem_" + nm))
        count[nm] = 0
    pool_idx = {"qs": 0, "qg": 0}
    waited = {e: {s: 0 for s in sems} for e in
              ["tensor", "vector", "scalar", "gpsimd", "sync"]}

    def inc(ins, nm):
        step = 16 if nm[0] == "q" else 1
        count[nm] += step
        if ins is not None:
            ins.then_inc(sems[nm], step)
        return (nm, count[nm])

    def dq(eng, stream, pool, out, in_, **kw):
        nm = f"{pool}{pool_idx[pool] % 4}"
        pool_idx[pool] += 1
        if count[nm] > 0:
            wait(eng, stream, (nm, count[nm]))
        ins = eng.dma_start(out=out, in_=in_, **kw)
        return inc(ins, nm)

    def wait(eng, stream, *toks):
        best = {}
        for t in toks:
            if t is None:
                continue
            ts = t if isinstance(t, list) else [t]
            for u in ts:
                if u is not None:
                    best[u[0]] = max(best.get(u[0], 0), u[1])
        for s, v in best.items():
            if v > waited[stream][s]:
                eng.wait_ge(sems[s], v)
                waited[stream][s] = v

    bank_w = [None] * 8
    bank_r = [[] for _ in range(8)]

    def pe_banks(te, bis):
        toks = []
        for bi in bis:
            toks += bank_r[bi]
        wait(te, "tensor", toks)

    def banks_written(tok, bis):
        for bi in bis:
            bank_w[bi] = tok
            bank_r[bi] = []

    def read_banks(eng, stream, bis):
        wait(eng, stream, [bank_w[bi] for bi in bis])

    def banks_read(tok, bis):
        for bi in bis:
            bank_r[bi].append(tok)

    def TE(fn):
        block.tensor(fn)

    def VE(fn):
        block.vector(fn)

    def SE(fn):
        block.scalar(fn)

    def GE(fn):
        block.gpsimd(fn)

    def SY(fn):
        block.sync(fn)

    final_toks = []
    dumped = set()
    xp_last = {0: None, 1: None}

    def dump(name, ap, tok):
        if name not in dumps or name in dumped:
            return None
        dumped.add(name)
        dd = dram_out("dump_" + name, list(ap.shape),
                      BF16 if ap.dtype == BF16 else F32)
        out = [None]

        def _(sy):
            wait(sy, "sync", tok)
            out[0] = dq(sy, "sync", "qs", dd[:], ap)
            final_toks.append(out[0])
        SY(_)
        return out[0]

    def transpose(te, lhsT, out_ps):
        kk = lhsT.shape[0]
        return te.matmul(out_ps, lhsT, sb["I"][0:kk, 0:kk], start=True, stop=True)

    init_tok = [None]

    def _(sy):
        tks = [dq(sy, "sync", "qs", sb["I"][:], C["c_I128"][:]),
               dq(sy, "sync", "qs", sb["If"][:], C["c_I128f"][:]),
               dq(sy, "sync", "qs", sb["onesr"][:], C["c_onesr"][:]),
               dq(sy, "sync", "qs", sb["tri_pq"][:, 1, :], C["c_tri_q"][:]),
               dq(sy, "sync", "qs", sb["tri_pq"][:, 0, :], C["c_tri_p"][:]),
               dq(sy, "sync", "qs", sb["cpool"][:], C["c_scal"][:])]
        wait(sy, "sync", tks)
        init_tok[0] = tks
    SY(_)

    def _(te):
        wait(te, "tensor", init_tok[0])
    TE(_)

    def _(v):
        wait(v, "vector", init_tok[0])
    VE(_)

    def _(s):
        wait(s, "scalar", init_tok[0])
    SE(_)

    def _(g):
        wait(g, "gpsimd", init_tok[0])
    GE(_)

    zeros_tok = [None]

    def _(v):
        ins = v.memset(sb["halo"][:], 0.0)
        zeros_tok[0] = inc(ins, "dve")
    VE(_)

    def load_rot(seq_w):
        tok = [None]

        def _(sy):
            tok[0] = dq(sy, "sync", "qs", sb["rot"][:],
                        C["c_rot"][:].rearrange("(n p) f d -> n p f d", p=128)[
                            4 * seq_w:4 * seq_w + 4].rearrange("n p f d -> p n f d"))
        SY(_)
        return tok[0]

    def ln_from_dram(src_ap, s_free_toks, dump_tag=None):
        ld = []
        for t in range(NT):
            def _(sy, t=t):
                wait(sy, "sync", xp_last[t % 2])
                ld.append(dq(sy, "sync", "qs", sb["xp"][:, t % 2, :],
                             src_ap.rearrange("(n p) d -> n p d", p=128)[t]))
            SY(_)
            if t % 2 == 0:
                continue
            t0, t1 = t - 1, t

            def _(v, t0=t0, t1=t1):
                wait(v, "vector", ld[t0], ld[t1], s_free_toks)
                for tt in (t0, t1):
                    v.reduce_sum(sb["st4"][:, tt, 0:1], sb["xp"][:, tt % 2, :],
                                 AXL.X)
                for tt in (t0, t1):
                    st = sb["st4"][:, tt, :]
                    v.tensor_scalar(st[:, 0:1], st[:, 0:1], 1.0 / D, None,
                                    ALU.mult)
                for tt in (t0, t1):
                    st = sb["st4"][:, tt, :]
                    v.scalar_tensor_tensor(out=st[:, 3:4], in0=st[:, 0:1],
                                           scalar=-1.0, in1=st[:, 0:1],
                                           op0=ALU.mult, op1=ALU.mult)
                for tt in (t0, t1):
                    st = sb["st4"][:, tt, :]
                    ins = v.tensor_scalar(st[:, 3:4], st[:, 3:4], EPS, None,
                                          ALU.add)
                inc(ins, "dve")
            VE(_)
            t_pre = ("dve", count["dve"])

            def _(s, t0=t0, t1=t1):
                wait(s, "scalar", t_pre)
                for tt in (t0, t1):
                    st = sb["st4"][:, tt, :]
                    s.activation(sb["s"][:, tt, :], sb["xp"][:, tt % 2, :],
                                 AF.Square, accum_out=st[:, 1:2])
                    ins = s.activation(st[:, 1:2], st[:, 1:2], AF.Sqrt,
                                       bias=st[:, 3:4], scale=1.0 / D)
                inc(ins, "act")
            SE(_)
            t_sq = ("act", count["act"])

            def _(v, t0=t0, t1=t1):
                wait(v, "vector", t_sq)
                for tt in (t0, t1):
                    v.reciprocal(sb["st4"][:, tt, 1:2], sb["st4"][:, tt, 1:2])
                v.tensor_copy(sb["scr"][:, 0:1], sb["st4"][:, t0, 0:1])
                for tt in (t0, t1):
                    st = sb["st4"][:, tt, :]
                    ins = v.tensor_scalar(sb["s"][:, tt, :],
                                          sb["xp"][:, tt % 2, :],
                                          st[:, 0:1], st[:, 1:2],
                                          ALU.subtract, ALU.mult)
                    xp_last[tt % 2] = inc(ins, "dve")
                    ln_from_dram.cons[tt] = xp_last[tt % 2]
            VE(_)
        out = [ln_from_dram.cons[t] for t in range(NT)]
        if dump_tag:
            dump(dump_tag, sb["s"][:], out)
            dump(dump_tag + "_st", sb["st4"][:], out)
        return out
    ln_from_dram.cons = {}

    def add_half_s_writeback(src_ap, dst_ap, s_toks, dump_tag=None):
        st_toks = []
        ld = []
        for t in range(NT):
            def _(sy, t=t):
                wait(sy, "sync", xp_last[t % 2])
                ld.append(dq(sy, "sync", "qs", sb["xp"][:, t % 2, :],
                             src_ap.rearrange("(n p) d -> n p d", p=128)[t]))
            SY(_)

            def _(v, t=t):
                wait(v, "vector", ld[t], s_toks[t] if isinstance(s_toks, list) else s_toks)
                ins = v.scalar_tensor_tensor(
                    out=sb["xp"][:, t % 2, :], in0=sb["s"][:, t, :], scalar=0.5,
                    in1=sb["xp"][:, t % 2, :], op0=ALU.mult, op1=ALU.add)
                inc(ins, "dve")
            VE(_)
            tadd = ("dve", count["dve"])
            add_half_s_writeback.s_readers.append(tadd)

            def _(sy, t=t):
                wait(sy, "sync", tadd)
                stok = dq(sy, "sync", "qs",
                          dst_ap.rearrange("(n p) d -> n p d", p=128)[t],
                          sb["xp"][:, t % 2, :])
                st_toks.append(stok)
                xp_last[t % 2] = stok
            SY(_)
        return st_toks
    add_half_s_writeback.s_readers = []

    state = {"cp_done": [False, False], "cp_tok": [None, None]}

    def load_ffn_attn_weights(layer, free_toks):
        toks = {}

        def _(g):
            wait(g, "gpsimd", free_toks)
            dq(g, "gpsimd", "qg", sb["qsrow"][0:1, 0, :],
               W["ffn_q_scale"][layer][None, :])
            toks["qsrow"] = dq(g, "gpsimd", "qg", sb["qsrow"][0:1, 1, :],
                               W["ffn_k_scale"][layer][None, :])
            wq = W["ffn_wqkv"][layer].rearrange("(c p) n -> p c n", p=128)
            for c in range(8):
                toks[("wqkv", c)] = dq(g, "gpsimd", "qg",
                                       at("wqkv")[:, c, :], wq[:, c, :])
            wor = W["ffn_wo"][layer].rearrange("(c p) n -> p c n", p=128)
            for c in range(4):
                toks[("wo", c)] = dq(g, "gpsimd", "qg",
                                     at("wo")[:, c, :], wor[:, c, :])
            toks["wqkv"] = [toks[("wqkv", c)] for c in range(8)]
            toks["wo"] = [toks[("wo", c)] for c in range(4)]
        GE(_)
        prevc = [None]
        for j, (nm, dst, ddt) in enumerate([("ffn_ln_g", "lgcol", F32),
                                            ("ffn_ln_b", "lbcol", BF16)]):
            rt = [None]

            def _(g, nm=nm):
                wait(g, "gpsimd", prevc[0])
                rt[0] = dq(g, "gpsimd", "qg", sb["lnrow"][0:1, :],
                           W[nm][layer][None, :])
            GE(_)

            def _(te):
                wait(te, "tensor", rt[0])
                pe_banks(te, [7])
                for c in range(8):
                    ins = te.matmul(pb(7, c, 1),
                                    sb["lnrow"][0:1, 128 * c:128 * (c + 1)],
                                    sb["onesr"][0:1, 0:1], start=True, stop=True)
                banks_written(inc(ins, "pe"), [7])
            TE(_)

            def _(v, dst=dst):
                read_banks(v, "vector", [7])
                ins = v.tensor_copy(sb[dst][:], pb(7, 0, 8))
                tk = inc(ins, "dve")
                banks_read(tk, [7])
                prevc[0] = tk
                toks["lncols"] = tk
            VE(_)

        def _(te):
            wait(te, "tensor", toks["qsrow"])
            pe_banks(te, [6])
            te.matmul(pb(6, 0, 64), sb["onesr"][0:1, :], sb["qsrow"][0:1, 0, :],
                      start=True, stop=True)
            ins = te.matmul(pb(6, 64, 64), sb["onesr"][0:1, :], sb["qsrow"][0:1, 1, :],
                            start=True, stop=True)
            banks_written(inc(ins, "pe"), [6])
        TE(_)
        qtok = [None]

        def _(v):
            read_banks(v, "vector", [6])
            ins = v.tensor_copy(sb["qsb"][:].rearrange("p a d -> p (a d)"),
                                pb(6, 0, 128))
            tk = inc(ins, "dve")
            banks_read(tk, [6])
            qtok[0] = tk
        VE(_)
        toks["qsb"] = qtok[0]
        return toks

    def ffn_attn_window(seq, layer, w, wtoks, s_toks, attn_state):
        cur = w % 2
        prv = 1 - cur
        tiles = [4 * w + i for i in range(4)]
        rot_tok = load_rot_win(w, attn_state.get("rot_free"))

        t_sh = [None]

        def _(v):
            wait(v, "vector", [s_toks[t] for t in tiles],
                 attn_state.get("kv_free"), attn_state.get("hfm_done"))
            if w > 0:
                v.tensor_copy(sb["halo"][:], at("h_fm")[:, :, 512:513])
            else:
                v.memset(sb["halo"][:], 0.0)
            for i in range(4):
                ins = v.tensor_copy(at("sbf")[:, i, :], sb["s"][:, tiles[i], :])
            t_sh[0] = inc(ins, "dve")
        VE(_)

        hfm_toks = []
        for i in range(4):
            hb0 = 6 if i % 2 == 0 else 4

            def _(te, i=i, hb0=hb0):
                wait(te, "tensor", t_sh[0])
                pe_banks(te, [hb0, hb0 + 1])
                for c in range(8):
                    ins = transpose(te, at("sbf")[:, i, 128 * c:128 * (c + 1)],
                                    ps[:, 512 * hb0 + 128 * c:
                                       512 * hb0 + 128 * (c + 1)])
                banks_written(inc(ins, "pe"), [hb0, hb0 + 1])
            TE(_)

            def _(s, i=i, hb0=hb0):
                read_banks(s, "scalar", [hb0, hb0 + 1])
                ins = s.activation(at("h_fm")[:, :, 1 + 128 * i:1 + 128 * (i + 1)],
                                   ps[:, 512 * hb0:512 * hb0 + 1024]
                                   .rearrange("p (c n) -> p c n", c=8), AF.Copy)
                tk = inc(ins, "act")
                banks_read(tk, [hb0, hb0 + 1])
                hfm_toks.append(tk)
            SE(_)

        def _(v):
            ins = v.tensor_copy(at("h_fm")[:, :, 0:1], sb["halo"][:])
            hfm_toks.append(inc(ins, "dve"))
        VE(_)
        attn_state["hfm_done"] = list(hfm_toks)

        prep_toks = []
        for i in range(4):
            qb = 0 if i % 2 == 0 else 3

            def _(te, i=i, qb=qb):
                wait(te, "tensor", hfm_toks)
                pe_banks(te, [qb, qb + 1, qb + 2])
                for c in range(8):
                    if i <= 1:
                        wait(te, "tensor", wtoks[("wqkv", c)])
                    off = (1 if c < 4 else 0) + 128 * i
                    for part in range(3):
                        ins = te.matmul(pb(qb + part),
                                        at("h_fm")[:, c, off:off + 128],
                                        at("wqkv")[:, c, 512 * part:512 * (part + 1)],
                                        start=(c == 0), stop=(c == 7))
                banks_written(inc(ins, "pe"), [qb, qb + 1, qb + 2])
            TE(_)

            def _(s, i=i, qb=qb):
                read_banks(s, "scalar", [qb, qb + 1])
                s.activation(at("q_tm")[:, i, :], pb(qb), AF.Copy)
                ins = s.activation(at("k_tm")[:, i, :], pb(qb + 1), AF.Copy)
                banks_read(inc(ins, "act"), [qb, qb + 1])
            SE(_)

            def _(v, i=i, qb=qb):
                read_banks(v, "vector", [qb, qb + 1, qb + 2])
                wait(v, "vector", ("act", count["act"]))
                v.tensor_copy(at("v_tm")[:, cur, i, :], pb(qb + 2))
                v.tensor_tensor(sb["scr"][:], at("q_tm")[:, i, :],
                                at("q_tm")[:, i, :], ALU.mult)
                v.tensor_reduce(sb["ssq"][:, i, :],
                                sb["scr"][:].rearrange("p (h d) -> p h d", h=8),
                                AXL.X, ALU.add)
                v.tensor_tensor(sb["scr"][:], at("k_tm")[:, i, :],
                                at("k_tm")[:, i, :], ALU.mult)
                ins = v.tensor_reduce(sb["ssk"][:, i, :],
                                      sb["scr"][:].rearrange("p (h d) -> p h d", h=8),
                                      AXL.X, ALU.add)
                tk = inc(ins, "dve")
                banks_read(tk, [qb, qb + 1, qb + 2])
                prep_toks.append(tk)
            VE(_)

        if f"qkv_w{w}" in dumps and seq == 0 and layer == 0:
            dump(f"q_raw_w{w}", at("q_tm")[:], prep_toks)
            dump(f"k_raw_w{w}", at("k_tm")[:], prep_toks)
            dump(f"v_raw_w{w}", at("v_tm")[:, cur], prep_toks)

        def _(s):
            wait(s, "scalar", prep_toks)
            for i in range(4):
                s.activation(sb["ssq"][:, i, :], sb["ssq"][:, i, :], AF.Sqrt)
                ins = s.activation(sb["ssk"][:, i, :], sb["ssk"][:, i, :], AF.Sqrt)
            inc(ins, "act")
        SE(_)
        t_sq = ("act", count["act"])

        def _(v):
            wait(v, "vector", t_sq)
            for nm in ("ssq", "ssk"):
                for i in range(4):
                    v.tensor_scalar(sb[nm][:, i, :], sb[nm][:, i, :], 1e-12, None,
                                    ALU.max)
            for nm in ("ssq", "ssk"):
                for i in range(4):
                    ins = v.reciprocal(sb[nm][:, i, :], sb[nm][:, i, :])
            inc(ins, "dve")
        VE(_)
        t_rs = ("dve", count["dve"])

        def qk_prep(e, nm, rn, ssn, qi, ci, si, scrn, iset=(0, 1, 2, 3)):
            for i in iset:
                src3 = at(nm)[:, i, :].rearrange("p (h d) -> p h d", h=8)
                dst = at(rn)[:, i, :]
                dst3 = dst.rearrange("p (h d) -> p h d", h=8)
                e.tensor_tensor(src3, src3,
                                sb[ssn][:, i, :, None].broadcast_to([128, 8, 64]),
                                ALU.mult)
                e.tensor_tensor(src3, src3,
                                sb["qsb"][:, None, qi, :].broadcast_to([128, 8, 64]),
                                ALU.mult)
                e.tensor_tensor(dst3, src3,
                                sb["rot"][:, i, None, ci, :].broadcast_to([128, 8, 64]),
                                ALU.mult)
                rv = at(nm)[:, i, :].rearrange("p (h a d) -> p h a d", h=8,
                                               a=2)[:, :, ::-1, :]
                e.tensor_tensor(sb[scrn][:].rearrange("p (h a d) -> p h a d", h=8, a=2),
                                rv,
                                sb["rot"][:, i, None, si, :]
                                .rearrange("p o (a d) -> p o a d", a=2)
                                .broadcast_to([128, 8, 2, 32]), ALU.mult)
                ins = e.tensor_tensor(dst, dst, sb[scrn][:], ALU.add)
            return ins

        tq = [None]
        tk_ = [None]

        def _(v):
            wait(v, "vector", t_rs, wtoks["qsb"], rot_tok)
            qk_prep(v, "q_tm", "qr_tm", "ssq", 0, 0, 1, "scr")
            tq[0] = inc(qk_prep(v, "k_tm", "kr_tm", "ssk", 1, 2, 3, "scr",
                                iset=(0, 1, 2)), "dve")
        VE(_)

        def _(g):
            wait(g, "gpsimd", t_rs, wtoks["qsb"], rot_tok)
            tk_[0] = inc(qk_prep(g, "k_tm", "kr_tm", "ssk", 1, 2, 3, "scr2",
                                 iset=(3,)), "pool")
        GE(_)
        attn_state["rot_free"] = [tq[0], tk_[0]]

        if f"qkv_w{w}" in dumps and seq == 0 and layer == 0:
            dump(f"qr_w{w}", at("qr_tm")[:], [tq[0]])
            dump(f"kr_w{w}", at("kr_tm")[:], [tk_[0]])

        fm_toks = []
        for p in range(4):
            fb = 6 if p % 2 == 0 else 4

            def _(te, p=p, fb=fb):
                wait(te, "tensor", tq[0], tk_[0])
                pe_banks(te, [fb, fb + 1])
                for i in range(4):
                    transpose(te, at("qr_tm")[:, i, 128 * p:128 * (p + 1)],
                              ps[:, 512 * fb + 128 * i:512 * fb + 128 * (i + 1)])
                for i in range(4):
                    ins = transpose(te, at("kr_tm")[:, i, 128 * p:128 * (p + 1)],
                                    ps[:, 512 * (fb + 1) + 128 * i:
                                       512 * (fb + 1) + 128 * (i + 1)])
                banks_written(inc(ins, "pe"), [fb, fb + 1])
            TE(_)

            def _(s, p=p, fb=fb):
                read_banks(s, "scalar", [fb, fb + 1])
                s.activation(at("q_fm")[:, p, :], pb(fb), AF.Copy)
                ins = s.activation(at("k_fm")[:, cur, p, :], pb(fb + 1), AF.Copy)
                tk2 = inc(ins, "act")
                banks_read(tk2, [fb, fb + 1])
                fm_toks.append(tk2)
            SE(_)

        if w == 1 and seq == 0 and layer == 0:
            dump("qfm_w1", at("q_fm")[:], fm_toks)
            dump("kfm_w1", at("k_fm")[:, cur], fm_toks)
            dump("kfmprev_w1", at("k_fm")[:, prv], fm_toks)
            dump("vtmprev_w1", at("v_tm")[:, prv], fm_toks)
            dump("vtmcur_w1", at("v_tm")[:, cur], fm_toks)

        otile_toks = []
        pv_toks = []
        for i in range(4):
            njt = 5 if w > 0 else i + 1
            msk_tok = {}

            def kf_vt(j, h, i=i):
                p = h // 2
                jt = tiles[i] - 4 + j if w > 0 else j
                jloc = jt - 4 * (w - 1)
                if w == 0:
                    return (at("k_fm")[:, cur, p, 128 * jt:128 * (jt + 1)],
                            at("v_tm")[:, cur, jt, 64 * h:64 * (h + 1)])
                if jloc >= 4:
                    return (at("k_fm")[:, cur, p, 128 * (jloc - 4):128 * (jloc - 3)],
                            at("v_tm")[:, cur, jloc - 4, 64 * h:64 * (h + 1)])
                return (at("k_fm")[:, prv, p, 128 * jloc:128 * (jloc + 1)],
                        at("v_tm")[:, prv, jloc, 64 * h:64 * (h + 1)])

            for h in range(8):
                p = h // 2
                hb = 64 * (h % 2)
                sreg = h % 4
                sbk = [0, 1] if h % 2 == 0 else [2, 3]

                def _(te, i=i, h=h, p=p, hb=hb, sbk=sbk, njt=njt):
                    wait(te, "tensor", fm_toks)
                    pe_banks(te, sbk)
                    base = 512 * sbk[0]
                    q = at("q_fm")[hb:hb + 64, p, 128 * i:128 * (i + 1)]
                    if w == 0:
                        ins = te.matmul(ps[:, base:base + 128 * njt], q,
                                        at("k_fm")[hb:hb + 64, cur, p, 0:128 * njt],
                                        start=True, stop=True)
                    else:
                        s0 = 128 * (4 - i)
                        ins = te.matmul(ps[:, base:base + s0], q,
                                        at("k_fm")[hb:hb + 64, prv, p, 128 * i:512],
                                        start=True, stop=True)
                        if s0 < 512:
                            ins = te.matmul(ps[:, base + s0:base + 512], q,
                                            at("k_fm")[hb:hb + 64, cur, p,
                                                       0:512 - s0],
                                            start=True, stop=True)
                        ins = te.matmul(ps[:, base + 512:base + 640], q,
                                        at("k_fm")[hb:hb + 64, cur, p,
                                                   512 - s0:640 - s0],
                                        start=True, stop=True)
                    banks_written(inc(ins, "pe"), sbk)
                TE(_)

                def _(v, i=i, sbk=sbk, njt=njt, sreg=sreg):
                    read_banks(v, "vector", sbk)
                    base = 512 * sbk[0]
                    if w > 0:
                        v.tensor_tensor(ps[:, base:base + 128], ps[:, base:base + 128],
                                        sb["tri_pq"][:, 0, :], ALU.add)
                    v.tensor_tensor(ps[:, base + 128 * (njt - 1):base + 128 * njt],
                                    ps[:, base + 128 * (njt - 1):base + 128 * njt],
                                    sb["tri_pq"][:, 1, :], ALU.add)
                    ins = v.reduce_max(sb["mx"][:, sreg:sreg + 1],
                                       ps[:, base:base + 128 * njt], AXL.X, negate=True)
                    inc(ins, "dve")
                VE(_)
                msk_tok[h] = ("dve", count["dve"])

                if h % 2 == 0:
                    continue
                h0 = h - 1
                s0, s1 = h0 % 4, h % 4

                def _(s, i=i, h0=h0, h1=h, njt=njt, s0=s0, s1=s1):
                    wait(s, "scalar", msk_tok[h0], msk_tok[h1],
                         attn_state.get(("pexp_free", s0)),
                         attn_state.get(("pexp_free", s1)),
                         attn_state.get(("rz_free", i % 2)) if h0 == 0 else None)
                    s.activation(at("pexp")[:, s0, 0:128 * njt],
                                 ps[:, 0:128 * njt],
                                 AF.Exp, bias=sb["mx"][:, s0:s0 + 1],
                                 scale=1.0,
                                 accum_out=sb["rz"][:, i % 2, h0:h0 + 1])
                    ins = s.activation(at("pexp")[:, s1, 0:128 * njt],
                                       ps[:, 512 * 2:512 * 2 + 128 * njt],
                                       AF.Exp, bias=sb["mx"][:, s1:s1 + 1],
                                       scale=1.0,
                                       accum_out=sb["rz"][:, i % 2, h1:h1 + 1])
                    tk3 = inc(ins, "act")
                    banks_read(tk3, [0, 1, 2, 3])
                SE(_)
                t_exp = ("act", count["act"])
                if w == 1 and i == 0 and h == 1 and seq == 0 and layer == 0:
                    dump("pexp_w1i0h0", at("pexp")[:, 0, :], t_exp)

                for hh in (h0, h):
                    sregh = hh % 4

                    def _(te, sregh=sregh, njt=njt):
                        wait(te, "tensor", t_exp)
                        pe_banks(te, [5, 6] if njt == 5 else [5])
                        for j in range(njt):
                            dst = (ps[:, 512 * 5 + 128 * j:512 * 5 + 128 * (j + 1)]
                                   if j < 4 else pb(6, 0, 128))
                            ins = transpose(te,
                                            at("pexp")[:, sregh, 128 * j:128 * (j + 1)],
                                            dst)
                        tk = inc(ins, "pe")
                        banks_written(tk, [5, 6] if njt == 5 else [5])
                        attn_state[("pexp_free", sregh)] = tk
                    TE(_)

                    def _(v, sregh=sregh, njt=njt):
                        read_banks(v, "vector", [5, 6] if njt == 5 else [5])
                        wait(v, "vector", attn_state.get(("pt_free", sregh)))
                        if njt == 5:
                            v.tensor_copy(at("pt")[:, sregh, 0:512], pb(5))
                            ins = v.tensor_copy(at("pt")[:, sregh, 512:640],
                                                pb(6, 0, 128))
                        else:
                            ins = v.tensor_copy(at("pt")[:, sregh, 0:128 * njt],
                                                pb(5, 0, 128 * njt))
                        tk4 = inc(ins, "dve")
                        banks_read(tk4, [5, 6] if njt == 5 else [5])
                    VE(_)
                    t_pt = ("dve", count["dve"])
                    if w == 1 and i == 0 and hh == 0 and seq == 0 and layer == 0:
                        dump("pt_w1i0h0", at("pt")[:, sregh, :], t_pt)

                    def _(te, i=i, hh=hh, sregh=sregh, njt=njt, t_pt=t_pt):
                        wait(te, "tensor", t_pt)
                        if hh == 0:
                            pe_banks(te, [4])
                        for j in range(njt):
                            _kf, vt = kf_vt(j, hh)
                            ins = te.matmul(pb(4, 64 * hh, 64),
                                            at("pt")[:, sregh, 128 * j:128 * (j + 1)],
                                            vt, start=(j == 0), stop=(j == njt - 1))
                        tk = inc(ins, "pe")
                        attn_state[("pt_free", sregh)] = tk
                        pv_toks.append(tk)
                        if hh == 7:
                            banks_written(tk, [4])
                    TE(_)

            def _(v, i=i):
                read_banks(v, "vector", [4])
                v.reciprocal(sb["rz"][:, i % 2, :], sb["rz"][:, i % 2, :])
                v.tensor_copy(sb["scr"][:, 0:1], sb["rz"][:, i % 2, 0:1])
                ins = v.tensor_tensor(
                    at("o_tm")[:, i, :].rearrange("p (h d) -> p h d", h=8),
                    pb(4).rearrange("p (h d) -> p h d", h=8),
                    sb["rz"][:, i % 2, :, None].broadcast_to([128, 8, 64]), ALU.mult)
                tk5 = inc(ins, "dve")
                banks_read(tk5, [4])
                otile_toks.append(tk5)
                attn_state[("rz_free", i % 2)] = tk5
            VE(_)

        if f"qkv_w{w}" in dumps and seq == 0 and layer == 0:
            dump(f"o_w{w}", at("o_tm")[:], otile_toks)
        attn_state["kv_free"] = list(pv_toks[-8:])

        s_upd = []
        for i in range(4):
            ob = 5 if i % 2 == 0 else 6
            wb = 2 if i % 2 == 0 else 0

            def _(te, i=i, ob=ob):
                wait(te, "tensor", otile_toks[i])
                pe_banks(te, [ob])
                for c in range(4):
                    ins = transpose(te, at("o_tm")[:, i, 128 * c:128 * (c + 1)],
                                    ps[:, 512 * ob + 128 * c:512 * ob + 128 * (c + 1)])
                banks_written(inc(ins, "pe"), [ob])
            TE(_)

            def _(s, i=i, ob=ob):
                read_banks(s, "scalar", [ob])
                ins = s.activation(at("o_fm")[:, i % 2],
                                   pb(ob).rearrange("p (c n) -> p c n", c=4), AF.Copy)
                tk6 = inc(ins, "act")
                banks_read(tk6, [ob])
            SE(_)
            t_ofm = ("act", count["act"])

            def _(te, i=i, wb=wb):
                wait(te, "tensor", t_ofm)
                pe_banks(te, [wb, wb + 1])
                for c in range(4):
                    if i == 0 and w == 0:
                        wait(te, "tensor", wtoks[("wo", c)])
                    for half in range(2):
                        ins = te.matmul(pb(wb + half), at("o_fm")[:, i % 2, c, :],
                                        at("wo")[:, c, 512 * half:512 * (half + 1)],
                                        start=(c == 0), stop=(c == 3))
                banks_written(inc(ins, "pe"), [wb, wb + 1])
            TE(_)

            def _(v, i=i, wb=wb):
                read_banks(v, "vector", [wb, wb + 1])
                t = tiles[i]
                v.tensor_tensor(sb["s"][:, t, 0:512], sb["s"][:, t, 0:512], pb(wb),
                                ALU.add)
                ins = v.tensor_tensor(sb["s"][:, t, 512:1024], sb["s"][:, t, 512:1024],
                                      pb(wb + 1), ALU.add)
                tk7 = inc(ins, "dve")
                banks_read(tk7, [wb, wb + 1])
                s_upd.append(tk7)
            VE(_)

        return s_upd

    def load_rot_win(w, rot_free):
        tok = [None]

        def _(sy):
            wait(sy, "sync", rot_free)
            tok[0] = dq(sy, "sync", "qs", sb["rot"][:],
                        C["c_rot"][:].rearrange("(n p) f d -> n p f d", p=128)[
                            4 * w:4 * w + 4].rearrange("n p f d -> p n f d"))
        SY(_)
        return tok[0]

    def load_mlp_weights(layer, arena_free, need_b1):
        toks = {}

        def _(v):
            wait(v, "vector", arena_free)
            v.memset(mt("w2res")[32:64, 21, :], 0.0)
            ins = v.memset(mt("w2res")[64:128, 21, :], 0.0)
            toks["w2z"] = inc(ins, "dve")
        VE(_)

        if need_b1:
            def _(v):
                wait(v, "vector", arena_free)
                ins = v.memset(mt("b1blk"), 0.0)
                toks["b1z"] = inc(ins, "dve")
            VE(_)

        def _(g):
            wait(g, "gpsimd", arena_free, toks["w2z"], toks.get("b1z"))
            w2 = W["ffn_w2"][layer]
            toks["w2a"] = dq(g, "gpsimd", "qg", mt("w2res")[:, 0:21, :],
                             w2[0:2688].rearrange("(c p) n -> p c n", p=128))
            toks["w2b"] = dq(g, "gpsimd", "qg", mt("w2res")[0:42, 21, :],
                             w2[2688:2730])
            toks["b2"] = dq(g, "gpsimd", "qg", mt("w2res")[64:65, 21, :],
                            W["ffn_b2"][layer][None, :])
            if need_b1:
                b1 = W["ffn_b1"][layer]
                dq(g, "gpsimd", "qg", mt("b1blk")[0:21, :],
                   b1[0:2688].rearrange("(m n) -> m n", n=128))
                dq(g, "gpsimd", "qg", mt("b1blk")[21:22, 0:42], b1[2688:2730][None, :])
                dq(g, "gpsimd", "qg", mt("b1blk")[22:43, :],
                   b1[2730:5418].rearrange("(m n) -> m n", n=128))
                toks["b1"] = dq(g, "gpsimd", "qg", mt("b1blk")[43:44, 0:42],
                                b1[5418:5460][None, :])
        GE(_)
        return toks

    def ffn_mlp(seq, layer, wtoks, s_toks, arena_free):
        cp_ready = state["cp_done"][layer]
        mw = load_mlp_weights(layer, arena_free, need_b1=not cp_ready)

        if not cp_ready:
            def _(te):
                wait(te, "tensor", mw["b1"])
                pe_banks(te, [5])
                ins = te.matmul(pb(5, 0, 44), mt("b1blk"),
                                sb["I"][0:44, 0:44], start=True, stop=True)
                banks_written(inc(ins, "pe"), [5])
            TE(_)

            def _(v):
                read_banks(v, "vector", [5])
                ins = v.tensor_copy(sb["cpcol"][:, layer, :], pb(5, 0, 44))
                tk = inc(ins, "dve")
                banks_read(tk, [5])
                state["cp_tok"][layer] = tk
            VE(_)

        def _(v):
            wait(v, "vector", s_toks, zeros_tok[0])
            for t in range(NT):
                stt = sb["st4"][:, t, :]
                v.reduce_sum(stt[:, 0:1], sb["s"][:, t, 0:512], AXL.X)
                ins = v.reduce_sum(stt[:, 1:2], sb["s"][:, t, 512:1024], AXL.X)
            inc(ins, "dve")
        VE(_)
        t_sums = ("dve", count["dve"])

        def _(s):
            wait(s, "scalar", s_toks)
            for t in range(NT):
                s.activation(sb["scr"][:], sb["s"][:, t, 0:512], AF.Square,
                             accum_out=sb["st4"][:, t, 2:3])
                ins = s.activation(sb["scr2"][:], sb["s"][:, t, 512:1024],
                                   AF.Square, accum_out=sb["st4"][:, t, 3:4])
            inc(ins, "act")
        SE(_)
        t_st = [t_sums, ("act", count["act"])]

        def _(v):
            wait(v, "vector", s_toks, arena_free)
            for j in range(4):
                ins = v.memset(srow_seg("s", j, 0, 1), 0.0)
            inc(ins, "dve")
        VE(_)
        z0 = ("dve", count["dve"])
        srow_toks = [z0]
        for tg in range(4):
            def _(te, tg=tg):
                wait(te, "tensor", t_st)
                pe_banks(te, [4, 5, 6, 7])
                for tt in range(4):
                    t = 4 * tg + tt
                    for j in range(4):
                        ins = te.matmul(ps[0:1, 512 * (4 + j) + 128 * tt:
                                           512 * (4 + j) + 128 * (tt + 1)],
                                        sb["st4"][:, t, j:j + 1], sb["If"][:],
                                        start=True, stop=True)
                banks_written(inc(ins, "pe"), [4, 5, 6, 7])
            TE(_)

            def _(v, tg=tg):
                read_banks(v, "vector", [4, 5, 6, 7])
                for j in range(4):
                    ins = v.tensor_copy(
                        srow_seg("s", j, 1 + 512 * tg, 1 + 512 * (tg + 1)),
                        ps[0:1, 512 * (4 + j):512 * (4 + j) + 512])
                tk = inc(ins, "dve")
                banks_read(tk, [4, 5, 6, 7])
                srow_toks.append(tk)
            VE(_)

        def _(v):
            wait(v, "vector", srow_toks)
            v.tensor_tensor(srow_seg("m", 0, 1, 2049), srow_seg("s", 0, 1, 2049),
                            srow_seg("s", 1, 0, 2048), ALU.add)
            v.tensor_tensor(srow_seg("m", 1, 1, 2049), srow_seg("s", 2, 1, 2049),
                            srow_seg("s", 3, 0, 2048), ALU.add)
            v.tensor_scalar(srow_seg("m", 0, 1, 2049), srow_seg("m", 0, 1, 2049),
                            1.0 / D, None, ALU.mult)
            v.tensor_scalar(srow_seg("m", 1, 1, 2049), srow_seg("m", 1, 1, 2049),
                            1.0 / D, EPS, ALU.mult, ALU.add)
            v.scalar_tensor_tensor(out=srow_seg("m", 2, 1, 2049),
                                   in0=srow_seg("m", 0, 1, 2049), scalar=-1.0,
                                   in1=srow_seg("m", 0, 1, 2049),
                                   op0=ALU.mult, op1=ALU.mult)
            ins = v.tensor_tensor(srow_seg("m", 1, 1, 2049), srow_seg("m", 1, 1, 2049),
                                  srow_seg("m", 2, 1, 2049), ALU.add)
            inc(ins, "dve")
        VE(_)
        t_mr = ("dve", count["dve"])

        def _(s):
            wait(s, "scalar", t_mr)
            s.activation(srow_seg("m", 1, 1, 2049), srow_seg("m", 1, 1, 2049),
                         AF.Ln)
            ins = s.activation(srow_seg("m", 1, 1, 2049), srow_seg("m", 1, 1, 2049),
                               AF.Exp, scale=-0.5)
            inc(ins, "act")
        SE(_)
        t_sqr = ("act", count["act"])

        def _(v):
            wait(v, "vector", t_sqr)
            ins = v.tensor_tensor(srow_seg("m", 3, 1, 2049), srow_seg("m", 0, 1, 2049),
                                  srow_seg("m", 1, 1, 2049), ALU.mult)
            inc(ins, "dve")
        VE(_)
        t_rstd = ("dve", count["dve"])

        mc_toks = []
        for tg in range(4):
            def _(te, tg=tg):
                wait(te, "tensor", t_rstd)
                pe_banks(te, [4])
                for tt in range(4):
                    t = 4 * tg + tt
                    srcs = [srow_seg("m", 0, 1 + 128 * t, 1 + 128 * (t + 1)),
                            srow_seg("m", 1, 1 + 128 * t, 1 + 128 * (t + 1)),
                            srow_seg("m", 0, 2 + 128 * t, 2 + 128 * (t + 1)),
                            srow_seg("m", 1, 2 + 128 * t, 2 + 128 * (t + 1))]
                    for j, srcap in enumerate(srcs):
                        ins = te.matmul(pb(4, 128 * tt + 32 * j, 1), srcap,
                                        sb["onesr"][64:65, 0:1],
                                        start=True, stop=True)
                banks_written(inc(ins, "pe"), [4])
            TE(_)

            def _(v, tg=tg):
                read_banks(v, "vector", [4])
                for tt in range(4):
                    t = 4 * tg + tt
                    ins = v.tensor_copy(sb["mcols"][:, t, :],
                                        pb(4)[:, 128 * tt:128 * tt + 128:32])
                tk = inc(ins, "dve")
                banks_read(tk, [4])
                mc_toks.append(tk)
            VE(_)

        if seq == 0 and layer == 0:
            dump("mcols_d", sb["mcols"][:], mc_toks)
            dump("st4_d", sb["st4"][:], t_st)

        def _(te):
            wait(te, "tensor", t_rstd)
            pe_banks(te, [5])
            ins = te.matmul(pb(5, 0, 1), sb["onesr"][64:65, :],
                            srow_seg("m", 3, 1, 2), start=True, stop=True)
            banks_written(inc(ins, "pe"), [5])
        TE(_)
        halo0_bank_tok = ("pe", count["pe"])

        s_out = list(s_toks) if isinstance(s_toks, list) else [s_toks] * NT
        u_free = [arena_free]
        y_free = [arena_free]
        gg_free = [None]
        w1_free = {0: None, 1: None}
        cp_tok = state["cp_tok"][layer]
        p12_T = [None]
        for half in range(2):
            yfm_toks = []
            for tt in range(8):
                t = 8 * half + tt

                def _(v, t=t, tt=tt):
                    wait(v, "vector", mc_toks, arena_free, p12_T[0])
                    v.tensor_scalar(mt("p12")[:, 0:512], sb["s"][:, t, 0:512],
                                    sb["mcols"][:, t, 0:1], sb["mcols"][:, t, 1:2],
                                    ALU.subtract, ALU.mult)
                    ins = v.tensor_scalar(mt("p12")[:, 512:1024],
                                          sb["s"][:, t, 512:1024],
                                          sb["mcols"][:, t, 2:3],
                                          sb["mcols"][:, t, 3:4],
                                          ALU.subtract, ALU.mult)
                    inc(ins, "dve")
                VE(_)
                t_p12 = ("dve", count["dve"])

                def _(te, t=t, tt=tt):
                    wait(te, "tensor", t_p12, y_free[0])
                    pe_banks(te, [6, 7])
                    for c in range(8):
                        ins = transpose(te, mt("p12")[:, 128 * c:128 * (c + 1)],
                                        ps[:, 512 * 6 + 128 * c:512 * 6 + 128 * (c + 1)])
                    tk = inc(ins, "pe")
                    banks_written(tk, [6, 7])
                    p12_T[0] = tk
                TE(_)

                def _(s, t=t, tt=tt):
                    read_banks(s, "scalar", [6, 7])
                    ins = s.activation(mt("y_fm")[:, :, 1 + 128 * tt:1 + 128 * (tt + 1)],
                                       ps[:, 512 * 6:512 * 6 + 1024]
                                       .rearrange("p (c n) -> p c n", c=8), AF.Copy)
                    tk = inc(ins, "act")
                    banks_read(tk, [6, 7])
                    yfm_toks.append(tk)
                SE(_)

            def _(v, half=half):
                wait(v, "vector", yfm_toks)
                if half == 0:
                    read_banks(v, "vector", [5])
                    wait(v, "vector", halo0_bank_tok)
                    ins = v.tensor_scalar(mt("y_fm")[:, 4:8, 0:1],
                                          pb(5, 0, 1)[:, None, :]
                                          .broadcast_to([128, 4, 1]),
                                          -1.0, None, ALU.mult)
                    banks_read(inc(ins, "dve"), [5])
                    yfm_toks.append(("dve", count["dve"]))
                else:
                    ins = v.tensor_copy(mt("y_fm")[:, 4:8, 0:1],
                                        sb["mx"][:, 0:4, None])
                    yfm_toks.append(inc(ins, "dve"))
            VE(_)

            if half == 0:
                def _(v):
                    wait(v, "vector", yfm_toks)
                    ins = v.tensor_copy(sb["mx"][:, 0:4],
                                        mt("y_fm")[:, 4:8, 1024])
                    yfm_toks.append(inc(ins, "dve"))
                VE(_)

            def _(v, half=half):
                wait(v, "vector", yfm_toks, wtoks["lncols"])
                ins = v.tensor_tensor(mt("y_fm")[:], mt("y_fm")[:],
                                      sb["lgcol"][:, :, None]
                                      .broadcast_to([128, 8, 1025]), ALU.mult)
                inc(ins, "dve")
            VE(_)
            t_y = ("dve", count["dve"])
            if seq == 0 and layer == 0 and half == 0:
                dump("yfm_h0", mt("y_fm")[:], t_y)
                dump("cpcol_d", sb["cpcol"][:, 0, :], state["cp_tok"][layer])
            y_free[0] = None

            def _(v):
                wait(v, "vector", u_free[0], arena_free)
                v.memset(mt("u_fm")[32:64, 21, :], 0.0)
                v.memset(mt("u_fm")[64:128, 21, :], 0.0)
                v.memset(sb["scr"][:, 0:1], 0.0)
                ins = v.memset(mt("u_fm")[64:65, 21, :], 1.0)
                inc(ins, "dve")
            VE(_)
            uz_tok = ("dve", count["dve"])

            glu_toks = []
            for mp in range(22):
                mrows = 42 if mp == 21 else 128
                ld2 = {}
                for ga in range(2):
                    col0 = (2730 if ga == 0 else 0) + 128 * mp
                    slot = ga

                    def _(g, col0=col0, slot=slot, mrows=mrows, ga=ga):
                        wait(g, "gpsimd", w1_free[slot])
                        ld2[ga] = dq(g, "gpsimd", "qg",
                                     mt("w1t")[:, slot, :, 0:mrows],
                                     W["ffn_w1"][layer][:, col0:col0 + mrows]
                                     .rearrange("(c p) n -> p c n", p=128))
                    GE(_)

                    if not cp_ready and half == 0:
                        mcol = (1 - ga) * 22 + mp

                        def _(te, slot=slot, mrows=mrows, w1_tok=ld2[ga]):
                            wait(te, "tensor", w1_tok, wtoks["lncols"],
                                 state["cp_tok"][layer])
                            pe_banks(te, [5])
                            for c in range(8):
                                ins = te.matmul(ps[0:mrows, 512 * 5:512 * 5 + 1],
                                                mt("w1t")[:, slot, c, 0:mrows],
                                                sb["lbcol"][:, c:c + 1],
                                                start=(c == 0), stop=(c == 7))
                            banks_written(inc(ins, "pe"), [5])
                        TE(_)

                        def _(v, mcol=mcol, mrows=mrows):
                            read_banks(v, "vector", [5])
                            wait(v, "vector", state["cp_tok"][layer])
                            ins = v.tensor_tensor(
                                sb["cpcol"][0:mrows, layer, mcol:mcol + 1],
                                sb["cpcol"][0:mrows, layer, mcol:mcol + 1],
                                ps[0:mrows, 512 * 5:512 * 5 + 1], ALU.add)
                            tk = inc(ins, "dve")
                            banks_read(tk, [5])
                            state["cp_tok"][layer] = tk
                        VE(_)

                for grp in range(2):
                    for ga in range(2):
                        bsel = (0 if ga == 0 else 1) + 2 * grp

                        def _(te, mrows=mrows, grp=grp, bsel=bsel, ga=ga,
                              w1_tok=ld2[ga]):
                            wait(te, "tensor", w1_tok, t_y)
                            pe_banks(te, [bsel])
                            for c in range(8):
                                off = (1 if c < 4 else 0) + 512 * grp
                                ins = te.matmul(pb(bsel, 0, 512)[0:mrows, :],
                                                mt("w1t")[:, ga, c, 0:mrows],
                                                mt("y_fm")[:, c, off:off + 512],
                                                start=(c == 0), stop=(c == 7))
                            tk = inc(ins, "pe")
                            banks_written(tk, [bsel])
                            w1_free[ga] = tk
                            if ga == 1 and mp == 21 and grp == 1:
                                y_free[0] = tk
                        TE(_)

                def _(s, mp=mp, mrows=mrows):
                    read_banks(s, "scalar", [0, 2])
                    wait(s, "scalar", state["cp_tok"][layer], u_free[0], uz_tok)
                    for grp in range(2):
                        ins = s.activation(
                            mt("u_fm")[0:mrows, mp, 512 * grp:512 * (grp + 1)],
                            pb(2 * grp, 0, 512)[0:mrows, :], AF.Gelu,
                            bias=sb["cpcol"][0:mrows, layer, 22 + mp:23 + mp],
                            scale=1.0)
                    tk = inc(ins, "act")
                    banks_read(tk, [0, 2])
                SE(_)
                gel_tok = ("act", count["act"])

                for grp in range(2):
                    def _(v, mp=mp, grp=grp, mrows=mrows, gel_tok=gel_tok):
                        bsel = 1 + 2 * grp
                        read_banks(v, "vector", [bsel])
                        wait(v, "vector", gel_tok)
                        ins = v.scalar_tensor_tensor(
                            out=mt("u_fm")[0:mrows, mp, 512 * grp:512 * (grp + 1)],
                            in0=pb(bsel, 0, 512)[0:mrows, :],
                            scalar=sb["cpcol"][0:mrows, layer, mp:mp + 1],
                            in1=mt("u_fm")[0:mrows, mp, 512 * grp:512 * (grp + 1)],
                            op0=ALU.add, op1=ALU.mult)
                        tk = inc(ins, "dve")
                        banks_read(tk, [bsel])
                        glu_toks.append(tk)
                    VE(_)

            if seq == 0 and layer == 0:
                dump(f"ufm_h{half}", mt("u_fm")[:], glu_toks)
                dump("w2res_d", mt("w2res")[:], [mw["w2a"], mw["w2b"], mw["b2"], mw["w2z"]])
                dump(f"yfm_hh{half}", mt("y_fm")[:], t_y)

            for tt in range(8):
                t = 8 * half + tt
                obk = [4, 5] if tt % 2 == 0 else [6, 7]

                def _(te, t=t, tt=tt, obk=obk):
                    wait(te, "tensor", glu_toks, uz_tok, mw["w2a"], mw["w2b"], mw["b2"], mw["w2z"])
                    pe_banks(te, obk)
                    DBG1 = (half == 1 and tt == 1 and seq == 0 and layer == 0
                            and ("w2psum_h1t1" in dumps))
                    if DBG1:
                        def _(v):
                            wait(v, "vector", glu_toks, uz_tok)
                            ins = v.tensor_copy(sb["scr2"][:, 0:128],
                                                mt("u_fm")[:, 0, 128:256])
                            inc(ins, "dve")
                        VE(_)
                        dump("u_at_mm", sb["scr2"][:], ("dve", count["dve"]))
                    for h2 in range(2):
                        kl = [0] if DBG1 else list(range(22))
                        for k in kl:
                            kr = 65 if k == 21 else 128
                            lhs = mt("u_fm")[0:kr, k, 128 * tt:128 * (tt + 1)]
                            ins = te.matmul(pb(obk[h2], 0, 512),
                                            lhs,
                                            mt("w2res")[0:kr, k,
                                                        512 * h2:512 * (h2 + 1)],
                                            start=(k == kl[0]), stop=(k == kl[-1]))
                    tk = inc(ins, "pe")
                    banks_written(tk, obk)
                    u_free[0] = tk
                TE(_)

                def _(v, t=t, obk=obk, half=half, tt=tt):
                    read_banks(v, "vector", obk)
                    if half == 1 and tt == 1 and seq == 0 and layer == 0:
                        v.tensor_copy(sb["scr"][:], pb(obk[0], 0, 512))
                    v.tensor_tensor(sb["s"][:, t, 0:512], sb["s"][:, t, 0:512],
                                    pb(obk[0], 0, 512), ALU.add)
                    ins = v.tensor_tensor(sb["s"][:, t, 512:1024],
                                          sb["s"][:, t, 512:1024],
                                          pb(obk[1], 0, 512), ALU.add)
                    tk = inc(ins, "dve")
                    banks_read(tk, obk)
                    s_out[t] = tk
                    if half == 1 and tt == 1 and seq == 0 and layer == 0:
                        dump("w2psum_h1t1", sb["scr"][:], tk)
                VE(_)

        if not cp_ready:
            state["cp_done"][layer] = True
        return s_out

    sb["wx2stg"] = keep(nc.sbuf_tensor("wx2stg", [128, 32], BF16))

    def rwkv_prep():

        wx2_toks = []
        last_evac = [None]
        for n in range(5):
            for c in range(8):
                stg_tok = [None]

                def _(g, n=n, c=c):
                    wait(g, "gpsimd", last_evac[0])
                    stg_tok[0] = dq(g, "gpsimd", "qg", sb["wx2stg"][:],
                                    W["attn_W_x2"][128 * c:128 * (c + 1), n, :])
                GE(_)

                def _(te, n=n, c=c):
                    wait(te, "tensor", stg_tok[0])
                    pe_banks(te, [6])
                    ins = transpose(te, sb["wx2stg"][:],
                                    ps[0:32, 512 * 6:512 * 6 + 128])
                    banks_written(inc(ins, "pe"), [6])
                TE(_)

                def _(v, n=n, c=c):
                    read_banks(v, "vector", [6])
                    ins = v.tensor_copy(
                        sb["wx2t"][32 * (n % 4):32 * (n % 4) + 32, n // 4, c, :],
                        pb(6, 0, 128)[0:32, :])
                    tk = inc(ins, "dve")
                    banks_read(tk, [6])
                    last_evac[0] = tk
                    wx2_toks.append(tk)
                VE(_)

        row_jobs = [("attn_mu_x", W["attn_mu_x"][None, :], 0, 8),
                    ("xb0", W["attn_x_bias"][0][None, :], 8, 8),
                    ("xb1", W["attn_x_bias"][1][None, :], 16, 8),
                    ("xb2", W["attn_x_bias"][2][None, :], 24, 8),
                    ("xb3", W["attn_x_bias"][3][None, :], 32, 8),
                    ("xb4", W["attn_x_bias"][4][None, :], 40, 8),
                    ("u", W["attn_u"][:].rearrange("a b -> (a b)")[None, :], 48, 4),
                    ("gn_w", W["attn_gn_w"][None, :], 52, 8),
                    ("gn_b", W["attn_gn_b"][None, :], 60, 8)]
        prev_cons = [None]
        for nm, ap_, col0, nch in row_jobs:
            rtok = [None]

            def _(g, ap_=ap_):
                wait(g, "gpsimd", prev_cons[0])
                rtok[0] = dq(g, "gpsimd", "qg", sb["lnrow"][0:1, 0:128 * nch], ap_)
            GE(_)

            def _(te, col0=col0, nch=nch):
                wait(te, "tensor", rtok[0])
                pe_banks(te, [6])
                for c in range(nch):
                    ins = te.matmul(pb(6, c, 1),
                                    sb["lnrow"][0:1, 128 * c:128 * (c + 1)],
                                    sb["onesr"][0:1, 0:1], start=True, stop=True)
                banks_written(inc(ins, "pe"), [6])
            TE(_)

            def _(v, col0=col0, nch=nch):
                read_banks(v, "vector", [6])
                ins = v.tensor_copy(sb["bcols"][:, col0:col0 + nch], pb(6, 0, nch))
                tk = inc(ins, "dve")
                banks_read(tk, [6])
                prev_cons[0] = tk
                wx2_toks.append(tk)
            VE(_)
        return wx2_toks

    def bc(col, n=1):
        return sb["bcols"][:, col:col + n]

    def load_rwkv_w(slot, dram_ap, ncols, free_tok, scale_col=None):
        tok = [None]

        def _(g):
            wait(g, "gpsimd", free_tok)
            tok[0] = dq(g, "gpsimd", "qg", bt("wsl")[:, slot, :, 0:ncols],
                        dram_ap.rearrange("(c p) n -> p c n", p=128))
        GE(_)
        if scale_col is not None:
            def _(v):
                wait(v, "vector", tok[0])
                for c in range(8):
                    ins = v.tensor_scalar(bt("wsl")[:, slot, c, 0:ncols],
                                          bt("wsl")[:, slot, c, 0:ncols],
                                          bc(scale_col + c), None, ALU.mult)
                tok[0] = inc(ins, "dve")
            VE(_)
        return tok[0]

    def rwkv_quarter(seq, qi, xsrc, xdst, bstate, prep_tok):
        tiles = [4 * qi + i for i in range(4)]
        NCH = 16

        ln_toks = []
        xp_cons = []
        ldts = {}
        for ii, t in enumerate(tiles):
            def _(sy, ii=ii, t=t):
                wait(sy, "sync", xp_last[ii % 2])
                ldts[ii] = dq(sy, "sync", "qs", sb["xp"][:, ii % 2, :],
                              xsrc.rearrange("(n p) d -> n p d", p=128)[t])
            SY(_)
            if ii % 2 == 0:
                continue
            i0_, i1_ = ii - 1, ii

            def _(v, i0_=i0_, i1_=i1_):
                wait(v, "vector", ldts[i0_], ldts[i1_], bstate.get("arena_free"))
                for tt in (i0_, i1_):
                    v.reduce_sum(sb["st4"][:, tt, 0:1], sb["xp"][:, tt % 2, :],
                                 AXL.X)
                for tt in (i0_, i1_):
                    st = sb["st4"][:, tt, :]
                    v.tensor_scalar(st[:, 0:1], st[:, 0:1], 1.0 / D, None,
                                    ALU.mult)
                for tt in (i0_, i1_):
                    st = sb["st4"][:, tt, :]
                    v.scalar_tensor_tensor(out=st[:, 3:4], in0=st[:, 0:1],
                                           scalar=-1.0, in1=st[:, 0:1],
                                           op0=ALU.mult, op1=ALU.mult)
                for tt in (i0_, i1_):
                    st = sb["st4"][:, tt, :]
                    ins = v.tensor_scalar(st[:, 3:4], st[:, 3:4], EPS, None,
                                          ALU.add)
                inc(ins, "dve")
            VE(_)
            t_pre = ("dve", count["dve"])

            def _(s, i0_=i0_, i1_=i1_):
                wait(s, "scalar", t_pre)
                for tt in (i0_, i1_):
                    st = sb["st4"][:, tt, :]
                    s.activation(bt("xln_bf")[:, tt, :], sb["xp"][:, tt % 2, :],
                                 AF.Square, accum_out=st[:, 1:2])
                    ins = s.activation(st[:, 1:2], st[:, 1:2], AF.Sqrt,
                                       bias=st[:, 3:4], scale=1.0 / D)
                inc(ins, "act")
            SE(_)
            t_sq = ("act", count["act"])

            def _(v, i0_=i0_, i1_=i1_):
                wait(v, "vector", t_sq)
                for tt in (i0_, i1_):
                    v.reciprocal(sb["st4"][:, tt, 1:2], sb["st4"][:, tt, 1:2])
                v.tensor_copy(sb["scr"][:, 0:1], sb["st4"][:, i0_, 0:1])
                for tt in (i0_, i1_):
                    st = sb["st4"][:, tt, :]
                    ins = v.tensor_scalar(bt("xln_bf")[:, tt, :],
                                          sb["xp"][:, tt % 2, :],
                                          st[:, 0:1], st[:, 1:2],
                                          ALU.subtract, ALU.mult)
                    tk = inc(ins, "dve")
                    xp_last[tt % 2] = tk
                    xp_cons.append(tk)
                    ln_toks.append(tk)
            VE(_)

        fm_toks = []
        for ii in range(4):
            def _(te, ii=ii):
                wait(te, "tensor", ln_toks[ii], bstate.get("arena_free"))
                pe_banks(te, [6, 7])
                for c in range(8):
                    ins = transpose(te, bt("xln_bf")[:, ii, 128 * c:128 * (c + 1)],
                                    ps[:, 512 * 6 + 128 * c:512 * 6 + 128 * (c + 1)])
                banks_written(inc(ins, "pe"), [6, 7])
            TE(_)

            def _(s, ii=ii):
                read_banks(s, "scalar", [6, 7])
                ins = s.activation(bt("xln_fm")[:, :, 1 + 128 * ii:1 + 128 * (ii + 1)],
                                   ps[:, 512 * 6:512 * 6 + 1024]
                                   .rearrange("p (c n) -> p c n", c=8), AF.Copy)
                tk = inc(ins, "act")
                banks_read(tk, [6, 7])
                fm_toks.append(tk)
            SE(_)

        def _(v):
            wait(v, "vector", bstate.get("arena_free"))
            if qi == 0:
                ins = v.memset(bt("xln_fm")[:, :, 0:1], 0.0)
            else:
                ins = v.tensor_copy(bt("xln_fm")[:, :, 0:1], sb["halo_b"][:])
            fm_toks.append(inc(ins, "dve"))
        VE(_)

        def _(v):
            wait(v, "vector", fm_toks)
            ins = v.tensor_copy(sb["halo_b"][:], bt("xln_fm")[:, :, 512:513])
            fm_toks.append(inc(ins, "dve"))
        VE(_)

        def _(v):
            wait(v, "vector", fm_toks, prep_tok)
            ins = v.tensor_tensor(bt("delta")[:], bt("xln_fm")[:, :, 0:512],
                                  bt("xln_fm")[:, :, 1:513], ALU.subtract)
            inc(ins, "dve")
        VE(_)
        t_delta = ("dve", count["dve"])

        def _(v):
            wait(v, "vector", t_delta)
            for c in range(8):
                ins = v.scalar_tensor_tensor(
                    out=bt("xi")[:, c, :], in0=bt("delta")[:, c, :],
                    scalar=bc(c), in1=bt("xln_fm")[:, c, 1:513],
                    op0=ALU.mult, op1=ALU.add)
            inc(ins, "dve")
        VE(_)
        t_xmu = ("dve", count["dve"])

        wx1_tok = load_rwkv_w(0, W["attn_W_x1"], 160, bstate.get("wsl_free", [None, None])[0])
        for m2 in range(2):
            mr = 128 if m2 == 0 else 32

            def _(te, m2=m2, mr=mr):
                wait(te, "tensor", t_xmu, wx1_tok)
                pe_banks(te, [m2])
                for c in range(8):
                    ins = te.matmul(pb(m2, 0, 512)[0:mr, :],
                                    bt("wsl")[:, 0, c, 128 * m2:128 * m2 + mr],
                                    bt("xi")[:, c, :], start=(c == 0), stop=(c == 7))
                banks_written(inc(ins, "pe"), [m2])
            TE(_)

            def _(s, m2=m2, mr=mr):
                read_banks(s, "scalar", [m2])
                ins = s.activation(bt("low")[0:mr, m2, :], pb(m2, 0, 512)[0:mr, :],
                                   AF.Tanh)
                tk = inc(ins, "act")
                banks_read(tk, [m2])
            SE(_)
        t_low = ("act", count["act"])

        projs = [("r", W["attn_W_r"], 512, "r_tm", None),
                 ("w", None, 0, None, None),
                 ("k", W["attn_W_k"], 512, "k_tm2", None),
                 ("v", W["attn_W_v"], 1024, "v_tm2", None),
                 ("g", W["attn_W_g"], 1024, "g_tm", None)]
        xi_free = [None]
        wsl_free = bstate.get("wsl_free", [None, None])
        proj_toks = {}
        for bi, (bname, wdram, dout, obuf, _sc) in enumerate(projs):
            lowrow = 32 * (bi % 4)
            lowcol = bi // 4
            xi_toks = []
            for c in range(8):
                bk = c % 2

                def _(te, bi=bi, c=c, bk=bk, lowrow=lowrow, lowcol=lowcol):
                    wait(te, "tensor", t_low, prep_tok, xi_free[0])
                    pe_banks(te, [bk])
                    tp = (lowrow, 0) if lowrow else None
                    ins = te.matmul(pb(bk, 0, 512),
                                    sb["wx2t"][lowrow:lowrow + 32, bi // 4, c, :],
                                    bt("low")[lowrow:lowrow + 32, lowcol, :],
                                    start=True, stop=True, tile_position=tp)
                    banks_written(inc(ins, "pe"), [bk])
                TE(_)

                def _(v, bi=bi, c=c, bk=bk):
                    read_banks(v, "vector", [bk])
                    wait(v, "vector", t_xmu, xi_free[0])
                    v.scalar_tensor_tensor(
                        out=sb["scr2"][:].bitcast(BF16)[:, 0:512],
                        in0=pb(bk, 0, 512), scalar=bc(8 + 8 * bi + c),
                        in1=bt("delta")[:, c, :], op0=ALU.add, op1=ALU.mult)
                    ins = v.tensor_tensor(bt("xi")[:, c, :],
                                          sb["scr2"][:].bitcast(BF16)[:, 0:512],
                                          bt("xln_fm")[:, c, 1:513], ALU.add)
                    tk = inc(ins, "dve")
                    banks_read(tk, [bk])
                    xi_toks.append(tk)
                VE(_)
            t_xi = list(xi_toks)

            if bname == "w":
                ww1_tok = load_rwkv_w(1, W["attn_W_w1"], 64, wsl_free[1])
                aug_tok = [None]

                def _(g):
                    wait(g, "gpsimd", bstate.get("ww2_free"))
                    dq(g, "gpsimd", "qg", bt("ww2aug")[0:64, :], W["attn_W_w2"][:])
                    aug_tok[0] = dq(g, "gpsimd", "qg", bt("ww2aug")[64:65, :],
                                    W["attn_b_w2"][None, :])
                GE(_)

                def _(te):
                    wait(te, "tensor", t_xi, ww1_tok)
                    pe_banks(te, [2])
                    for c in range(8):
                        ins = te.matmul(pb(2, 0, 512)[0:64, :],
                                        bt("wsl")[:, 1, c, 0:64],
                                        bt("xi")[:, c, :], start=(c == 0),
                                        stop=(c == 7))
                    banks_written(inc(ins, "pe"), [2])
                TE(_)

                def _(s):
                    read_banks(s, "scalar", [2])
                    ins = s.activation(bt("lora")[0:64, :], pb(2, 0, 512)[0:64, :],
                                       AF.Tanh)
                    tk = inc(ins, "act")
                    banks_read(tk, [2])
                SE(_)
                t_lora = ("act", count["act"])

                def _(v):
                    wait(v, "vector", bstate.get("arena_free"))
                    ins = v.memset(bt("lora")[64:65, :], 1.0)
                    inc(ins, "dve")
                VE(_)
                t_ones = ("dve", count["dve"])

                for ii in range(4):
                    def _(te, ii=ii):
                        wait(te, "tensor", t_lora, t_ones, aug_tok[0])
                        pe_banks(te, [3])
                        ins = te.matmul(pb(3, 0, 512),
                                        bt("lora")[0:65, 128 * ii:128 * (ii + 1)],
                                        bt("ww2aug")[0:65, :], start=True, stop=True)
                        banks_written(inc(ins, "pe"), [3])
                    TE(_)

                    def _(s, ii=ii):
                        read_banks(s, "scalar", [3])
                        ins = s.activation(w_tm(ii), pb(3, 0, 512), AF.Exp)
                        tk = inc(ins, "act")
                        banks_read(tk, [3])
                        proj_toks.setdefault("w", []).append(tk)
                    SE(_)
                xi_free[0] = ("pe", count["pe"])
                bstate["ww2_free"] = ("pe", count["pe"])
                wsl_free[1] = ("pe", count["pe"])
                if seq == 0 and qi == 0:
                    dump("w_e", sb["s"][:, 4:6, :], proj_toks["w"])
                    dump("lora_d", bt("lora")[0:64, :], proj_toks["w"])
                    dump("xi_w", bt("xi")[:], proj_toks["w"])
            else:
                slot = (1 + bi) % 2
                wtok = load_rwkv_w(slot, wdram, dout, wsl_free[slot])
                nb = dout // 512
                for ii in range(4):
                    bks = [2, 3] if nb == 1 else [2, 3]

                    def _(te, ii=ii, bi=bi, slot=slot, dout=dout, nb=nb):
                        wait(te, "tensor", t_xi, wtok)
                        pe_banks(te, [2, 3][:nb] if ii % 2 == 0 else [4, 5][:nb])
                        bset = [2, 3][:nb] if ii % 2 == 0 else [4, 5][:nb]
                        for h2 in range(nb):
                            for c in range(8):
                                ins = te.matmul(
                                    pb(bset[h2], 0, 512),
                                    bt("xi")[:, c, 128 * ii:128 * (ii + 1)],
                                    bt("wsl")[:, slot, c, 512 * h2:512 * (h2 + 1)],
                                    start=(c == 0), stop=(c == 7))
                        banks_written(inc(ins, "pe"), bset)
                    TE(_)
                    mm_tok = ("pe", count["pe"])

                    def _(v, ii=ii, bi=bi, obuf=obuf, nb=nb):
                        bset = [2, 3][:nb] if ii % 2 == 0 else [4, 5][:nb]
                        read_banks(v, "vector", bset)
                        for h2 in range(nb):
                            ins = v.tensor_copy(
                                bt(obuf)[:, ii, 512 * h2:512 * (h2 + 1)],
                                pb(bset[h2], 0, 512))
                        tk = inc(ins, "dve")
                        banks_read(tk, bset)
                        proj_toks.setdefault(obuf, []).append(tk)
                    VE(_)
                xi_free[0] = ("pe", count["pe"])
                wsl_free[slot] = ("pe", count["pe"])
        bstate["wsl_free"] = [("pe", count["pe"]), ("pe", count["pe"])]

        if bstop == "proj":
            bstate["arena_free"] = [("dve", count["dve"]), ("pe", count["pe"]),
                                    ("act", count["act"])]
            bstate["wsl_free"] = [("pe", count["pe"]), ("pe", count["pe"])]
            return []
        if seq == 0 and qi == 0 and dumps:
            dump("low_d", bt("low")[:, 0, :], ("act", count["act"]))
            dump("low_d2", bt("low")[0:32, 1, :], ("act", count["act"]))
            for nmx in ["r_tm", "k_tm2", "v_tm2", "g_tm", "khat"]:
                dump(nmx + "_d", bt(nmx)[:],
                     [("dve", count["dve"]), ("act", count["act"])])

        o_toks = []
        s_upd_tok = [bstate.get("S0_tok")]
        for gi in range(bng):
            def _(te, gi=gi):
                wait(te, "tensor", proj_toks["w"],
                     bstate.get("fact_free"), state.get("rwkv_lc_tok"))
                pe_banks(te, [0, 1, 2, 3, 4])
                te.matmul(pb(0), bt("Lc")[:, 0, :], w_tm_g(gi), start=True, stop=True)
                te.matmul(pb(1), bt("Lc")[:, 1, :], w_tm_g(gi), start=True, stop=True)
                te.matmul(pb(2), bt("Lc")[:, 2, :], w_tm_g(gi), start=True, stop=True)
                te.matmul(pb(4), bt("Lc")[:, 4, :], w_tm_g(gi), start=True, stop=True)
                for g4 in range(4):
                    ins = te.matmul(pb(3, 4 * g4, 4),
                                    w_tm_g(gi)[:, 128 * g4:128 * (g4 + 1)],
                                    bt("Lc")[:, 3, 0:128:32], start=True, stop=True)
                banks_written(inc(ins, "pe"), [0, 1, 2, 3, 4])
            TE(_)

            def _(s, gi=gi):
                read_banks(s, "scalar", [0, 1, 2, 3, 4])
                s.activation(bt("Fr")[:], pb(0), AF.Exp)
                s.activation(bt("Fk")[:], pb(1), AF.Exp, scale=-1.0)
                s.activation(bt("Fd")[:], pb(2), AF.Exp, scale=-1.0)
                s.activation(bt("Fr0")[:], pb(4), AF.Exp)
                ins = s.activation(bt("dcols")[:].rearrange("p a b -> p (a b)"),
                                   pb(3, 0, 16), AF.Exp, scale=-1.0)
                tk = inc(ins, "act")
                banks_read(tk, [0, 1, 2, 3, 4])
            SE(_)
            t_facts = ("act", count["act"])

            def _(v, gi=gi):
                wait(v, "vector", t_facts, proj_toks["r_tm"], proj_toks["k_tm2"])
                v.tensor_tensor(bt("r_tm")[:, gi, :].bitcast(BF16) if False else
                                sb["scr"][:].bitcast(BF16)[:, 0:512],
                                bt("r_tm")[:, gi, :], bt("Fr")[:], ALU.mult)
                v.tensor_tensor(sb["scr2"][:].bitcast(BF16)[:, 0:512],
                                bt("k_tm2")[:, gi, :], bt("Fk")[:], ALU.mult)
                v.tensor_tensor(bt("khat")[:, gi, :], bt("k_tm2")[:, gi, :],
                                bt("Fd")[:], ALU.mult)
                ins = v.tensor_tensor(sb["gg"][:], bt("r_tm")[:, gi, :],
                                      bt("Fr0")[:], ALU.mult)
                inc(ins, "dve")
            VE(_)
            t_fm0 = ("dve", count["dve"])

            def _(te, gi=gi):
                wait(te, "tensor", t_fm0, bstate.get("bfm_free"))
                pe_banks(te, [4, 5])
                for g4 in range(4):
                    transpose(te, sb["scr"][:].bitcast(BF16)[:, 0:512]
                              [:, 128 * g4:128 * (g4 + 1)],
                              ps[:, 512 * 4 + 128 * g4:512 * 4 + 128 * (g4 + 1)])
                for g4 in range(4):
                    ins = transpose(te, sb["scr2"][:].bitcast(BF16)[:, 0:512]
                                    [:, 128 * g4:128 * (g4 + 1)],
                                    ps[:, 512 * 5 + 128 * g4:512 * 5 + 128 * (g4 + 1)])
                banks_written(inc(ins, "pe"), [4, 5])
            TE(_)

            def _(v, gi=gi):
                read_banks(v, "vector", [4, 5])
                v.tensor_copy(bt("rt_fm")[:].rearrange("p a b -> p (a b)"), pb(4))
                ins = v.tensor_copy(bt("kt_fm")[:].rearrange("p a b -> p (a b)"),
                                    pb(5))
                tk = inc(ins, "dve")
                banks_read(tk, [4, 5])
            VE(_)
            t_fm1 = ("dve", count["dve"])

            def _(te, gi=gi):
                wait(te, "tensor", t_fm1)
                pe_banks(te, [3, 4, 5])
                for g4 in range(4):
                    transpose(te, bt("r_tm")[:, gi, 128 * g4:128 * (g4 + 1)],
                              ps[:, 512 * 4 + 128 * g4:512 * 4 + 128 * (g4 + 1)])
                for g4 in range(4):
                    transpose(te, bt("k_tm2")[:, gi, 128 * g4:128 * (g4 + 1)],
                              ps[:, 512 * 5 + 128 * g4:512 * 5 + 128 * (g4 + 1)])
                for g4 in range(4):
                    ins = transpose(te, sb["gg"][:, 128 * g4:128 * (g4 + 1)],
                                    ps[:, 512 * 3 + 128 * g4:512 * 3 + 128 * (g4 + 1)])
                banks_written(inc(ins, "pe"), [3, 4, 5])
            TE(_)

            def _(v, gi=gi):
                read_banks(v, "vector", [3, 4, 5])
                v.tensor_copy(bt("rr_fm")[:].rearrange("p a b -> p (a b)"), pb(4))
                v.tensor_copy(bt("ku_fm")[:].rearrange("p a b -> p (a b)"), pb(5))
                v.tensor_copy(bt("r0t_fm")[:].rearrange("p a b -> p (a b)"), pb(3))
                ins = v.tensor_tensor(bt("ku_fm")[:], bt("ku_fm")[:],
                                      bc(48, 4)[:, :, None]
                                      .broadcast_to([128, 4, 128]), ALU.mult)
                tk = inc(ins, "dve")
                banks_read(tk, [3, 4, 5])
            VE(_)
            t_fm2 = ("dve", count["dve"])
            bstate["bfm_free"] = None
            if bstop == "fm":
                continue

            if seq == 0 and qi == 0 and gi == 1 and dumps:
                dump("otile0_d", sb["s"][:, 6, :], o_toks)
            if seq == 0 and qi == 0 and gi == 0 and dumps:
                dump("Fr_d", bt("Fr")[:], t_facts)
                dump("Fk_d", bt("Fk")[:], t_facts)
                dump("dcols_d", bt("dcols")[:], t_facts)
                dump("rtfm_d", bt("rt_fm")[:], t_fm2)
                dump("ktfm_d", bt("kt_fm")[:], t_fm2)

            for cl in range(bnc):
                ch = 4 * gi + cl
                cb = 32 * cl

                def _(te, gi=gi, cl=cl, cb=cb):
                    wait(te, "tensor", t_fm1, t_fm2)
                    pe_banks(te, [6, 7])
                    for h in range(16):
                        hb = 32 * (h % 4)
                        hg = h // 4
                        te.matmul(ps[hb:hb + 32,
                                     512 * 6 + 32 * h:512 * 6 + 32 * h + 32],
                                  bt("kt_fm")[hb:hb + 32, hg, cb:cb + 32],
                                  bt("rt_fm")[hb:hb + 32, hg, cb:cb + 32],
                                  start=True, stop=True, tile_position=(hb, hb))
                    for h in range(16):
                        hb = 32 * (h % 4)
                        hg = h // 4
                        ins = te.matmul(ps[hb:hb + 32,
                                           512 * 7 + 32 * h:512 * 7 + 32 * h + 32],
                                        bt("ku_fm")[hb:hb + 32, hg, cb:cb + 32],
                                        bt("rr_fm")[hb:hb + 32, hg, cb:cb + 32],
                                        start=True, stop=True,
                                        tile_position=(hb, hb))
                    banks_written(inc(ins, "pe"), [6, 7])
                TE(_)
                if bstop == "amm":
                    continue

                def _(v, cl=cl, cb=cb):
                    read_banks(v, "vector", [6, 7])
                    wait(v, "vector", s_upd_tok[0], state.get("rwkv_lc_tok"))
                    tri = bt("masks")[0:32, 0, 0:128].rearrange(
                        "p (j c) -> p j c", j=4)
                    dia = bt("masks")[0:32, 1, 0:128].rearrange(
                        "p (j c) -> p j c", j=4)
                    for q in range(4):
                        v.scalar_tensor_tensor(
                            out=bt("B1")[cb:cb + 32, :]
                            .rearrange("p (j c) -> p j c", j=4)
                            [:, :, 32 * q:32 * q + 32],
                            in0=pb(6)[32 * q:32 * q + 32, :]
                            .rearrange("p (j c) -> p j c", j=4)
                            [:, :, 32 * q:32 * q + 32],
                            scalar=1.0, in1=tri, op0=ALU.mult, op1=ALU.mult)
                    for q in range(4):
                        ins = v.scalar_tensor_tensor(
                            out=bt("B2")[cb:cb + 32, :]
                            .rearrange("p (j c) -> p j c", j=4)
                            [:, :, 32 * q:32 * q + 32],
                            in0=pb(7)[32 * q:32 * q + 32, :]
                            .rearrange("p (j c) -> p j c", j=4)
                            [:, :, 32 * q:32 * q + 32],
                            scalar=1.0, in1=dia, op0=ALU.mult, op1=ALU.mult)
                    tk = inc(ins, "dve")
                    banks_read(tk, [6, 7])
                VE(_)
                t_B = ("dve", count["dve"])
                if bstop == "bevac":
                    continue

                def _(te, gi=gi, cl=cl, cb=cb):
                    wait(te, "tensor", t_B, proj_toks["v_tm2"])
                    pe_banks(te, [0, 1, 4, 5])
                    for h in range(16):
                        hb = 32 * (h % 4)
                        hg = h // 4
                        ot = pb(0, 0, 512) if h < 8 else pb(1, 0, 512)
                        oc = 64 * (h % 8)
                        te.matmul(ot[:, oc:oc + 64][cb:cb + 32, :],
                                  bt("B1")[cb:cb + 32, 32 * h:32 * h + 32],
                                  bt("v_tm2")[cb:cb + 32, gi, 64 * h:64 * h + 64],
                                  start=True, stop=False, tile_position=(cb, cb))
                        ins = te.matmul(ot[:, oc:oc + 64][cb:cb + 32, :],
                                  bt("B2")[cb:cb + 32, 32 * h:32 * h + 32],
                                  bt("v_tm2")[cb:cb + 32, gi, 64 * h:64 * h + 64],
                                  start=False, stop=True, tile_position=(cb, cb))
                    if bstop != "omm":
                        for h in range(16):
                            ins = te.matmul(
                                ps[cb:cb + 32,
                                   512 * 4 + 64 * h:512 * 4 + 64 * h + 64],
                                bt("khat")[cb:cb + 32, gi, 32 * h:32 * h + 32],
                                bt("v_tm2")[cb:cb + 32, gi, 64 * h:64 * h + 64],
                                start=True, stop=True, tile_position=(cb, cb))
                    banks_written(inc(ins, "pe"), [0, 1, 4, 5])
                TE(_)

                def _(te, gi=gi, cl=cl, cb=cb):
                    wait(te, "tensor", s_upd_tok[0])
                    pe_banks(te, [3])
                    for h in range(16):
                        hb = 32 * (h % 4)
                        hg = h // 4
                        ins = te.matmul(pb(3)[hb:hb + 32, 64 * hg:64 * hg + 64],
                                        bt("r0t_fm")[hb:hb + 32, hg, cb:cb + 32],
                                        bt("S0b")[hb:hb + 32, hg, :],
                                        start=True, stop=True,
                                        tile_position=(hb, hb))
                    banks_written(inc(ins, "pe"), [3])
                TE(_)
                t_osmm = ("pe", count["pe"])
                if bstop == "omm":
                    continue

                def _(v, gi=gi, cl=cl, cb=cb):
                    read_banks(v, "vector", [0, 1, 3])
                    sc2 = sb["scr2"][:].bitcast(BF16)
                    for q in range(4):
                        v.tensor_copy(
                            sc2[cb:cb + 32, :].rearrange("p (j c) -> p j c", j=4)
                            [:, :, 64 * q:64 * q + 64],
                            pb(3)[32 * q:32 * q + 32, 0:256]
                            .rearrange("p (g c) -> p g c", g=4))
                    v.tensor_tensor(o_tm_t(gi)[cb:cb + 32, 0:512],
                                    pb(0, 0, 512)[cb:cb + 32, :],
                                    sc2[cb:cb + 32, 0:512], ALU.add)
                    ins = v.tensor_tensor(o_tm_t(gi)[cb:cb + 32, 512:1024],
                                          pb(1, 0, 512)[cb:cb + 32, :],
                                          sc2[cb:cb + 32, 512:1024], ALU.add)
                    tk = inc(ins, "dve")
                    banks_read(tk, [0, 1, 3])
                    o_toks.append(tk)
                VE(_)

                def _(v, gi=gi, cl=cl, ch=ch, cb=cb):
                    read_banks(v, "vector", [4, 5])
                    wait(v, "vector", s_upd_tok[0], t_facts)
                    sdl = ps[:, 512 * 4:512 * 6]
                    for q in range(4):
                        v.tensor_copy(pb(2, 0, 256)[32 * q:32 * q + 32, :]
                                      .rearrange("p (g c) -> p g c", g=4),
                                      sdl[cb:cb + 32, :]
                                      .rearrange("p (j c) -> p j c", j=4)
                                      [:, :, 64 * q:64 * q + 64])
                    v.tensor_tensor(bt("Stmp")[:], bt("S0")[:],
                                    bt("dcols")[:, :, cl:cl + 1]
                                    .broadcast_to([128, 4, 64]), ALU.mult)
                    v.tensor_tensor(bt("S0")[:].rearrange("p a b -> p (a b)"),
                                    bt("Stmp")[:].rearrange("p a b -> p (a b)"),
                                    pb(2, 0, 256), ALU.add)
                    ins = v.tensor_copy(bt("S0b")[:].rearrange("p a b -> p (a b)"),
                                        bt("S0")[:].rearrange("p a b -> p (a b)"))
                    tk = inc(ins, "dve")
                    banks_read(tk, [2, 4, 5])
                    s_upd_tok[0] = tk
                VE(_)
                if seq == 0 and qi == 0 and gi == 0 and cl == 0 and dumps:
                    dump("B1_d", bt("B1")[0:32, :], t_B)
                    dump("B2_d", bt("B2")[0:32, :], t_B)
                    dump("S0_d", bt("S0")[:], s_upd_tok[0])
                    dump("o0_d", sb["s"][0:32, 6, :], o_toks)
        bstate["S0_tok"] = s_upd_tok[0]
        bstate["fact_free"] = ("pe", count["pe"])
        if bstop in ("chunks", "fm", "amm", "bevac", "omm"):
            bstate["arena_free"] = [("dve", count["dve"]), ("pe", count["pe"]),
                                    ("act", count["act"])]
            bstate["wsl_free"] = [("pe", count["pe"]), ("pe", count["pe"])]
            return []

        wog_tok = load_rwkv_w(0, W["attn_W_o"], 1024, bstate["wsl_free"][0],
                              scale_col=52)
        wob_tok = load_rwkv_w(1, W["attn_W_o"], 1024, bstate["wsl_free"][1],
                              scale_col=60)
        st_toks = []
        for ii in range(4):
            def _(v, ii=ii):
                wait(v, "vector", o_toks)
                ot = o_tm_t(ii)
                v.tensor_reduce(sb["ssq"][:, ii, 0:8].rearrange("p a -> p a")
                                if False else sb["scr"][:, 0:16],
                                ot.rearrange("p (h d) -> p h d", h=16),
                                AXL.X, ALU.add)
                v.tensor_tensor(gn_scr(ii), ot, ot, ALU.mult)
                ins = v.tensor_reduce(sb["scr"][:, 16:32],
                                      gn_scr(ii).rearrange("p (h d) -> p h d", h=16),
                                      AXL.X, ALU.add)
                inc(ins, "dve")
            VE(_)
            t_gs = ("dve", count["dve"])

            def _(v, ii=ii):
                wait(v, "vector", t_gs)
                v.tensor_scalar(sb["scr"][:, 0:16], sb["scr"][:, 0:16], 1.0 / 64,
                                None, ALU.mult)
                v.tensor_scalar(sb["scr"][:, 16:32], sb["scr"][:, 16:32], 1.0 / 64,
                                EPS, ALU.mult, ALU.add)
                v.scalar_tensor_tensor(out=sb["scr"][:, 32:48],
                                       in0=sb["scr"][:, 0:16], scalar=-1.0,
                                       in1=sb["scr"][:, 0:16],
                                       op0=ALU.mult, op1=ALU.mult)
                v.tensor_copy(sb["scr2"][:, 0:1], sb["scr"][:, 0:1])
                ins = v.tensor_tensor(sb["scr"][:, 16:32], sb["scr"][:, 16:32],
                                      sb["scr"][:, 32:48], ALU.add)
                inc(ins, "dve")
            VE(_)
            t_gs2 = ("dve", count["dve"])

            def _(s, ii=ii):
                wait(s, "scalar", t_gs2)
                ins = s.activation(sb["scr"][:, 16:32], sb["scr"][:, 16:32],
                                   AF.Sqrt)
                inc(ins, "act")
            SE(_)
            t_gsq = ("act", count["act"])

            def _(v, ii=ii):
                wait(v, "vector", t_gsq)
                v.reciprocal(sb["scr"][:, 16:32], sb["scr"][:, 16:32])
                v.tensor_copy(sb["scr2"][:, 0:1], sb["scr"][:, 0:1])
                ot = o_tm_t(ii)
                v.tensor_tensor(ot.rearrange("p (h d) -> p h d", h=16), 
                                ot.rearrange("p (h d) -> p h d", h=16),
                                sb["scr"][:, 0:16, None].broadcast_to([128, 16, 64])
                                if False else
                                sb["scr"][:, 0:16][:, :, None]
                                .broadcast_to([128, 16, 64]), ALU.subtract)
                ins = v.tensor_tensor(ot.rearrange("p (h d) -> p h d", h=16),
                                      ot.rearrange("p (h d) -> p h d", h=16),
                                      sb["scr"][:, 16:32][:, :, None]
                                      .broadcast_to([128, 16, 64]), ALU.mult)
                inc(ins, "dve")
            VE(_)
            t_on = ("dve", count["dve"])

            def _(s, ii=ii):
                wait(s, "scalar", t_on, proj_toks["g_tm"])
                ins = s.activation(bt("g_tm")[:, ii, :], bt("g_tm")[:, ii, :],
                                   AF.Silu)
                inc(ins, "act")
            SE(_)
            t_sg = ("act", count["act"])

            def _(v, ii=ii):
                wait(v, "vector", t_sg)
                ins = v.tensor_tensor(gn_scr(ii).bitcast(BF16)[:, 0:1024],
                                      o_tm_t(ii), bt("g_tm")[:, ii, :], ALU.mult)
                inc(ins, "dve")
            VE(_)
            t_t1 = ("dve", count["dve"])

            def _(te, ii=ii):
                wait(te, "tensor", t_t1)
                pe_banks(te, [4, 5])
                for c in range(8):
                    pslot = 4 if c < 4 else 5
                    transpose(te, gn_scr(ii).bitcast(BF16)[:, 0:1024]
                              [:, 128 * c:128 * (c + 1)],
                              ps[:, 512 * pslot + 128 * (c % 4):
                                 512 * pslot + 128 * (c % 4 + 1)])
                banks_written(inc(te.nop(), "pe"), [4, 5])
            TE(_)

            def _(s, ii=ii):
                read_banks(s, "scalar", [4, 5])
                ins = s.activation(bt("t1sg_fm")[:, 0, :, :]
                                   .rearrange("p c n -> p (c n)"),
                                   ps[:, 512 * 4:512 * 4 + 1024], AF.Copy)
                tk = inc(ins, "act")
                banks_read(tk, [4, 5])
            SE(_)
            t_t1f = ("act", count["act"])

            def _(te, ii=ii):
                wait(te, "tensor", t_t1f)
                pe_banks(te, [4, 5])
                for c in range(8):
                    pslot = 4 if c < 4 else 5
                    transpose(te, bt("g_tm")[:, ii, 128 * c:128 * (c + 1)],
                              ps[:, 512 * pslot + 128 * (c % 4):
                                 512 * pslot + 128 * (c % 4 + 1)])
                banks_written(inc(te.nop(), "pe"), [4, 5])
            TE(_)

            def _(s, ii=ii):
                read_banks(s, "scalar", [4, 5])
                ins = s.activation(bt("t1sg_fm")[:, 1, :, :]
                                   .rearrange("p c n -> p (c n)"),
                                   ps[:, 512 * 4:512 * 4 + 1024], AF.Copy)
                tk = inc(ins, "act")
                banks_read(tk, [4, 5])
            SE(_)
            t_sgf = ("act", count["act"])

            def _(te, ii=ii):
                wait(te, "tensor", t_t1f, t_sgf, wog_tok, wob_tok)
                pe_banks(te, [6, 7])
                for h2 in range(2):
                    for c in range(8):
                        te.matmul(pb(6 + h2), bt("t1sg_fm")[:, 0, c, :],
                                  bt("wsl")[:, 0, c, 512 * h2:512 * (h2 + 1)],
                                  start=(c == 0), stop=False)
                    for c in range(8):
                        ins = te.matmul(pb(6 + h2), bt("t1sg_fm")[:, 1, c, :],
                                        bt("wsl")[:, 1, c, 512 * h2:512 * (h2 + 1)],
                                        start=False, stop=(c == 7))
                banks_written(inc(ins, "pe"), [6, 7])
            TE(_)

            t = tiles[ii]
            ldt = [None]

            def _(sy, ii=ii, t=t):
                wait(sy, "sync", xp_last[ii % 2])
                ldt[0] = dq(sy, "sync", "qs", sb["xp"][:, ii % 2, :],
                            xsrc.rearrange("(n p) d -> n p d", p=128)[t])
            SY(_)

            def _(v, ii=ii, t=t):
                read_banks(v, "vector", [6, 7])
                wait(v, "vector", ldt[0])
                v.tensor_tensor(sb["xp"][:, ii % 2, 0:512], sb["xp"][:, ii % 2, 0:512],
                                pb(6), ALU.add)
                ins = v.tensor_tensor(sb["xp"][:, ii % 2, 512:1024],
                                      sb["xp"][:, ii % 2, 512:1024], pb(7), ALU.add)
                tk = inc(ins, "dve")
                banks_read(tk, [6, 7])
            VE(_)
            t_x2 = ("dve", count["dve"])

            def _(sy, ii=ii, t=t):
                wait(sy, "sync", t_x2)
                stok = dq(sy, "sync", "qs",
                          xdst.rearrange("(n p) d -> n p d", p=128)[t],
                          sb["xp"][:, ii % 2, :])
                st_toks.append(stok)
                xp_last[ii % 2] = stok
            SY(_)

        bstate["arena_free"] = [("dve", count["dve"]), ("pe", count["pe"]),
                                ("act", count["act"])]
        bstate["wsl_free"] = [("pe", count["pe"]), ("pe", count["pe"])]
        return st_toks

    def w_tm(ii):
        return sb["s"][:, 4 + ii // 2, 512 * (ii % 2):512 * (ii % 2) + 512]

    def w_tm_g(gi):
        return w_tm(gi)

    def o_tm_t(ii):
        return sb["s"][:, 6 + ii, :]

    def gn_scr(ii):
        return sb["s"][:, 10 + (ii % 2), :]

    PH = [p for p in ("A", "B", "C") if p in set(phases)]
    s_free = None
    prev_arena_free = None
    for seq in range(NSEQ):
        xsrc = xin[seq]
        for pi, ph in enumerate(PH):
            last = (pi == len(PH) - 1)
            xdst = xout[seq] if last else xscr
            if ph in ("A", "C"):
                s_toks = ln_from_dram(xsrc, s_free, dump_tag=f"s{ph}{seq}")
                attn_state = {"kv_free": prev_arena_free}
                arena_free = prev_arena_free
                for layer in range(NLAYERS_FFN):
                    wt = load_ffn_attn_weights(layer, arena_free)
                    for w in range(4):
                        s_toks2 = ffn_attn_window(seq, layer, w, wt, s_toks,
                                                  attn_state)
                        for i in range(4):
                            s_toks[4 * w + i] = s_toks2[i]
                    if seq == 0 and layer == 0 and ph == "A":
                        dtk = dump(f"sAttn{seq}", sb["s"][:], s_toks)
                        if dtk is not None:
                            s_toks = [dtk] * NT
                    arena_free = list(s_toks)
                    s_toks = ffn_mlp(seq, layer, wt, s_toks, arena_free)
                    arena_free = list(s_toks)
                    if seq == 0 and ph == "A":
                        dtk = dump(f"sL{layer}", sb["s"][:], s_toks)
                        if dtk is not None:
                            s_toks = [dtk] * NT
                st = add_half_s_writeback(xsrc, xdst, s_toks)
                if last:
                    final_toks.extend(st)
                s_free = list(add_half_s_writeback.s_readers)
                add_half_s_writeback.s_readers = []
                prev_arena_free = (list(s_toks) if isinstance(s_toks, list)
                                   else [s_toks])
                xsrc = xdst
            else:
                if state.get("rwkv_prep_tok") is None:
                    state["rwkv_prep_tok"] = rwkv_prep()
                bstate = state.setdefault("bstate", {})
                bstate["arena_free"] = prev_arena_free
                paf = (list(prev_arena_free) if isinstance(prev_arena_free, list)
                       else ([prev_arena_free] if prev_arena_free else []))
                bstate["wsl_free"] = [list(paf), list(paf)]
                bstate["ww2_free"] = list(paf)
                bstate["S0_tok"] = None
                lctok = [None]

                def _(sy):
                    wait(sy, "sync", prev_arena_free)
                    dq(sy, "sync", "qs", bt("Lc")[:, 0, :], C["c_Lex"][:])
                    dq(sy, "sync", "qs", bt("Lc")[:, 1, :], C["c_Lin"][:])
                    dq(sy, "sync", "qs", bt("Lc")[:, 2, :], C["c_Lend"][:])
                    dq(sy, "sync", "qs", bt("Lc")[:, 3, :], C["c_blko"][:])
                    dq(sy, "sync", "qs", bt("Lc")[:, 4, :], C["c_Lex0"][:])
                    dq(sy, "sync", "qs", bt("masks")[0:32, 0, :], C["c_triu01"][:])
                    lctok[0] = dq(sy, "sync", "qs", bt("masks")[0:32, 1, :],
                                  C["c_diag01"][:])
                SY(_)
                state["rwkv_lc_tok"] = lctok[0]

                def _(v):
                    wait(v, "vector", prev_arena_free)
                    v.memset(bt("S0")[:].rearrange("p a b -> p (a b)"), 0.0)
                    ins = v.memset(bt("S0b")[:].rearrange("p a b -> p (a b)"), 0.0)
                    inc(ins, "dve")
                VE(_)
                bstate["S0_tok"] = ("dve", count["dve"])
                allst = []
                for qi in range(bq):
                    stq_ = rwkv_quarter(seq, qi, xsrc, xdst, bstate,
                                        state["rwkv_prep_tok"])
                    allst.extend(stq_)
                if last:
                    final_toks.extend(allst)
                s_free = None
                prev_arena_free = bstate["arena_free"]
                xsrc = xdst

    def _(sy):
        wait(sy, "sync", final_toks)
    SY(_)

    for g in reversed(guards):
        g.__exit__(None, None, None)
    block.__exit__ if False else None

    return nc


def kernel(**inputs):
    nc = build_program()
    consts = CONSTS
    n_cores = 8
    in_maps = []
    for c in range(n_cores):
        m = {"xin": np.ascontiguousarray(inputs["L0_attn_x"][2 * c:2 * c + 2])}
        for k, v in inputs.items():
            if k == "L0_attn_x":
                continue
            name = k.replace("L0_", "")
            m[name] = np.ascontiguousarray(v)
        for k, v in consts.items():
            m[k] = v
        in_maps.append(m)
    res = run_bass_kernel_spmd(nc, in_maps, core_ids=list(range(n_cores)))
    out = np.concatenate([r["xout"] for r in res.results], axis=0)
    return out.astype(np.float32)


# version 10
# speedup vs baseline: 1.0066x; 1.0066x over previous
import numpy as np
import ml_dtypes
import concourse.bass as bass
import concourse.mybir as mybir
from concourse.bass_utils import run_bass_kernel_spmd

F32 = mybir.dt.float32
BF16 = mybir.dt.bfloat16
AF = mybir.ActivationFunctionType
ALU = mybir.AluOpType
AXL = mybir.AxisListType
BF = ml_dtypes.bfloat16

D = 1024
T = 2048
NSEQ = 2
NT = 16
WIN = 512
HID = 2730
EPS = 1e-5
NEG_BIG = -1.0e9
CH = 32
NH_RW = 16


def build_constants():
    c = {}
    c["c_I128"] = np.eye(128, dtype=BF)
    c["c_I128f"] = np.eye(128, dtype=np.float32)
    c["c_onesr"] = np.ones((128, 128), dtype=np.float32)
    i = np.arange(128)
    c["c_tri_q"] = np.where(i[None, :] > i[:, None], NEG_BIG, 0.0).astype(BF)
    c["c_tri_p"] = np.where(i[None, :] < i[:, None], NEG_BIG, 0.0).astype(BF)
    dh = 64
    half_idx = np.arange(0, dh, 2, dtype=np.float64)
    inv_freq = 1.0 / (10000.0 ** (half_idx / dh))
    t = np.arange(T, dtype=np.float64)
    freqs = np.concatenate([t[:, None] * inv_freq[None, :]] * 2, axis=-1)
    base = (half_idx + 0.4 * dh) / (1.4 * dh)
    power = (t - T // 2) / float(WIN // 2)
    scale = np.concatenate([base[None, :] ** power[:, None]] * 2, axis=-1)
    sign = np.concatenate([-np.ones(dh // 2), np.ones(dh // 2)])[None, :]
    QK_SCALE = 8.0
    c["c_rot"] = np.stack([
        QK_SCALE * np.cos(freqs) * scale, QK_SCALE * sign * np.sin(freqs) * scale,
        np.cos(freqs) / scale, sign * np.sin(freqs) / scale,
    ], axis=1).astype(np.float32)
    j = np.arange(CH)
    Lin = (j[None, :] <= j[:, None]).astype(np.float64)
    Lex = (j[None, :] < j[:, None]).astype(np.float64)
    ref = (j[None, :] <= 15).astype(np.float64)

    def blockdiag(blk):
        m = np.zeros((128, 128))
        for q in range(4):
            m[q * CH:(q + 1) * CH, q * CH:(q + 1) * CH] = blk.T
        return m.astype(np.float32)

    c["c_Lin"] = blockdiag(-(Lin - ref))
    c["c_Lex"] = blockdiag(-(Lex - ref))
    c["c_Lend"] = blockdiag(-(Lin - 1.0))
    c["c_Lex0"] = blockdiag(-Lex)
    m = np.zeros((128, 128))
    for q in range(4):
        m[q * CH:(q + 1) * CH, q * CH:(q + 1) * CH] = 1.0
    c["c_blko"] = m.astype(np.float32)
    jj = np.arange(CH)
    c["c_triu01"] = np.tile((jj[:, None] < jj[None, :]).astype(np.float32),
                            (1, NH_RW)).astype(BF)
    c["c_diag01"] = np.tile(np.eye(CH, dtype=np.float32), (1, NH_RW)).astype(BF)
    c["c_scal"] = np.tile(np.array([0.0, 1.0, -1.0, 1e-5, 0.5, -0.5, 8.0, 1e-12],
                           dtype=np.float32)[None, :], (128, 1))
    return c


CONSTS = build_constants()


NLAYERS_FFN = 2


def build_program(dumps=(), phases=("A", "B", "C"), bq=4, bstop="full", bng=4, bnc=4):
    dumps = set(dumps)
    nc = bass.Bass("TRN2", target_bir_lowering=False)
    guards = []

    def keep(g):
        t = g.__enter__()
        guards.append(g)
        return t

    def sbuf(name, shape, dtype):
        return keep(nc.sbuf_tensor(name, list(shape), dtype))

    din, dout = {}, {}

    def dram_in(name, shape, dtype=F32):
        din[name] = nc.dram_tensor(name, list(shape), dtype, kind="ExternalInput").ap()
        return din[name]

    def dram_out(name, shape, dtype=F32):
        dout[name] = nc.dram_tensor(name, list(shape), dtype, kind="ExternalOutput").ap()
        return dout[name]

    xin = dram_in("xin", [NSEQ, T, D])
    xout = dram_out("xout", [NSEQ, T, D])
    xscr = nc.dram_tensor("xscr", [T, D], F32, kind="Internal").ap()

    W = {}
    for name, shape in [
        ("attn_mu_x", [D]), ("attn_W_x1", [D, 160]), ("attn_W_x2", [D, 5, 32]),
        ("attn_x_bias", [5, D]), ("attn_W_r", [D, 512]), ("attn_W_w1", [D, 64]),
        ("attn_W_w2", [64, 512]), ("attn_b_w2", [512]), ("attn_W_k", [D, 512]),
        ("attn_W_v", [D, D]), ("attn_W_g", [D, D]), ("attn_u", [16, 32]),
        ("attn_gn_w", [D]), ("attn_gn_b", [D]), ("attn_W_o", [D, D]),
        ("ffn_wqkv", [2, D, 1536]), ("ffn_q_scale", [2, 64]), ("ffn_k_scale", [2, 64]),
        ("ffn_wo", [2, 512, D]), ("ffn_ln_g", [2, D]), ("ffn_ln_b", [2, D]),
        ("ffn_w1", [2, D, 2 * HID]), ("ffn_b1", [2, 2 * HID]),
        ("ffn_w2", [2, HID, D]), ("ffn_b2", [2, D]),
    ]:
        W[name] = dram_in(name, shape)
    C = {}
    for name, arr in CONSTS.items():
        C[name] = dram_in(name, list(arr.shape), BF16 if arr.dtype == BF else F32)

    sb = {}
    sb["s"] = sbuf("s_res", [128, NT, D], F32)
    ARENA_HW = 57 * 1024
    arena = sbuf("arena", [128, ARENA_HW], BF16)

    def av(off_bytes, shape, dtype=BF16):
        es = 2 if dtype == BF16 else 4
        assert off_bytes % 64 == 0
        n = int(np.prod(shape))
        base = arena[:].bitcast(dtype) if dtype != BF16 else arena[:]
        flat = base[:, off_bytes // es: off_bytes // es + n]
        if len(shape) == 1:
            assert shape[0] == n
            return flat
        pat = " ".join(f"d{i}" for i in range(len(shape)))
        kw = {f"d{i}": shape[i] for i in range(len(shape))}
        return flat.rearrange(f"p ({pat}) -> p {pat}", **kw)

    AT = {}
    o = 0
    def _alloc(tag, shape, dtype=BF16):
        nonlocal o
        es = 2 if dtype == BF16 else 4
        n = int(np.prod(shape)) * es
        AT[tag] = (o, shape, dtype)
        o += (n + 63) // 64 * 64
    _alloc("h_fm", [8, 513]); _alloc("sbf", [4, D])
    _alloc("q_tm", [4, 512]); _alloc("k_tm", [4, 512])
    _alloc("qr_tm", [4, 512]); _alloc("kr_tm", [4, 512])
    _alloc("v_tm", [2, 4, 512]); _alloc("q_fm", [4, 512]); _alloc("k_fm", [2, 4, 512])
    _alloc("pexp", [4, 640]); _alloc("pt", [4, 640])
    _alloc("o_tm", [4, 512]); _alloc("o_fm", [4, 4, 128])
    _alloc("wqkv", [8, 1536]); _alloc("wo", [4, D])
    ATT_END = o
    assert ATT_END <= 2 * ARENA_HW, ATT_END

    MT = {}
    o = 0
    def _malloc(tag, shape, dtype=BF16):
        nonlocal o
        es = 2 if dtype == BF16 else 4
        n = int(np.prod(shape)) * es
        MT[tag] = (o, shape, dtype)
        o += (n + 63) // 64 * 64
    _malloc("y_fm", [8, 1025])
    _malloc("u_fm", [22, 1024])
    _malloc("w1t", [2, 8, 128])
    _malloc("w2res", [22, 1024])
    _malloc("p12", [D]); _malloc("b1blk", [128])
    MLP_END = o
    assert MLP_END <= 2 * ARENA_HW, MLP_END

    def at(tag):
        off, shape, dt = AT[tag]
        return av(off, shape, dt)

    BT = {}
    o = 0
    def _balloc(tag, shape, dtype=BF16):
        nonlocal o
        es = 2 if dtype == BF16 else 4
        n = int(np.prod(shape)) * es
        BT[tag] = (o, shape, dtype)
        o += (n + 63) // 64 * 64
    _balloc("xln_fm", [8, 513])
    _balloc("delta", [8, 512]); _balloc("xi", [8, 512])
    _balloc("low", [2, 512]); _balloc("lora", [512]); _balloc("ww2aug", [512])
    _balloc("r_tm", [4, 512]); _balloc("k_tm2", [4, 512]); _balloc("khat", [4, 512])
    _balloc("Fr", [512]); _balloc("Fk", [512]); _balloc("Fd", [512])
    _balloc("v_tm2", [4, D]); _balloc("g_tm", [4, D])
    _balloc("rt_fm", [4, 128]); _balloc("kt_fm", [4, 128])
    _balloc("r0t_fm", [4, 128]); _balloc("Fr0", [512])
    _balloc("rr_fm", [4, 128]); _balloc("ku_fm", [4, 128])
    _balloc("B1", [512]); _balloc("B2", [512])
    _balloc("dcols", [4, 4], F32); _balloc("S0", [4, 64], F32)
    _balloc("S0b", [4, 64])
    _balloc("Stmp", [4, 64], F32)
    _balloc("dvt", [16], F32)
    _balloc("urep", [512])
    _balloc("gnstat", [48], F32)
    _balloc("dvv", [1024])
    _balloc("t1sg_fm", [2, 8, 128])
    _balloc("wsl", [2, 8, D])
    _balloc("Lc", [5, 128], F32)
    _balloc("masks", [2, 512])
    RWK_END = o
    assert RWK_END <= 2 * ARENA_HW, RWK_END

    def bt(tag):
        if tag == "xln_bf":
            off, _shape, dt = BT["v_tm2"]
            return av(off, [4, D], BF16)
        off, shape, dt = BT[tag]
        return av(off, shape, dt)

    def mt(tag):
        off, shape, dt = MT[tag]
        if tag == "b1blk":
            return arena[0:44, off // 2: off // 2 + 128]
        return av(off, shape, dt)

    def srow_seg(which, j, a, b):
        p = 0 if which == "s" else 64
        return av(MT["u_fm"][0], [8320], F32)[p:p + 1, 2080 * j + a: 2080 * j + b]

    sb["I"] = sbuf("I128", [128, 128], BF16)
    sb["If"] = sbuf("I128f", [128, 128], F32)
    sb["onesr"] = sbuf("onesr", [128, 128], F32)
    sb["tri_pq"] = sbuf("tri_pq", [128, 2, 128], BF16)
    sb["rot"] = sbuf("rot", [128, 4, 4, 64], F32)
    sb["xp"] = sbuf("xpair", [128, 2, D], F32)
    sb["scr"] = sbuf("scr", [128, 512], F32)
    sb["scr2"] = sbuf("scr2", [128, 512], F32)
    sb["st4"] = sbuf("st4", [128, NT, 4], F32)
    sb["mcols"] = sbuf("mcols", [128, NT, 4], F32)
    sb["ssq"] = sbuf("ssq", [128, 4, 8], F32)
    sb["ssk"] = sbuf("ssk", [128, 4, 8], F32)
    sb["qsb"] = sbuf("qsb", [128, 2, 64], F32)
    sb["qsrow"] = sbuf("qsrow", [1, 2, 64], F32)
    sb["rz"] = sbuf("rz", [128, 2, 8], F32)
    sb["mx"] = sbuf("mx", [128, 4], F32)
    sb["halo"] = sbuf("halo", [128, 8, 1], BF16)
    sb["gg"] = sbuf("gg", [128, 512], BF16)
    sb["cpcol"] = sbuf("cpcol", [128, 2, 44], F32)
    sb["lbcol"] = sbuf("lbcol", [128, 8], BF16)
    sb["lgcol"] = sbuf("lgcol", [128, 8], F32)
    sb["hrow"] = sbuf("hrow", [1, 4], F32)
    sb["lnrow"] = sbuf("lnrow", [1, D], F32)
    sb["wx2t"] = sbuf("wx2t", [128, 2, 8, 128], BF16)
    sb["bcols"] = sbuf("bcols", [128, 70], F32)
    sb["halo_b"] = sbuf("halo_b", [128, 8, 1], BF16)
    sb["cpool"] = sbuf("cpool", [128, 8], F32)
    for _ci, _cv in enumerate([0.0, 1.0, -1.0, 1e-5, 0.5, -0.5, 8.0, 1e-12]):
        nc.const_aps.register(F32, _cv, sb["cpool"][:, _ci:_ci + 1])

    ps = keep(nc.psum_tensor("ps", [128, 4096], F32))

    def pb(bi, off=0, width=512):
        return ps[:, 512 * bi + off: 512 * bi + off + width]

    block = keep(nc.Block())

    sems, count = {}, {}
    for nm in (["pe", "dve", "act", "pool"]
               + [f"qs{i}" for i in range(4)] + [f"qg{i}" for i in range(4)]):
        sems[nm] = keep(nc.semaphore("sem_" + nm))
        count[nm] = 0
    pool_idx = {"qs": 0, "qg": 0}
    waited = {e: {s: 0 for s in sems} for e in
              ["tensor", "vector", "scalar", "gpsimd", "sync"]}

    def inc(ins, nm):
        step = 16 if nm[0] == "q" else 1
        count[nm] += step
        if ins is not None:
            ins.then_inc(sems[nm], step)
        return (nm, count[nm])

    def dq(eng, stream, pool, out, in_, **kw):
        nm = f"{pool}{pool_idx[pool] % 4}"
        pool_idx[pool] += 1
        if count[nm] > 0:
            wait(eng, stream, (nm, count[nm]))
        ins = eng.dma_start(out=out, in_=in_, **kw)
        return inc(ins, nm)

    def wait(eng, stream, *toks):
        best = {}
        for t in toks:
            if t is None:
                continue
            ts = t if isinstance(t, list) else [t]
            for u in ts:
                if u is not None:
                    best[u[0]] = max(best.get(u[0], 0), u[1])
        for s, v in best.items():
            if v > waited[stream][s]:
                eng.wait_ge(sems[s], v)
                waited[stream][s] = v

    bank_w = [None] * 8
    bank_r = [[] for _ in range(8)]

    def pe_banks(te, bis):
        toks = []
        for bi in bis:
            toks += bank_r[bi]
        wait(te, "tensor", toks)

    def banks_written(tok, bis):
        for bi in bis:
            bank_w[bi] = tok
            bank_r[bi] = []

    def read_banks(eng, stream, bis):
        wait(eng, stream, [bank_w[bi] for bi in bis])

    def banks_read(tok, bis):
        for bi in bis:
            bank_r[bi].append(tok)

    def TE(fn):
        block.tensor(fn)

    def VE(fn):
        block.vector(fn)

    def SE(fn):
        block.scalar(fn)

    def GE(fn):
        block.gpsimd(fn)

    def SY(fn):
        block.sync(fn)

    final_toks = []
    dumped = set()
    xp_last = {0: None, 1: None}

    def dump(name, ap, tok):
        if name not in dumps or name in dumped:
            return None
        dumped.add(name)
        dd = dram_out("dump_" + name, list(ap.shape),
                      BF16 if ap.dtype == BF16 else F32)
        out = [None]

        def _(sy):
            wait(sy, "sync", tok)
            out[0] = dq(sy, "sync", "qs", dd[:], ap)
            final_toks.append(out[0])
        SY(_)
        return out[0]

    def transpose(te, lhsT, out_ps):
        kk = lhsT.shape[0]
        return te.matmul(out_ps, lhsT, sb["I"][0:kk, 0:kk], start=True, stop=True)

    init_tok = [None]

    def _(sy):
        tks = [dq(sy, "sync", "qs", sb["I"][:], C["c_I128"][:]),
               dq(sy, "sync", "qs", sb["If"][:], C["c_I128f"][:]),
               dq(sy, "sync", "qs", sb["onesr"][:], C["c_onesr"][:]),
               dq(sy, "sync", "qs", sb["tri_pq"][:, 1, :], C["c_tri_q"][:]),
               dq(sy, "sync", "qs", sb["tri_pq"][:, 0, :], C["c_tri_p"][:]),
               dq(sy, "sync", "qs", sb["cpool"][:], C["c_scal"][:])]
        wait(sy, "sync", tks)
        init_tok[0] = tks
    SY(_)

    def _(te):
        wait(te, "tensor", init_tok[0])
    TE(_)

    def _(v):
        wait(v, "vector", init_tok[0])
    VE(_)

    def _(s):
        wait(s, "scalar", init_tok[0])
    SE(_)

    def _(g):
        wait(g, "gpsimd", init_tok[0])
    GE(_)

    zeros_tok = [None]

    def _(v):
        ins = v.memset(sb["halo"][:], 0.0)
        zeros_tok[0] = inc(ins, "dve")
    VE(_)

    def load_rot(seq_w):
        tok = [None]

        def _(sy):
            tok[0] = dq(sy, "sync", "qs", sb["rot"][:],
                        C["c_rot"][:].rearrange("(n p) f d -> n p f d", p=128)[
                            4 * seq_w:4 * seq_w + 4].rearrange("n p f d -> p n f d"))
        SY(_)
        return tok[0]

    def ln_from_dram(src_ap, s_free_toks, dump_tag=None):
        ld = []
        for t in range(NT):
            def _(sy, t=t):
                wait(sy, "sync", xp_last[t % 2])
                ld.append(dq(sy, "sync", "qs", sb["xp"][:, t % 2, :],
                             src_ap.rearrange("(n p) d -> n p d", p=128)[t]))
            SY(_)
            if t % 2 == 0:
                continue
            t0, t1 = t - 1, t

            def _(v, t0=t0, t1=t1):
                wait(v, "vector", ld[t0], ld[t1], s_free_toks)
                for tt in (t0, t1):
                    v.reduce_sum(sb["st4"][:, tt, 0:1], sb["xp"][:, tt % 2, :],
                                 AXL.X)
                for tt in (t0, t1):
                    st = sb["st4"][:, tt, :]
                    v.tensor_scalar(st[:, 0:1], st[:, 0:1], 1.0 / D, None,
                                    ALU.mult)
                for tt in (t0, t1):
                    st = sb["st4"][:, tt, :]
                    v.scalar_tensor_tensor(out=st[:, 3:4], in0=st[:, 0:1],
                                           scalar=-1.0, in1=st[:, 0:1],
                                           op0=ALU.mult, op1=ALU.mult)
                for tt in (t0, t1):
                    st = sb["st4"][:, tt, :]
                    ins = v.tensor_scalar(st[:, 3:4], st[:, 3:4], EPS, None,
                                          ALU.add)
                inc(ins, "dve")
            VE(_)
            t_pre = ("dve", count["dve"])

            def _(s, t0=t0, t1=t1):
                wait(s, "scalar", t_pre)
                for tt in (t0, t1):
                    st = sb["st4"][:, tt, :]
                    s.activation(sb["s"][:, tt, :], sb["xp"][:, tt % 2, :],
                                 AF.Square, accum_out=st[:, 1:2])
                    ins = s.activation(st[:, 1:2], st[:, 1:2], AF.Sqrt,
                                       bias=st[:, 3:4], scale=1.0 / D)
                inc(ins, "act")
            SE(_)
            t_sq = ("act", count["act"])

            def _(v, t0=t0, t1=t1):
                wait(v, "vector", t_sq)
                for tt in (t0, t1):
                    v.reciprocal(sb["st4"][:, tt, 1:2], sb["st4"][:, tt, 1:2])
                v.tensor_copy(sb["scr"][:, 0:1], sb["st4"][:, t0, 0:1])
                for tt in (t0, t1):
                    st = sb["st4"][:, tt, :]
                    ins = v.tensor_scalar(sb["s"][:, tt, :],
                                          sb["xp"][:, tt % 2, :],
                                          st[:, 0:1], st[:, 1:2],
                                          ALU.subtract, ALU.mult)
                    xp_last[tt % 2] = inc(ins, "dve")
                    ln_from_dram.cons[tt] = xp_last[tt % 2]
            VE(_)
        out = [ln_from_dram.cons[t] for t in range(NT)]
        if dump_tag:
            dump(dump_tag, sb["s"][:], out)
            dump(dump_tag + "_st", sb["st4"][:], out)
        return out
    ln_from_dram.cons = {}

    def add_half_s_writeback(src_ap, dst_ap, s_toks, dump_tag=None):
        st_toks = []
        ld = []
        for t in range(NT):
            def _(sy, t=t):
                wait(sy, "sync", xp_last[t % 2])
                ld.append(dq(sy, "sync", "qs", sb["xp"][:, t % 2, :],
                             src_ap.rearrange("(n p) d -> n p d", p=128)[t]))
            SY(_)

            def _(v, t=t):
                wait(v, "vector", ld[t], s_toks[t] if isinstance(s_toks, list) else s_toks)
                ins = v.scalar_tensor_tensor(
                    out=sb["xp"][:, t % 2, :], in0=sb["s"][:, t, :], scalar=0.5,
                    in1=sb["xp"][:, t % 2, :], op0=ALU.mult, op1=ALU.add)
                inc(ins, "dve")
            VE(_)
            tadd = ("dve", count["dve"])
            add_half_s_writeback.s_readers.append(tadd)

            def _(sy, t=t):
                wait(sy, "sync", tadd)
                stok = dq(sy, "sync", "qs",
                          dst_ap.rearrange("(n p) d -> n p d", p=128)[t],
                          sb["xp"][:, t % 2, :])
                st_toks.append(stok)
                xp_last[t % 2] = stok
            SY(_)
        return st_toks
    add_half_s_writeback.s_readers = []

    state = {"cp_done": [False, False], "cp_tok": [None, None]}

    def load_ffn_attn_weights(layer, free_toks):
        toks = {}

        def _(g):
            wait(g, "gpsimd", free_toks)
            dq(g, "gpsimd", "qg", sb["qsrow"][0:1, 0, :],
               W["ffn_q_scale"][layer][None, :])
            toks["qsrow"] = dq(g, "gpsimd", "qg", sb["qsrow"][0:1, 1, :],
                               W["ffn_k_scale"][layer][None, :])
            wq = W["ffn_wqkv"][layer].rearrange("(c p) n -> p c n", p=128)
            for c in range(8):
                toks[("wqkv", c)] = dq(g, "gpsimd", "qg",
                                       at("wqkv")[:, c, :], wq[:, c, :])
            wor = W["ffn_wo"][layer].rearrange("(c p) n -> p c n", p=128)
            for c in range(4):
                toks[("wo", c)] = dq(g, "gpsimd", "qg",
                                     at("wo")[:, c, :], wor[:, c, :])
            toks["wqkv"] = [toks[("wqkv", c)] for c in range(8)]
            toks["wo"] = [toks[("wo", c)] for c in range(4)]
        GE(_)
        prevc = [None]
        for j, (nm, dst, ddt) in enumerate([("ffn_ln_g", "lgcol", F32),
                                            ("ffn_ln_b", "lbcol", BF16)]):
            rt = [None]

            def _(g, nm=nm):
                wait(g, "gpsimd", prevc[0])
                rt[0] = dq(g, "gpsimd", "qg", sb["lnrow"][0:1, :],
                           W[nm][layer][None, :])
            GE(_)

            def _(te):
                wait(te, "tensor", rt[0])
                pe_banks(te, [7])
                for c in range(8):
                    ins = te.matmul(pb(7, c, 1),
                                    sb["lnrow"][0:1, 128 * c:128 * (c + 1)],
                                    sb["onesr"][0:1, 0:1], start=True, stop=True)
                banks_written(inc(ins, "pe"), [7])
            TE(_)

            def _(v, dst=dst):
                read_banks(v, "vector", [7])
                ins = v.tensor_copy(sb[dst][:], pb(7, 0, 8))
                tk = inc(ins, "dve")
                banks_read(tk, [7])
                prevc[0] = tk
                toks["lncols"] = tk
            VE(_)

        def _(te):
            wait(te, "tensor", toks["qsrow"])
            pe_banks(te, [6])
            te.matmul(pb(6, 0, 64), sb["onesr"][0:1, :], sb["qsrow"][0:1, 0, :],
                      start=True, stop=True)
            ins = te.matmul(pb(6, 64, 64), sb["onesr"][0:1, :], sb["qsrow"][0:1, 1, :],
                            start=True, stop=True)
            banks_written(inc(ins, "pe"), [6])
        TE(_)
        qtok = [None]

        def _(v):
            read_banks(v, "vector", [6])
            ins = v.tensor_copy(sb["qsb"][:].rearrange("p a d -> p (a d)"),
                                pb(6, 0, 128))
            tk = inc(ins, "dve")
            banks_read(tk, [6])
            qtok[0] = tk
        VE(_)
        toks["qsb"] = qtok[0]
        return toks

    def ffn_attn_window(seq, layer, w, wtoks, s_toks, attn_state):
        cur = w % 2
        prv = 1 - cur
        tiles = [4 * w + i for i in range(4)]
        rot_tok = load_rot_win(w, attn_state.get("rot_free"))

        t_sh = [None]

        def _(v):
            wait(v, "vector", [s_toks[t] for t in tiles],
                 attn_state.get("kv_free"), attn_state.get("hfm_done"))
            if w > 0:
                v.tensor_copy(sb["halo"][:], at("h_fm")[:, :, 512:513])
            else:
                v.memset(sb["halo"][:], 0.0)
            for i in range(4):
                ins = v.tensor_copy(at("sbf")[:, i, :], sb["s"][:, tiles[i], :])
            t_sh[0] = inc(ins, "dve")
        VE(_)

        hfm_toks = []
        for i in range(4):
            hb0 = 6 if i % 2 == 0 else 4

            def _(te, i=i, hb0=hb0):
                wait(te, "tensor", t_sh[0])
                pe_banks(te, [hb0, hb0 + 1])
                for c in range(8):
                    ins = transpose(te, at("sbf")[:, i, 128 * c:128 * (c + 1)],
                                    ps[:, 512 * hb0 + 128 * c:
                                       512 * hb0 + 128 * (c + 1)])
                banks_written(inc(ins, "pe"), [hb0, hb0 + 1])
            TE(_)

            def _(s, i=i, hb0=hb0):
                read_banks(s, "scalar", [hb0, hb0 + 1])
                ins = s.activation(at("h_fm")[:, :, 1 + 128 * i:1 + 128 * (i + 1)],
                                   ps[:, 512 * hb0:512 * hb0 + 1024]
                                   .rearrange("p (c n) -> p c n", c=8), AF.Copy)
                tk = inc(ins, "act")
                banks_read(tk, [hb0, hb0 + 1])
                hfm_toks.append(tk)
            SE(_)

        def _(v):
            ins = v.tensor_copy(at("h_fm")[:, :, 0:1], sb["halo"][:])
            hfm_toks.append(inc(ins, "dve"))
        VE(_)
        attn_state["hfm_done"] = list(hfm_toks)

        prep_toks = []
        for i in range(4):
            qb = 0 if i % 2 == 0 else 3

            def _(te, i=i, qb=qb):
                wait(te, "tensor", hfm_toks)
                pe_banks(te, [qb, qb + 1, qb + 2])
                for c in range(8):
                    if i <= 1:
                        wait(te, "tensor", wtoks[("wqkv", c)])
                    off = (1 if c < 4 else 0) + 128 * i
                    for part in range(3):
                        ins = te.matmul(pb(qb + part),
                                        at("h_fm")[:, c, off:off + 128],
                                        at("wqkv")[:, c, 512 * part:512 * (part + 1)],
                                        start=(c == 0), stop=(c == 7))
                banks_written(inc(ins, "pe"), [qb, qb + 1, qb + 2])
            TE(_)

            def _(s, i=i, qb=qb):
                read_banks(s, "scalar", [qb, qb + 1])
                s.activation(at("q_tm")[:, i, :], pb(qb), AF.Copy)
                ins = s.activation(at("k_tm")[:, i, :], pb(qb + 1), AF.Copy)
                banks_read(inc(ins, "act"), [qb, qb + 1])
            SE(_)

            def _(v, i=i, qb=qb):
                read_banks(v, "vector", [qb, qb + 1, qb + 2])
                wait(v, "vector", ("act", count["act"]))
                v.tensor_copy(at("v_tm")[:, cur, i, :], pb(qb + 2))
                v.tensor_tensor(sb["scr"][:], at("q_tm")[:, i, :],
                                at("q_tm")[:, i, :], ALU.mult)
                v.tensor_reduce(sb["ssq"][:, i, :],
                                sb["scr"][:].rearrange("p (h d) -> p h d", h=8),
                                AXL.X, ALU.add)
                v.tensor_tensor(sb["scr"][:], at("k_tm")[:, i, :],
                                at("k_tm")[:, i, :], ALU.mult)
                ins = v.tensor_reduce(sb["ssk"][:, i, :],
                                      sb["scr"][:].rearrange("p (h d) -> p h d", h=8),
                                      AXL.X, ALU.add)
                tk = inc(ins, "dve")
                banks_read(tk, [qb, qb + 1, qb + 2])
                prep_toks.append(tk)
            VE(_)

        if f"qkv_w{w}" in dumps and seq == 0 and layer == 0:
            dump(f"q_raw_w{w}", at("q_tm")[:], prep_toks)
            dump(f"k_raw_w{w}", at("k_tm")[:], prep_toks)
            dump(f"v_raw_w{w}", at("v_tm")[:, cur], prep_toks)

        def _(s):
            wait(s, "scalar", prep_toks)
            for i in range(4):
                s.activation(sb["ssq"][:, i, :], sb["ssq"][:, i, :], AF.Sqrt)
                ins = s.activation(sb["ssk"][:, i, :], sb["ssk"][:, i, :], AF.Sqrt)
            inc(ins, "act")
        SE(_)
        t_sq = ("act", count["act"])

        def _(v):
            wait(v, "vector", t_sq)
            for nm in ("ssq", "ssk"):
                for i in range(4):
                    v.tensor_scalar(sb[nm][:, i, :], sb[nm][:, i, :], 1e-12, None,
                                    ALU.max)
            for nm in ("ssq", "ssk"):
                for i in range(4):
                    ins = v.reciprocal(sb[nm][:, i, :], sb[nm][:, i, :])
            inc(ins, "dve")
        VE(_)
        t_rs = ("dve", count["dve"])

        def qk_prep(e, nm, rn, ssn, qi, ci, si, scrn, iset=(0, 1, 2, 3)):
            for i in iset:
                src3 = at(nm)[:, i, :].rearrange("p (h d) -> p h d", h=8)
                dst = at(rn)[:, i, :]
                dst3 = dst.rearrange("p (h d) -> p h d", h=8)
                e.tensor_tensor(src3, src3,
                                sb[ssn][:, i, :, None].broadcast_to([128, 8, 64]),
                                ALU.mult)
                e.tensor_tensor(src3, src3,
                                sb["qsb"][:, None, qi, :].broadcast_to([128, 8, 64]),
                                ALU.mult)
                e.tensor_tensor(dst3, src3,
                                sb["rot"][:, i, None, ci, :].broadcast_to([128, 8, 64]),
                                ALU.mult)
                rv = at(nm)[:, i, :].rearrange("p (h a d) -> p h a d", h=8,
                                               a=2)[:, :, ::-1, :]
                e.tensor_tensor(sb[scrn][:].rearrange("p (h a d) -> p h a d", h=8, a=2),
                                rv,
                                sb["rot"][:, i, None, si, :]
                                .rearrange("p o (a d) -> p o a d", a=2)
                                .broadcast_to([128, 8, 2, 32]), ALU.mult)
                ins = e.tensor_tensor(dst, dst, sb[scrn][:], ALU.add)
            return ins

        tq = [None]
        tk_ = [None]

        def _(v):
            wait(v, "vector", t_rs, wtoks["qsb"], rot_tok)
            qk_prep(v, "q_tm", "qr_tm", "ssq", 0, 0, 1, "scr")
            tq[0] = inc(qk_prep(v, "k_tm", "kr_tm", "ssk", 1, 2, 3, "scr",
                                iset=(0, 1, 2)), "dve")
        VE(_)

        def _(g):
            wait(g, "gpsimd", t_rs, wtoks["qsb"], rot_tok)
            tk_[0] = inc(qk_prep(g, "k_tm", "kr_tm", "ssk", 1, 2, 3, "scr2",
                                 iset=(3,)), "pool")
        GE(_)
        attn_state["rot_free"] = [tq[0], tk_[0]]

        if f"qkv_w{w}" in dumps and seq == 0 and layer == 0:
            dump(f"qr_w{w}", at("qr_tm")[:], [tq[0]])
            dump(f"kr_w{w}", at("kr_tm")[:], [tk_[0]])

        fm_toks = []
        for p in range(4):
            fb = 6 if p % 2 == 0 else 4

            def _(te, p=p, fb=fb):
                wait(te, "tensor", tq[0], tk_[0])
                pe_banks(te, [fb, fb + 1])
                for i in range(4):
                    transpose(te, at("qr_tm")[:, i, 128 * p:128 * (p + 1)],
                              ps[:, 512 * fb + 128 * i:512 * fb + 128 * (i + 1)])
                for i in range(4):
                    ins = transpose(te, at("kr_tm")[:, i, 128 * p:128 * (p + 1)],
                                    ps[:, 512 * (fb + 1) + 128 * i:
                                       512 * (fb + 1) + 128 * (i + 1)])
                banks_written(inc(ins, "pe"), [fb, fb + 1])
            TE(_)

            def _(s, p=p, fb=fb):
                read_banks(s, "scalar", [fb, fb + 1])
                s.activation(at("q_fm")[:, p, :], pb(fb), AF.Copy)
                ins = s.activation(at("k_fm")[:, cur, p, :], pb(fb + 1), AF.Copy)
                tk2 = inc(ins, "act")
                banks_read(tk2, [fb, fb + 1])
                fm_toks.append(tk2)
            SE(_)

        if w == 1 and seq == 0 and layer == 0:
            dump("qfm_w1", at("q_fm")[:], fm_toks)
            dump("kfm_w1", at("k_fm")[:, cur], fm_toks)
            dump("kfmprev_w1", at("k_fm")[:, prv], fm_toks)
            dump("vtmprev_w1", at("v_tm")[:, prv], fm_toks)
            dump("vtmcur_w1", at("v_tm")[:, cur], fm_toks)

        otile_toks = []
        pv_toks = []
        for i in range(4):
            njt = 5 if w > 0 else i + 1
            msk_tok = {}

            def kf_vt(j, h, i=i):
                p = h // 2
                jt = tiles[i] - 4 + j if w > 0 else j
                jloc = jt - 4 * (w - 1)
                if w == 0:
                    return (at("k_fm")[:, cur, p, 128 * jt:128 * (jt + 1)],
                            at("v_tm")[:, cur, jt, 64 * h:64 * (h + 1)])
                if jloc >= 4:
                    return (at("k_fm")[:, cur, p, 128 * (jloc - 4):128 * (jloc - 3)],
                            at("v_tm")[:, cur, jloc - 4, 64 * h:64 * (h + 1)])
                return (at("k_fm")[:, prv, p, 128 * jloc:128 * (jloc + 1)],
                        at("v_tm")[:, prv, jloc, 64 * h:64 * (h + 1)])

            for h in range(8):
                p = h // 2
                hb = 64 * (h % 2)
                sreg = h % 4
                sbk = [0, 1] if h % 2 == 0 else [2, 3]

                def _(te, i=i, h=h, p=p, hb=hb, sbk=sbk, njt=njt):
                    wait(te, "tensor", fm_toks)
                    pe_banks(te, sbk)
                    base = 512 * sbk[0]
                    q = at("q_fm")[hb:hb + 64, p, 128 * i:128 * (i + 1)]
                    if w == 0:
                        ins = te.matmul(ps[:, base:base + 128 * njt], q,
                                        at("k_fm")[hb:hb + 64, cur, p, 0:128 * njt],
                                        start=True, stop=True)
                    else:
                        s0 = 128 * (4 - i)
                        ins = te.matmul(ps[:, base:base + s0], q,
                                        at("k_fm")[hb:hb + 64, prv, p, 128 * i:512],
                                        start=True, stop=True)
                        if s0 < 512:
                            ins = te.matmul(ps[:, base + s0:base + 512], q,
                                            at("k_fm")[hb:hb + 64, cur, p,
                                                       0:512 - s0],
                                            start=True, stop=True)
                        ins = te.matmul(ps[:, base + 512:base + 640], q,
                                        at("k_fm")[hb:hb + 64, cur, p,
                                                   512 - s0:640 - s0],
                                        start=True, stop=True)
                    banks_written(inc(ins, "pe"), sbk)
                TE(_)

                def _(v, i=i, sbk=sbk, njt=njt, sreg=sreg):
                    read_banks(v, "vector", sbk)
                    base = 512 * sbk[0]
                    if w > 0:
                        v.tensor_tensor(ps[:, base:base + 128], ps[:, base:base + 128],
                                        sb["tri_pq"][:, 0, :], ALU.add)
                    v.tensor_tensor(ps[:, base + 128 * (njt - 1):base + 128 * njt],
                                    ps[:, base + 128 * (njt - 1):base + 128 * njt],
                                    sb["tri_pq"][:, 1, :], ALU.add)
                    ins = v.reduce_max(sb["mx"][:, sreg:sreg + 1],
                                       ps[:, base:base + 128 * njt], AXL.X, negate=True)
                    inc(ins, "dve")
                VE(_)
                msk_tok[h] = ("dve", count["dve"])

                if h % 2 == 0:
                    continue
                h0 = h - 1
                s0, s1 = h0 % 4, h % 4

                def _(s, i=i, h0=h0, h1=h, njt=njt, s0=s0, s1=s1):
                    wait(s, "scalar", msk_tok[h0], msk_tok[h1],
                         attn_state.get(("pexp_free", s0)),
                         attn_state.get(("pexp_free", s1)),
                         attn_state.get(("rz_free", i % 2)) if h0 == 0 else None)
                    s.activation(at("pexp")[:, s0, 0:128 * njt],
                                 ps[:, 0:128 * njt],
                                 AF.Exp, bias=sb["mx"][:, s0:s0 + 1],
                                 scale=1.0,
                                 accum_out=sb["rz"][:, i % 2, h0:h0 + 1])
                    ins = s.activation(at("pexp")[:, s1, 0:128 * njt],
                                       ps[:, 512 * 2:512 * 2 + 128 * njt],
                                       AF.Exp, bias=sb["mx"][:, s1:s1 + 1],
                                       scale=1.0,
                                       accum_out=sb["rz"][:, i % 2, h1:h1 + 1])
                    tk3 = inc(ins, "act")
                    banks_read(tk3, [0, 1, 2, 3])
                SE(_)
                t_exp = ("act", count["act"])
                if w == 1 and i == 0 and h == 1 and seq == 0 and layer == 0:
                    dump("pexp_w1i0h0", at("pexp")[:, 0, :], t_exp)

                for hh in (h0, h):
                    sregh = hh % 4

                    def _(te, sregh=sregh, njt=njt):
                        wait(te, "tensor", t_exp)
                        pe_banks(te, [5, 6] if njt == 5 else [5])
                        for j in range(njt):
                            dst = (ps[:, 512 * 5 + 128 * j:512 * 5 + 128 * (j + 1)]
                                   if j < 4 else pb(6, 0, 128))
                            ins = transpose(te,
                                            at("pexp")[:, sregh, 128 * j:128 * (j + 1)],
                                            dst)
                        tk = inc(ins, "pe")
                        banks_written(tk, [5, 6] if njt == 5 else [5])
                        attn_state[("pexp_free", sregh)] = tk
                    TE(_)

                    def _(v, sregh=sregh, njt=njt):
                        read_banks(v, "vector", [5, 6] if njt == 5 else [5])
                        wait(v, "vector", attn_state.get(("pt_free", sregh)))
                        if njt == 5:
                            v.tensor_copy(at("pt")[:, sregh, 0:512], pb(5))
                            ins = v.tensor_copy(at("pt")[:, sregh, 512:640],
                                                pb(6, 0, 128))
                        else:
                            ins = v.tensor_copy(at("pt")[:, sregh, 0:128 * njt],
                                                pb(5, 0, 128 * njt))
                        tk4 = inc(ins, "dve")
                        banks_read(tk4, [5, 6] if njt == 5 else [5])
                    VE(_)
                    t_pt = ("dve", count["dve"])
                    if w == 1 and i == 0 and hh == 0 and seq == 0 and layer == 0:
                        dump("pt_w1i0h0", at("pt")[:, sregh, :], t_pt)

                    def _(te, i=i, hh=hh, sregh=sregh, njt=njt, t_pt=t_pt):
                        wait(te, "tensor", t_pt)
                        if hh == 0:
                            pe_banks(te, [4])
                        for j in range(njt):
                            _kf, vt = kf_vt(j, hh)
                            ins = te.matmul(pb(4, 64 * hh, 64),
                                            at("pt")[:, sregh, 128 * j:128 * (j + 1)],
                                            vt, start=(j == 0), stop=(j == njt - 1))
                        tk = inc(ins, "pe")
                        attn_state[("pt_free", sregh)] = tk
                        pv_toks.append(tk)
                        if hh == 7:
                            banks_written(tk, [4])
                    TE(_)

            def _(v, i=i):
                read_banks(v, "vector", [4])
                v.reciprocal(sb["rz"][:, i % 2, :], sb["rz"][:, i % 2, :])
                v.tensor_copy(sb["scr"][:, 0:1], sb["rz"][:, i % 2, 0:1])
                ins = v.tensor_tensor(
                    at("o_tm")[:, i, :].rearrange("p (h d) -> p h d", h=8),
                    pb(4).rearrange("p (h d) -> p h d", h=8),
                    sb["rz"][:, i % 2, :, None].broadcast_to([128, 8, 64]), ALU.mult)
                tk5 = inc(ins, "dve")
                banks_read(tk5, [4])
                otile_toks.append(tk5)
                attn_state[("rz_free", i % 2)] = tk5
            VE(_)

        if f"qkv_w{w}" in dumps and seq == 0 and layer == 0:
            dump(f"o_w{w}", at("o_tm")[:], otile_toks)
        attn_state["kv_free"] = list(pv_toks[-8:])

        s_upd = []
        for i in range(4):
            ob = 5 if i % 2 == 0 else 6
            wb = 2 if i % 2 == 0 else 0

            def _(te, i=i, ob=ob):
                wait(te, "tensor", otile_toks[i])
                pe_banks(te, [ob])
                for c in range(4):
                    ins = transpose(te, at("o_tm")[:, i, 128 * c:128 * (c + 1)],
                                    ps[:, 512 * ob + 128 * c:512 * ob + 128 * (c + 1)])
                banks_written(inc(ins, "pe"), [ob])
            TE(_)

            def _(s, i=i, ob=ob):
                read_banks(s, "scalar", [ob])
                ins = s.activation(at("o_fm")[:, i % 2],
                                   pb(ob).rearrange("p (c n) -> p c n", c=4), AF.Copy)
                tk6 = inc(ins, "act")
                banks_read(tk6, [ob])
            SE(_)
            t_ofm = ("act", count["act"])

            def _(te, i=i, wb=wb):
                wait(te, "tensor", t_ofm)
                pe_banks(te, [wb, wb + 1])
                for c in range(4):
                    if i == 0 and w == 0:
                        wait(te, "tensor", wtoks[("wo", c)])
                    for half in range(2):
                        ins = te.matmul(pb(wb + half), at("o_fm")[:, i % 2, c, :],
                                        at("wo")[:, c, 512 * half:512 * (half + 1)],
                                        start=(c == 0), stop=(c == 3))
                banks_written(inc(ins, "pe"), [wb, wb + 1])
            TE(_)

            def _(v, i=i, wb=wb):
                read_banks(v, "vector", [wb, wb + 1])
                t = tiles[i]
                v.tensor_tensor(sb["s"][:, t, 0:512], sb["s"][:, t, 0:512], pb(wb),
                                ALU.add)
                ins = v.tensor_tensor(sb["s"][:, t, 512:1024], sb["s"][:, t, 512:1024],
                                      pb(wb + 1), ALU.add)
                tk7 = inc(ins, "dve")
                banks_read(tk7, [wb, wb + 1])
                s_upd.append(tk7)
            VE(_)

        return s_upd

    def load_rot_win(w, rot_free):
        tok = [None]

        def _(sy):
            wait(sy, "sync", rot_free)
            tok[0] = dq(sy, "sync", "qs", sb["rot"][:],
                        C["c_rot"][:].rearrange("(n p) f d -> n p f d", p=128)[
                            4 * w:4 * w + 4].rearrange("n p f d -> p n f d"))
        SY(_)
        return tok[0]

    def load_mlp_weights(layer, arena_free, need_b1):
        toks = {}

        def _(v):
            wait(v, "vector", arena_free)
            v.memset(mt("w2res")[32:64, 21, :], 0.0)
            ins = v.memset(mt("w2res")[64:128, 21, :], 0.0)
            toks["w2z"] = inc(ins, "dve")
        VE(_)

        if need_b1:
            def _(v):
                wait(v, "vector", arena_free)
                ins = v.memset(mt("b1blk"), 0.0)
                toks["b1z"] = inc(ins, "dve")
            VE(_)

        def _(g):
            wait(g, "gpsimd", arena_free, toks["w2z"], toks.get("b1z"))
            w2 = W["ffn_w2"][layer]
            toks["w2a"] = dq(g, "gpsimd", "qg", mt("w2res")[:, 0:21, :],
                             w2[0:2688].rearrange("(c p) n -> p c n", p=128))
            toks["w2b"] = dq(g, "gpsimd", "qg", mt("w2res")[0:42, 21, :],
                             w2[2688:2730])
            toks["b2"] = dq(g, "gpsimd", "qg", mt("w2res")[64:65, 21, :],
                            W["ffn_b2"][layer][None, :])
            if need_b1:
                b1 = W["ffn_b1"][layer]
                dq(g, "gpsimd", "qg", mt("b1blk")[0:21, :],
                   b1[0:2688].rearrange("(m n) -> m n", n=128))
                dq(g, "gpsimd", "qg", mt("b1blk")[21:22, 0:42], b1[2688:2730][None, :])
                dq(g, "gpsimd", "qg", mt("b1blk")[22:43, :],
                   b1[2730:5418].rearrange("(m n) -> m n", n=128))
                toks["b1"] = dq(g, "gpsimd", "qg", mt("b1blk")[43:44, 0:42],
                                b1[5418:5460][None, :])
        GE(_)
        return toks

    def ffn_mlp(seq, layer, wtoks, s_toks, arena_free):
        cp_ready = state["cp_done"][layer]
        mw = load_mlp_weights(layer, arena_free, need_b1=not cp_ready)

        if not cp_ready:
            def _(te):
                wait(te, "tensor", mw["b1"])
                pe_banks(te, [5])
                ins = te.matmul(pb(5, 0, 44), mt("b1blk"),
                                sb["I"][0:44, 0:44], start=True, stop=True)
                banks_written(inc(ins, "pe"), [5])
            TE(_)

            def _(v):
                read_banks(v, "vector", [5])
                ins = v.tensor_copy(sb["cpcol"][:, layer, :], pb(5, 0, 44))
                tk = inc(ins, "dve")
                banks_read(tk, [5])
                state["cp_tok"][layer] = tk
            VE(_)

        def _(v):
            wait(v, "vector", s_toks, zeros_tok[0])
            for t in range(NT):
                stt = sb["st4"][:, t, :]
                v.reduce_sum(stt[:, 0:1], sb["s"][:, t, 0:512], AXL.X)
                ins = v.reduce_sum(stt[:, 1:2], sb["s"][:, t, 512:1024], AXL.X)
            inc(ins, "dve")
        VE(_)
        t_sums = ("dve", count["dve"])

        def _(s):
            wait(s, "scalar", s_toks)
            for t in range(NT):
                s.activation(sb["scr"][:], sb["s"][:, t, 0:512], AF.Square,
                             accum_out=sb["st4"][:, t, 2:3])
                ins = s.activation(sb["scr2"][:], sb["s"][:, t, 512:1024],
                                   AF.Square, accum_out=sb["st4"][:, t, 3:4])
            inc(ins, "act")
        SE(_)
        t_st = [t_sums, ("act", count["act"])]

        def _(v):
            wait(v, "vector", s_toks, arena_free)
            for j in range(4):
                ins = v.memset(srow_seg("s", j, 0, 1), 0.0)
            inc(ins, "dve")
        VE(_)
        z0 = ("dve", count["dve"])
        srow_toks = [z0]
        for tg in range(4):
            def _(te, tg=tg):
                wait(te, "tensor", t_st)
                pe_banks(te, [4, 5, 6, 7])
                for tt in range(4):
                    t = 4 * tg + tt
                    for j in range(4):
                        ins = te.matmul(ps[0:1, 512 * (4 + j) + 128 * tt:
                                           512 * (4 + j) + 128 * (tt + 1)],
                                        sb["st4"][:, t, j:j + 1], sb["If"][:],
                                        start=True, stop=True)
                banks_written(inc(ins, "pe"), [4, 5, 6, 7])
            TE(_)

            def _(v, tg=tg):
                read_banks(v, "vector", [4, 5, 6, 7])
                for j in range(4):
                    ins = v.tensor_copy(
                        srow_seg("s", j, 1 + 512 * tg, 1 + 512 * (tg + 1)),
                        ps[0:1, 512 * (4 + j):512 * (4 + j) + 512])
                tk = inc(ins, "dve")
                banks_read(tk, [4, 5, 6, 7])
                srow_toks.append(tk)
            VE(_)

        def _(v):
            wait(v, "vector", srow_toks)
            v.tensor_tensor(srow_seg("m", 0, 1, 2049), srow_seg("s", 0, 1, 2049),
                            srow_seg("s", 1, 0, 2048), ALU.add)
            v.tensor_tensor(srow_seg("m", 1, 1, 2049), srow_seg("s", 2, 1, 2049),
                            srow_seg("s", 3, 0, 2048), ALU.add)
            v.tensor_scalar(srow_seg("m", 0, 1, 2049), srow_seg("m", 0, 1, 2049),
                            1.0 / D, None, ALU.mult)
            v.tensor_scalar(srow_seg("m", 1, 1, 2049), srow_seg("m", 1, 1, 2049),
                            1.0 / D, EPS, ALU.mult, ALU.add)
            v.scalar_tensor_tensor(out=srow_seg("m", 2, 1, 2049),
                                   in0=srow_seg("m", 0, 1, 2049), scalar=-1.0,
                                   in1=srow_seg("m", 0, 1, 2049),
                                   op0=ALU.mult, op1=ALU.mult)
            ins = v.tensor_tensor(srow_seg("m", 1, 1, 2049), srow_seg("m", 1, 1, 2049),
                                  srow_seg("m", 2, 1, 2049), ALU.add)
            inc(ins, "dve")
        VE(_)
        t_mr = ("dve", count["dve"])

        def _(s):
            wait(s, "scalar", t_mr)
            s.activation(srow_seg("m", 1, 1, 2049), srow_seg("m", 1, 1, 2049),
                         AF.Ln)
            ins = s.activation(srow_seg("m", 1, 1, 2049), srow_seg("m", 1, 1, 2049),
                               AF.Exp, scale=-0.5)
            inc(ins, "act")
        SE(_)
        t_sqr = ("act", count["act"])

        def _(v):
            wait(v, "vector", t_sqr)
            ins = v.tensor_tensor(srow_seg("m", 3, 1, 2049), srow_seg("m", 0, 1, 2049),
                                  srow_seg("m", 1, 1, 2049), ALU.mult)
            inc(ins, "dve")
        VE(_)
        t_rstd = ("dve", count["dve"])

        mc_toks = []
        for tg in range(4):
            def _(te, tg=tg):
                wait(te, "tensor", t_rstd)
                pe_banks(te, [4])
                for tt in range(4):
                    t = 4 * tg + tt
                    srcs = [srow_seg("m", 0, 1 + 128 * t, 1 + 128 * (t + 1)),
                            srow_seg("m", 1, 1 + 128 * t, 1 + 128 * (t + 1)),
                            srow_seg("m", 0, 2 + 128 * t, 2 + 128 * (t + 1)),
                            srow_seg("m", 1, 2 + 128 * t, 2 + 128 * (t + 1))]
                    for j, srcap in enumerate(srcs):
                        ins = te.matmul(pb(4, 128 * tt + 32 * j, 1), srcap,
                                        sb["onesr"][64:65, 0:1],
                                        start=True, stop=True)
                banks_written(inc(ins, "pe"), [4])
            TE(_)

            def _(v, tg=tg):
                read_banks(v, "vector", [4])
                for tt in range(4):
                    t = 4 * tg + tt
                    ins = v.tensor_copy(sb["mcols"][:, t, :],
                                        pb(4)[:, 128 * tt:128 * tt + 128:32])
                tk = inc(ins, "dve")
                banks_read(tk, [4])
                mc_toks.append(tk)
            VE(_)

        if seq == 0 and layer == 0:
            dump("mcols_d", sb["mcols"][:], mc_toks)
            dump("st4_d", sb["st4"][:], t_st)

        def _(te):
            wait(te, "tensor", t_rstd)
            pe_banks(te, [5])
            ins = te.matmul(pb(5, 0, 1), sb["onesr"][64:65, :],
                            srow_seg("m", 3, 1, 2), start=True, stop=True)
            banks_written(inc(ins, "pe"), [5])
        TE(_)
        halo0_bank_tok = ("pe", count["pe"])

        s_out = list(s_toks) if isinstance(s_toks, list) else [s_toks] * NT
        u_free = [arena_free]
        y_free = [arena_free]
        gg_free = [None]
        w1_free = {0: None, 1: None}
        cp_tok = state["cp_tok"][layer]
        p12_T = [None]
        for half in range(2):
            yfm_toks = []
            for tt in range(8):
                t = 8 * half + tt

                def _(v, t=t, tt=tt):
                    wait(v, "vector", mc_toks, arena_free, p12_T[0])
                    v.tensor_scalar(mt("p12")[:, 0:512], sb["s"][:, t, 0:512],
                                    sb["mcols"][:, t, 0:1], sb["mcols"][:, t, 1:2],
                                    ALU.subtract, ALU.mult)
                    ins = v.tensor_scalar(mt("p12")[:, 512:1024],
                                          sb["s"][:, t, 512:1024],
                                          sb["mcols"][:, t, 2:3],
                                          sb["mcols"][:, t, 3:4],
                                          ALU.subtract, ALU.mult)
                    inc(ins, "dve")
                VE(_)
                t_p12 = ("dve", count["dve"])

                def _(te, t=t, tt=tt):
                    wait(te, "tensor", t_p12, y_free[0])
                    pe_banks(te, [6, 7])
                    for c in range(8):
                        ins = transpose(te, mt("p12")[:, 128 * c:128 * (c + 1)],
                                        ps[:, 512 * 6 + 128 * c:512 * 6 + 128 * (c + 1)])
                    tk = inc(ins, "pe")
                    banks_written(tk, [6, 7])
                    p12_T[0] = tk
                TE(_)

                def _(s, t=t, tt=tt):
                    read_banks(s, "scalar", [6, 7])
                    ins = s.activation(mt("y_fm")[:, :, 1 + 128 * tt:1 + 128 * (tt + 1)],
                                       ps[:, 512 * 6:512 * 6 + 1024]
                                       .rearrange("p (c n) -> p c n", c=8), AF.Copy)
                    tk = inc(ins, "act")
                    banks_read(tk, [6, 7])
                    yfm_toks.append(tk)
                SE(_)

            def _(v, half=half):
                wait(v, "vector", yfm_toks)
                if half == 0:
                    read_banks(v, "vector", [5])
                    wait(v, "vector", halo0_bank_tok)
                    ins = v.tensor_scalar(mt("y_fm")[:, 4:8, 0:1],
                                          pb(5, 0, 1)[:, None, :]
                                          .broadcast_to([128, 4, 1]),
                                          -1.0, None, ALU.mult)
                    banks_read(inc(ins, "dve"), [5])
                    yfm_toks.append(("dve", count["dve"]))
                else:
                    ins = v.tensor_copy(mt("y_fm")[:, 4:8, 0:1],
                                        sb["mx"][:, 0:4, None])
                    yfm_toks.append(inc(ins, "dve"))
            VE(_)

            if half == 0:
                def _(v):
                    wait(v, "vector", yfm_toks)
                    ins = v.tensor_copy(sb["mx"][:, 0:4],
                                        mt("y_fm")[:, 4:8, 1024])
                    yfm_toks.append(inc(ins, "dve"))
                VE(_)

            def _(v, half=half):
                wait(v, "vector", yfm_toks, wtoks["lncols"])
                ins = v.tensor_tensor(mt("y_fm")[:], mt("y_fm")[:],
                                      sb["lgcol"][:, :, None]
                                      .broadcast_to([128, 8, 1025]), ALU.mult)
                inc(ins, "dve")
            VE(_)
            t_y = ("dve", count["dve"])
            if seq == 0 and layer == 0 and half == 0:
                dump("yfm_h0", mt("y_fm")[:], t_y)
                dump("cpcol_d", sb["cpcol"][:, 0, :], state["cp_tok"][layer])
            y_free[0] = None

            def _(v):
                wait(v, "vector", u_free[0], arena_free)
                v.memset(mt("u_fm")[32:64, 21, :], 0.0)
                v.memset(mt("u_fm")[64:128, 21, :], 0.0)
                v.memset(sb["scr"][:, 0:1], 0.0)
                ins = v.memset(mt("u_fm")[64:65, 21, :], 1.0)
                inc(ins, "dve")
            VE(_)
            uz_tok = ("dve", count["dve"])

            glu_toks = []
            for mp in range(22):
                mrows = 42 if mp == 21 else 128
                ld2 = {}
                for ga in range(2):
                    col0 = (2730 if ga == 0 else 0) + 128 * mp
                    slot = ga

                    def _(g, col0=col0, slot=slot, mrows=mrows, ga=ga):
                        wait(g, "gpsimd", w1_free[slot])
                        ld2[ga] = dq(g, "gpsimd", "qg",
                                     mt("w1t")[:, slot, :, 0:mrows],
                                     W["ffn_w1"][layer][:, col0:col0 + mrows]
                                     .rearrange("(c p) n -> p c n", p=128))
                    GE(_)

                    if not cp_ready and half == 0:
                        mcol = (1 - ga) * 22 + mp

                        def _(te, slot=slot, mrows=mrows, w1_tok=ld2[ga]):
                            wait(te, "tensor", w1_tok, wtoks["lncols"],
                                 state["cp_tok"][layer])
                            pe_banks(te, [5])
                            for c in range(8):
                                ins = te.matmul(ps[0:mrows, 512 * 5:512 * 5 + 1],
                                                mt("w1t")[:, slot, c, 0:mrows],
                                                sb["lbcol"][:, c:c + 1],
                                                start=(c == 0), stop=(c == 7))
                            banks_written(inc(ins, "pe"), [5])
                        TE(_)

                        def _(v, mcol=mcol, mrows=mrows):
                            read_banks(v, "vector", [5])
                            wait(v, "vector", state["cp_tok"][layer])
                            ins = v.tensor_tensor(
                                sb["cpcol"][0:mrows, layer, mcol:mcol + 1],
                                sb["cpcol"][0:mrows, layer, mcol:mcol + 1],
                                ps[0:mrows, 512 * 5:512 * 5 + 1], ALU.add)
                            tk = inc(ins, "dve")
                            banks_read(tk, [5])
                            state["cp_tok"][layer] = tk
                        VE(_)

                for grp in range(2):
                    for ga in range(2):
                        bsel = (0 if ga == 0 else 1) + 2 * grp

                        def _(te, mrows=mrows, grp=grp, bsel=bsel, ga=ga,
                              w1_tok=ld2[ga]):
                            wait(te, "tensor", w1_tok, t_y)
                            pe_banks(te, [bsel])
                            for c in range(8):
                                off = (1 if c < 4 else 0) + 512 * grp
                                ins = te.matmul(pb(bsel, 0, 512)[0:mrows, :],
                                                mt("w1t")[:, ga, c, 0:mrows],
                                                mt("y_fm")[:, c, off:off + 512],
                                                start=(c == 0), stop=(c == 7))
                            tk = inc(ins, "pe")
                            banks_written(tk, [bsel])
                            w1_free[ga] = tk
                            if ga == 1 and mp == 21 and grp == 1:
                                y_free[0] = tk
                        TE(_)

                def _(s, mp=mp, mrows=mrows):
                    read_banks(s, "scalar", [0, 2])
                    wait(s, "scalar", state["cp_tok"][layer], u_free[0], uz_tok)
                    for grp in range(2):
                        ins = s.activation(
                            mt("u_fm")[0:mrows, mp, 512 * grp:512 * (grp + 1)],
                            pb(2 * grp, 0, 512)[0:mrows, :], AF.Gelu,
                            bias=sb["cpcol"][0:mrows, layer, 22 + mp:23 + mp],
                            scale=1.0)
                    tk = inc(ins, "act")
                    banks_read(tk, [0, 2])
                SE(_)
                gel_tok = ("act", count["act"])

                for grp in range(2):
                    def _(v, mp=mp, grp=grp, mrows=mrows, gel_tok=gel_tok):
                        bsel = 1 + 2 * grp
                        read_banks(v, "vector", [bsel])
                        wait(v, "vector", gel_tok)
                        ins = v.scalar_tensor_tensor(
                            out=mt("u_fm")[0:mrows, mp, 512 * grp:512 * (grp + 1)],
                            in0=pb(bsel, 0, 512)[0:mrows, :],
                            scalar=sb["cpcol"][0:mrows, layer, mp:mp + 1],
                            in1=mt("u_fm")[0:mrows, mp, 512 * grp:512 * (grp + 1)],
                            op0=ALU.add, op1=ALU.mult)
                        tk = inc(ins, "dve")
                        banks_read(tk, [bsel])
                        glu_toks.append(tk)
                    VE(_)

            if seq == 0 and layer == 0:
                dump(f"ufm_h{half}", mt("u_fm")[:], glu_toks)
                dump("w2res_d", mt("w2res")[:], [mw["w2a"], mw["w2b"], mw["b2"], mw["w2z"]])
                dump(f"yfm_hh{half}", mt("y_fm")[:], t_y)

            for tt in range(8):
                t = 8 * half + tt
                obk = [4, 5] if tt % 2 == 0 else [6, 7]

                def _(te, t=t, tt=tt, obk=obk):
                    wait(te, "tensor", glu_toks, uz_tok, mw["w2a"], mw["w2b"], mw["b2"], mw["w2z"])
                    pe_banks(te, obk)
                    DBG1 = (half == 1 and tt == 1 and seq == 0 and layer == 0
                            and ("w2psum_h1t1" in dumps))
                    if DBG1:
                        def _(v):
                            wait(v, "vector", glu_toks, uz_tok)
                            ins = v.tensor_copy(sb["scr2"][:, 0:128],
                                                mt("u_fm")[:, 0, 128:256])
                            inc(ins, "dve")
                        VE(_)
                        dump("u_at_mm", sb["scr2"][:], ("dve", count["dve"]))
                    for h2 in range(2):
                        kl = [0] if DBG1 else list(range(22))
                        for k in kl:
                            kr = 65 if k == 21 else 128
                            lhs = mt("u_fm")[0:kr, k, 128 * tt:128 * (tt + 1)]
                            ins = te.matmul(pb(obk[h2], 0, 512),
                                            lhs,
                                            mt("w2res")[0:kr, k,
                                                        512 * h2:512 * (h2 + 1)],
                                            start=(k == kl[0]), stop=(k == kl[-1]))
                    tk = inc(ins, "pe")
                    banks_written(tk, obk)
                    u_free[0] = tk
                TE(_)

                def _(v, t=t, obk=obk, half=half, tt=tt):
                    read_banks(v, "vector", obk)
                    if half == 1 and tt == 1 and seq == 0 and layer == 0:
                        v.tensor_copy(sb["scr"][:], pb(obk[0], 0, 512))
                    v.tensor_tensor(sb["s"][:, t, 0:512], sb["s"][:, t, 0:512],
                                    pb(obk[0], 0, 512), ALU.add)
                    ins = v.tensor_tensor(sb["s"][:, t, 512:1024],
                                          sb["s"][:, t, 512:1024],
                                          pb(obk[1], 0, 512), ALU.add)
                    tk = inc(ins, "dve")
                    banks_read(tk, obk)
                    s_out[t] = tk
                    if half == 1 and tt == 1 and seq == 0 and layer == 0:
                        dump("w2psum_h1t1", sb["scr"][:], tk)
                VE(_)

        if not cp_ready:
            state["cp_done"][layer] = True
        return s_out

    sb["wx2stg"] = keep(nc.sbuf_tensor("wx2stg", [128, 32], BF16))

    def rwkv_prep():

        wx2_toks = []
        last_evac = [None]
        for n in range(5):
            for c in range(8):
                stg_tok = [None]

                def _(g, n=n, c=c):
                    wait(g, "gpsimd", last_evac[0])
                    stg_tok[0] = dq(g, "gpsimd", "qg", sb["wx2stg"][:],
                                    W["attn_W_x2"][128 * c:128 * (c + 1), n, :])
                GE(_)

                def _(te, n=n, c=c):
                    wait(te, "tensor", stg_tok[0])
                    pe_banks(te, [6])
                    ins = transpose(te, sb["wx2stg"][:],
                                    ps[0:32, 512 * 6:512 * 6 + 128])
                    banks_written(inc(ins, "pe"), [6])
                TE(_)

                def _(v, n=n, c=c):
                    read_banks(v, "vector", [6])
                    ins = v.tensor_copy(
                        sb["wx2t"][32 * (n % 4):32 * (n % 4) + 32, n // 4, c, :],
                        pb(6, 0, 128)[0:32, :])
                    tk = inc(ins, "dve")
                    banks_read(tk, [6])
                    last_evac[0] = tk
                    wx2_toks.append(tk)
                VE(_)

        row_jobs = [("attn_mu_x", W["attn_mu_x"][None, :], 0, 8),
                    ("xb0", W["attn_x_bias"][0][None, :], 8, 8),
                    ("xb1", W["attn_x_bias"][1][None, :], 16, 8),
                    ("xb2", W["attn_x_bias"][2][None, :], 24, 8),
                    ("xb3", W["attn_x_bias"][3][None, :], 32, 8),
                    ("xb4", W["attn_x_bias"][4][None, :], 40, 8),
                    ("u", W["attn_u"][:].rearrange("a b -> (a b)")[None, :], 48, 4),
                    ("gn_w", W["attn_gn_w"][None, :], 52, 8),
                    ("gn_b", W["attn_gn_b"][None, :], 60, 8)]
        prev_cons = [None]
        for nm, ap_, col0, nch in row_jobs:
            rtok = [None]

            def _(g, ap_=ap_):
                wait(g, "gpsimd", prev_cons[0])
                rtok[0] = dq(g, "gpsimd", "qg", sb["lnrow"][0:1, 0:128 * nch], ap_)
            GE(_)

            def _(te, col0=col0, nch=nch):
                wait(te, "tensor", rtok[0])
                pe_banks(te, [6])
                for c in range(nch):
                    ins = te.matmul(pb(6, c, 1),
                                    sb["lnrow"][0:1, 128 * c:128 * (c + 1)],
                                    sb["onesr"][0:1, 0:1], start=True, stop=True)
                banks_written(inc(ins, "pe"), [6])
            TE(_)

            def _(v, col0=col0, nch=nch):
                read_banks(v, "vector", [6])
                ins = v.tensor_copy(sb["bcols"][:, col0:col0 + nch], pb(6, 0, nch))
                tk = inc(ins, "dve")
                banks_read(tk, [6])
                prev_cons[0] = tk
                wx2_toks.append(tk)
            VE(_)
        return wx2_toks

    def bc(col, n=1):
        return sb["bcols"][:, col:col + n]

    def load_rwkv_w(slot, dram_ap, ncols, free_tok, scale_col=None):
        tok = [None]

        def _(g):
            wait(g, "gpsimd", free_tok)
            tok[0] = dq(g, "gpsimd", "qg", bt("wsl")[:, slot, :, 0:ncols],
                        dram_ap.rearrange("(c p) n -> p c n", p=128))
        GE(_)
        if scale_col is not None:
            def _(v):
                wait(v, "vector", tok[0])
                for c in range(8):
                    ins = v.tensor_scalar(bt("wsl")[:, slot, c, 0:ncols],
                                          bt("wsl")[:, slot, c, 0:ncols],
                                          bc(scale_col + c), None, ALU.mult)
                tok[0] = inc(ins, "dve")
            VE(_)
        return tok[0]

    def rwkv_quarter(seq, qi, xsrc, xdst, bstate, prep_tok):
        tiles = [4 * qi + i for i in range(4)]
        NCH = 16

        ln_toks = []
        xp_cons = []
        ldts = {}
        for ii, t in enumerate(tiles):
            def _(sy, ii=ii, t=t):
                wait(sy, "sync", xp_last[ii % 2])
                ldts[ii] = dq(sy, "sync", "qs", sb["xp"][:, ii % 2, :],
                              xsrc.rearrange("(n p) d -> n p d", p=128)[t])
            SY(_)
            if ii % 2 == 0:
                continue
            i0_, i1_ = ii - 1, ii

            def _(v, i0_=i0_, i1_=i1_):
                wait(v, "vector", ldts[i0_], ldts[i1_], bstate.get("arena_free"))
                for tt in (i0_, i1_):
                    v.reduce_sum(sb["st4"][:, tt, 0:1], sb["xp"][:, tt % 2, :],
                                 AXL.X)
                for tt in (i0_, i1_):
                    st = sb["st4"][:, tt, :]
                    v.tensor_scalar(st[:, 0:1], st[:, 0:1], 1.0 / D, None,
                                    ALU.mult)
                for tt in (i0_, i1_):
                    st = sb["st4"][:, tt, :]
                    v.scalar_tensor_tensor(out=st[:, 3:4], in0=st[:, 0:1],
                                           scalar=-1.0, in1=st[:, 0:1],
                                           op0=ALU.mult, op1=ALU.mult)
                for tt in (i0_, i1_):
                    st = sb["st4"][:, tt, :]
                    ins = v.tensor_scalar(st[:, 3:4], st[:, 3:4], EPS, None,
                                          ALU.add)
                inc(ins, "dve")
            VE(_)
            t_pre = ("dve", count["dve"])

            def _(s, i0_=i0_, i1_=i1_):
                wait(s, "scalar", t_pre)
                for tt in (i0_, i1_):
                    st = sb["st4"][:, tt, :]
                    s.activation(bt("xln_bf")[:, tt, :], sb["xp"][:, tt % 2, :],
                                 AF.Square, accum_out=st[:, 1:2])
                    ins = s.activation(st[:, 1:2], st[:, 1:2], AF.Sqrt,
                                       bias=st[:, 3:4], scale=1.0 / D)
                inc(ins, "act")
            SE(_)
            t_sq = ("act", count["act"])

            def _(v, i0_=i0_, i1_=i1_):
                wait(v, "vector", t_sq)
                for tt in (i0_, i1_):
                    v.reciprocal(sb["st4"][:, tt, 1:2], sb["st4"][:, tt, 1:2])
                v.tensor_copy(sb["scr"][:, 0:1], sb["st4"][:, i0_, 0:1])
                for tt in (i0_, i1_):
                    st = sb["st4"][:, tt, :]
                    ins = v.tensor_scalar(bt("xln_bf")[:, tt, :],
                                          sb["xp"][:, tt % 2, :],
                                          st[:, 0:1], st[:, 1:2],
                                          ALU.subtract, ALU.mult)
                    tk = inc(ins, "dve")
                    xp_last[tt % 2] = tk
                    xp_cons.append(tk)
                    ln_toks.append(tk)
            VE(_)

        fm_toks = []
        for ii in range(4):
            def _(te, ii=ii):
                wait(te, "tensor", ln_toks[ii], bstate.get("arena_free"))
                pe_banks(te, [6, 7])
                for c in range(8):
                    ins = transpose(te, bt("xln_bf")[:, ii, 128 * c:128 * (c + 1)],
                                    ps[:, 512 * 6 + 128 * c:512 * 6 + 128 * (c + 1)])
                banks_written(inc(ins, "pe"), [6, 7])
            TE(_)

            def _(s, ii=ii):
                read_banks(s, "scalar", [6, 7])
                ins = s.activation(bt("xln_fm")[:, :, 1 + 128 * ii:1 + 128 * (ii + 1)],
                                   ps[:, 512 * 6:512 * 6 + 1024]
                                   .rearrange("p (c n) -> p c n", c=8), AF.Copy)
                tk = inc(ins, "act")
                banks_read(tk, [6, 7])
                fm_toks.append(tk)
            SE(_)

        def _(v):
            wait(v, "vector", bstate.get("arena_free"))
            if qi == 0:
                ins = v.memset(bt("xln_fm")[:, :, 0:1], 0.0)
            else:
                ins = v.tensor_copy(bt("xln_fm")[:, :, 0:1], sb["halo_b"][:])
            fm_toks.append(inc(ins, "dve"))
        VE(_)

        def _(v):
            wait(v, "vector", fm_toks)
            ins = v.tensor_copy(sb["halo_b"][:], bt("xln_fm")[:, :, 512:513])
            fm_toks.append(inc(ins, "dve"))
        VE(_)

        def _(v):
            wait(v, "vector", fm_toks, prep_tok)
            ins = v.tensor_tensor(bt("delta")[:], bt("xln_fm")[:, :, 0:512],
                                  bt("xln_fm")[:, :, 1:513], ALU.subtract)
            inc(ins, "dve")
        VE(_)
        t_delta = ("dve", count["dve"])

        def _(v):
            wait(v, "vector", t_delta)
            for c in range(8):
                ins = v.scalar_tensor_tensor(
                    out=bt("xi")[:, c, :], in0=bt("delta")[:, c, :],
                    scalar=bc(c), in1=bt("xln_fm")[:, c, 1:513],
                    op0=ALU.mult, op1=ALU.add)
            inc(ins, "dve")
        VE(_)
        t_xmu = ("dve", count["dve"])

        wx1_tok = load_rwkv_w(0, W["attn_W_x1"], 160, bstate.get("wsl_free", [None, None])[0])
        for m2 in range(2):
            mr = 128 if m2 == 0 else 32

            def _(te, m2=m2, mr=mr):
                wait(te, "tensor", t_xmu, wx1_tok)
                pe_banks(te, [m2])
                for c in range(8):
                    ins = te.matmul(pb(m2, 0, 512)[0:mr, :],
                                    bt("wsl")[:, 0, c, 128 * m2:128 * m2 + mr],
                                    bt("xi")[:, c, :], start=(c == 0), stop=(c == 7))
                banks_written(inc(ins, "pe"), [m2])
            TE(_)

            def _(s, m2=m2, mr=mr):
                read_banks(s, "scalar", [m2])
                ins = s.activation(bt("low")[0:mr, m2, :], pb(m2, 0, 512)[0:mr, :],
                                   AF.Tanh)
                tk = inc(ins, "act")
                banks_read(tk, [m2])
            SE(_)
        t_low = ("act", count["act"])

        projs = [("r", W["attn_W_r"], 512, "r_tm", None),
                 ("w", None, 0, None, None),
                 ("k", W["attn_W_k"], 512, "k_tm2", None),
                 ("v", W["attn_W_v"], 1024, "v_tm2", None),
                 ("g", W["attn_W_g"], 1024, "g_tm", None)]
        xi_free = [None]
        wsl_free = bstate.get("wsl_free", [None, None])
        proj_toks = {}
        for bi, (bname, wdram, dout, obuf, _sc) in enumerate(projs):
            lowrow = 32 * (bi % 4)
            lowcol = bi // 4
            xi_toks = []
            for c in range(8):
                bk = c % 2

                def _(te, bi=bi, c=c, bk=bk, lowrow=lowrow, lowcol=lowcol):
                    wait(te, "tensor", t_low, prep_tok, xi_free[0])
                    pe_banks(te, [bk])
                    tp = (lowrow, 0) if lowrow else None
                    ins = te.matmul(pb(bk, 0, 512),
                                    sb["wx2t"][lowrow:lowrow + 32, bi // 4, c, :],
                                    bt("low")[lowrow:lowrow + 32, lowcol, :],
                                    start=True, stop=True, tile_position=tp)
                    banks_written(inc(ins, "pe"), [bk])
                TE(_)

                def _(v, bi=bi, c=c, bk=bk):
                    read_banks(v, "vector", [bk])
                    wait(v, "vector", t_xmu, xi_free[0])
                    v.scalar_tensor_tensor(
                        out=sb["scr2"][:].bitcast(BF16)[:, 0:512],
                        in0=pb(bk, 0, 512), scalar=bc(8 + 8 * bi + c),
                        in1=bt("delta")[:, c, :], op0=ALU.add, op1=ALU.mult)
                    ins = v.tensor_tensor(bt("xi")[:, c, :],
                                          sb["scr2"][:].bitcast(BF16)[:, 0:512],
                                          bt("xln_fm")[:, c, 1:513], ALU.add)
                    tk = inc(ins, "dve")
                    banks_read(tk, [bk])
                    xi_toks.append(tk)
                VE(_)
            t_xi = list(xi_toks)

            if bname == "w":
                ww1_tok = load_rwkv_w(1, W["attn_W_w1"], 64, wsl_free[1])
                aug_tok = [None]

                def _(g):
                    wait(g, "gpsimd", bstate.get("ww2_free"))
                    dq(g, "gpsimd", "qg", bt("ww2aug")[0:64, :], W["attn_W_w2"][:])
                    aug_tok[0] = dq(g, "gpsimd", "qg", bt("ww2aug")[64:65, :],
                                    W["attn_b_w2"][None, :])
                GE(_)

                def _(te):
                    wait(te, "tensor", t_xi, ww1_tok)
                    pe_banks(te, [2])
                    for c in range(8):
                        ins = te.matmul(pb(2, 0, 512)[0:64, :],
                                        bt("wsl")[:, 1, c, 0:64],
                                        bt("xi")[:, c, :], start=(c == 0),
                                        stop=(c == 7))
                    banks_written(inc(ins, "pe"), [2])
                TE(_)

                def _(s):
                    read_banks(s, "scalar", [2])
                    ins = s.activation(bt("lora")[0:64, :], pb(2, 0, 512)[0:64, :],
                                       AF.Tanh)
                    tk = inc(ins, "act")
                    banks_read(tk, [2])
                SE(_)
                t_lora = ("act", count["act"])

                def _(v):
                    wait(v, "vector", bstate.get("arena_free"))
                    ins = v.memset(bt("lora")[64:65, :], 1.0)
                    inc(ins, "dve")
                VE(_)
                t_ones = ("dve", count["dve"])

                for ii in range(4):
                    def _(te, ii=ii):
                        wait(te, "tensor", t_lora, t_ones, aug_tok[0])
                        pe_banks(te, [3])
                        ins = te.matmul(pb(3, 0, 512),
                                        bt("lora")[0:65, 128 * ii:128 * (ii + 1)],
                                        bt("ww2aug")[0:65, :], start=True, stop=True)
                        banks_written(inc(ins, "pe"), [3])
                    TE(_)

                    def _(s, ii=ii):
                        read_banks(s, "scalar", [3])
                        ins = s.activation(w_tm(ii), pb(3, 0, 512), AF.Exp)
                        tk = inc(ins, "act")
                        banks_read(tk, [3])
                        proj_toks.setdefault("w", []).append(tk)
                    SE(_)
                xi_free[0] = ("pe", count["pe"])
                bstate["ww2_free"] = ("pe", count["pe"])
                wsl_free[1] = ("pe", count["pe"])
                if seq == 0 and qi == 0:
                    dump("w_e", sb["s"][:, 4:6, :], proj_toks["w"])
                    dump("lora_d", bt("lora")[0:64, :], proj_toks["w"])
                    dump("xi_w", bt("xi")[:], proj_toks["w"])
            else:
                slot = (1 + bi) % 2
                wtok = load_rwkv_w(slot, wdram, dout, wsl_free[slot])
                nb = dout // 512
                for ii in range(4):
                    bks = [2, 3] if nb == 1 else [2, 3]

                    def _(te, ii=ii, bi=bi, slot=slot, dout=dout, nb=nb):
                        wait(te, "tensor", t_xi, wtok)
                        pe_banks(te, [2, 3][:nb] if ii % 2 == 0 else [4, 5][:nb])
                        bset = [2, 3][:nb] if ii % 2 == 0 else [4, 5][:nb]
                        for h2 in range(nb):
                            for c in range(8):
                                ins = te.matmul(
                                    pb(bset[h2], 0, 512),
                                    bt("xi")[:, c, 128 * ii:128 * (ii + 1)],
                                    bt("wsl")[:, slot, c, 512 * h2:512 * (h2 + 1)],
                                    start=(c == 0), stop=(c == 7))
                        banks_written(inc(ins, "pe"), bset)
                    TE(_)
                    mm_tok = ("pe", count["pe"])

                    def _(v, ii=ii, bi=bi, obuf=obuf, nb=nb):
                        bset = [2, 3][:nb] if ii % 2 == 0 else [4, 5][:nb]
                        read_banks(v, "vector", bset)
                        for h2 in range(nb):
                            ins = v.tensor_copy(
                                bt(obuf)[:, ii, 512 * h2:512 * (h2 + 1)],
                                pb(bset[h2], 0, 512))
                        tk = inc(ins, "dve")
                        banks_read(tk, bset)
                        proj_toks.setdefault(obuf, []).append(tk)
                    VE(_)
                xi_free[0] = ("pe", count["pe"])
                wsl_free[slot] = ("pe", count["pe"])
        bstate["wsl_free"] = [("pe", count["pe"]), ("pe", count["pe"])]

        if bstop == "proj":
            bstate["arena_free"] = [("dve", count["dve"]), ("pe", count["pe"]),
                                    ("act", count["act"])]
            bstate["wsl_free"] = [("pe", count["pe"]), ("pe", count["pe"])]
            return []
        if seq == 0 and qi == 0 and dumps:
            dump("low_d", bt("low")[:, 0, :], ("act", count["act"]))
            dump("low_d2", bt("low")[0:32, 1, :], ("act", count["act"]))
            for nmx in ["r_tm", "k_tm2", "v_tm2", "g_tm", "khat"]:
                dump(nmx + "_d", bt(nmx)[:],
                     [("dve", count["dve"]), ("act", count["act"])])

        o_toks = []
        s_upd_tok = [bstate.get("S0_tok")]
        for gi in range(bng):
            def _(te, gi=gi):
                wait(te, "tensor", proj_toks["w"],
                     bstate.get("fact_free"), state.get("rwkv_lc_tok"))
                pe_banks(te, [0, 1, 2, 3, 4])
                te.matmul(pb(0), bt("Lc")[:, 0, :], w_tm_g(gi), start=True, stop=True)
                te.matmul(pb(1), bt("Lc")[:, 1, :], w_tm_g(gi), start=True, stop=True)
                te.matmul(pb(2), bt("Lc")[:, 2, :], w_tm_g(gi), start=True, stop=True)
                te.matmul(pb(4), bt("Lc")[:, 4, :], w_tm_g(gi), start=True, stop=True)
                for g4 in range(4):
                    ins = te.matmul(pb(3, 4 * g4, 4),
                                    w_tm_g(gi)[:, 128 * g4:128 * (g4 + 1)],
                                    bt("Lc")[:, 3, 0:128:32], start=True, stop=True)
                banks_written(inc(ins, "pe"), [0, 1, 2, 3, 4])
            TE(_)

            def _(s, gi=gi):
                read_banks(s, "scalar", [0, 1, 2, 3, 4])
                s.activation(bt("Fr")[:], pb(0), AF.Exp)
                s.activation(bt("Fk")[:], pb(1), AF.Exp, scale=-1.0)
                s.activation(bt("Fd")[:], pb(2), AF.Exp, scale=-1.0)
                s.activation(bt("Fr0")[:], pb(4), AF.Exp)
                ins = s.activation(bt("dcols")[:].rearrange("p a b -> p (a b)"),
                                   pb(3, 0, 16), AF.Exp, scale=-1.0)
                tk = inc(ins, "act")
                banks_read(tk, [0, 1, 2, 3, 4])
            SE(_)
            t_facts = ("act", count["act"])

            def _(v, gi=gi):
                wait(v, "vector", t_facts, proj_toks["r_tm"], proj_toks["k_tm2"])
                v.tensor_tensor(bt("r_tm")[:, gi, :].bitcast(BF16) if False else
                                sb["scr"][:].bitcast(BF16)[:, 0:512],
                                bt("r_tm")[:, gi, :], bt("Fr")[:], ALU.mult)
                v.tensor_tensor(sb["scr2"][:].bitcast(BF16)[:, 0:512],
                                bt("k_tm2")[:, gi, :], bt("Fk")[:], ALU.mult)
                v.tensor_tensor(bt("khat")[:, gi, :], bt("k_tm2")[:, gi, :],
                                bt("Fd")[:], ALU.mult)
                ins = v.tensor_tensor(sb["gg"][:], bt("r_tm")[:, gi, :],
                                      bt("Fr0")[:], ALU.mult)
                inc(ins, "dve")
            VE(_)
            t_fm0 = ("dve", count["dve"])

            def _(te, gi=gi):
                wait(te, "tensor", t_fm0, bstate.get("bfm_free"))
                pe_banks(te, [4, 5])
                for g4 in range(4):
                    transpose(te, sb["scr"][:].bitcast(BF16)[:, 0:512]
                              [:, 128 * g4:128 * (g4 + 1)],
                              ps[:, 512 * 4 + 128 * g4:512 * 4 + 128 * (g4 + 1)])
                for g4 in range(4):
                    ins = transpose(te, sb["scr2"][:].bitcast(BF16)[:, 0:512]
                                    [:, 128 * g4:128 * (g4 + 1)],
                                    ps[:, 512 * 5 + 128 * g4:512 * 5 + 128 * (g4 + 1)])
                banks_written(inc(ins, "pe"), [4, 5])
            TE(_)

            def _(v, gi=gi):
                read_banks(v, "vector", [4, 5])
                v.tensor_copy(bt("rt_fm")[:].rearrange("p a b -> p (a b)"), pb(4))
                ins = v.tensor_copy(bt("kt_fm")[:].rearrange("p a b -> p (a b)"),
                                    pb(5))
                tk = inc(ins, "dve")
                banks_read(tk, [4, 5])
            VE(_)
            t_fm1 = ("dve", count["dve"])

            def _(te, gi=gi):
                wait(te, "tensor", t_fm1)
                pe_banks(te, [3, 4, 5])
                for g4 in range(4):
                    transpose(te, bt("r_tm")[:, gi, 128 * g4:128 * (g4 + 1)],
                              ps[:, 512 * 4 + 128 * g4:512 * 4 + 128 * (g4 + 1)])
                for g4 in range(4):
                    transpose(te, bt("k_tm2")[:, gi, 128 * g4:128 * (g4 + 1)],
                              ps[:, 512 * 5 + 128 * g4:512 * 5 + 128 * (g4 + 1)])
                for g4 in range(4):
                    ins = transpose(te, sb["gg"][:, 128 * g4:128 * (g4 + 1)],
                                    ps[:, 512 * 3 + 128 * g4:512 * 3 + 128 * (g4 + 1)])
                banks_written(inc(ins, "pe"), [3, 4, 5])
            TE(_)

            def _(v, gi=gi):
                read_banks(v, "vector", [3, 4, 5])
                v.tensor_copy(bt("rr_fm")[:].rearrange("p a b -> p (a b)"), pb(4))
                v.tensor_copy(bt("ku_fm")[:].rearrange("p a b -> p (a b)"), pb(5))
                v.tensor_copy(bt("r0t_fm")[:].rearrange("p a b -> p (a b)"), pb(3))
                ins = v.tensor_tensor(bt("ku_fm")[:], bt("ku_fm")[:],
                                      bc(48, 4)[:, :, None]
                                      .broadcast_to([128, 4, 128]), ALU.mult)
                tk = inc(ins, "dve")
                banks_read(tk, [3, 4, 5])
            VE(_)
            t_fm2 = ("dve", count["dve"])
            bstate["bfm_free"] = None

            def _(v, gi=gi):
                wait(v, "vector", proj_toks["r_tm"], proj_toks["k_tm2"],
                     prep_tok, bstate.get("urep_tok"))
                v.tensor_tensor(sb["scr"][:], bt("r_tm")[:, gi, :],
                                bt("k_tm2")[:, gi, :], ALU.mult)
                v.tensor_tensor(sb["scr"][:], sb["scr"][:], bt("urep")[:],
                                ALU.mult)
                v.tensor_reduce(bt("dvt")[:],
                                sb["scr"][:].rearrange("p (h d) -> p h d",
                                                       h=16),
                                AXL.X, ALU.add)
                v.tensor_copy(sb["scr2"][:, 0:1], sb["scr"][:, 0:1])
                ins = v.tensor_tensor(bt("dvv")[:].rearrange("p (h d) -> p h d",
                                                             h=16),
                                      bt("v_tm2")[:, gi, :]
                                      .rearrange("p (h d) -> p h d", h=16),
                                      bt("dvt")[:, :, None]
                                      .broadcast_to([128, 16, 64]), ALU.mult)
                inc(ins, "dve")
            VE(_)
            t_dv = ("dve", count["dve"])
            if seq == 0 and qi == 0 and gi == 0 and dumps:
                dump("dvt_d", bt("dvt")[:], t_dv)
                dump("urep_d", bt("urep")[:], t_dv)
            if bstop == "fm":
                continue

            if seq == 0 and qi == 0 and gi == 1 and dumps:
                dump("otile0_d", sb["s"][:, 6, :], o_toks)
            if seq == 0 and qi == 0 and gi == 2 and dumps:
                dump("otile1_d", sb["s"][:, 7, :], o_toks)
            if seq == 0 and qi == 0 and gi == 0 and dumps:
                dump("Fr_d", bt("Fr")[:], t_facts)
                dump("Fk_d", bt("Fk")[:], t_facts)
                dump("dcols_d", bt("dcols")[:], t_facts)
                dump("rtfm_d", bt("rt_fm")[:], t_fm2)
                dump("ktfm_d", bt("kt_fm")[:], t_fm2)

            for cl in range(bnc):
                ch = 4 * gi + cl
                cb = 32 * cl

                def _(te, gi=gi, cl=cl, cb=cb):
                    wait(te, "tensor", t_fm1, t_fm2)
                    pe_banks(te, [6])
                    for h in range(16):
                        hb = 32 * (h % 4)
                        hg = h // 4
                        ins = te.matmul(ps[hb:hb + 32,
                                           512 * 6 + 32 * h:512 * 6 + 32 * h + 32],
                                        bt("kt_fm")[hb:hb + 32, hg, cb:cb + 32],
                                        bt("rt_fm")[hb:hb + 32, hg, cb:cb + 32],
                                        start=True, stop=True,
                                        tile_position=(hb, hb))
                    banks_written(inc(ins, "pe"), [6])
                TE(_)
                if bstop == "amm":
                    continue

                def _(v, cl=cl, cb=cb):
                    read_banks(v, "vector", [6])
                    wait(v, "vector", s_upd_tok[0], state.get("rwkv_lc_tok"))
                    tri = bt("masks")[0:32, 0, 0:128].rearrange(
                        "p (j c) -> p j c", j=4)
                    for q in range(4):
                        ins = v.scalar_tensor_tensor(
                            out=bt("B1")[cb:cb + 32, :]
                            .rearrange("p (j c) -> p j c", j=4)
                            [:, :, 32 * q:32 * q + 32],
                            in0=pb(6)[32 * q:32 * q + 32, :]
                            .rearrange("p (j c) -> p j c", j=4)
                            [:, :, 32 * q:32 * q + 32],
                            scalar=1.0, in1=tri, op0=ALU.mult, op1=ALU.mult)
                    tk = inc(ins, "dve")
                    banks_read(tk, [6])
                VE(_)
                t_B = ("dve", count["dve"])
                if bstop == "bevac":
                    continue

                def _(te, gi=gi, cl=cl, cb=cb):
                    wait(te, "tensor", t_B, proj_toks["v_tm2"])
                    pe_banks(te, [0, 1, 4, 5])
                    for h in range(16):
                        hb = 32 * (h % 4)
                        hg = h // 4
                        ot = pb(0, 0, 512) if h < 8 else pb(1, 0, 512)
                        oc = 64 * (h % 8)
                        ins = te.matmul(ot[:, oc:oc + 64][cb:cb + 32, :],
                                  bt("B1")[cb:cb + 32, 32 * h:32 * h + 32],
                                  bt("v_tm2")[cb:cb + 32, gi, 64 * h:64 * h + 64],
                                  start=True, stop=True, tile_position=(cb, cb))
                    if bstop != "omm":
                        for h in range(16):
                            ins = te.matmul(
                                ps[cb:cb + 32,
                                   512 * 4 + 64 * h:512 * 4 + 64 * h + 64],
                                bt("khat")[cb:cb + 32, gi, 32 * h:32 * h + 32],
                                bt("v_tm2")[cb:cb + 32, gi, 64 * h:64 * h + 64],
                                start=True, stop=True, tile_position=(cb, cb))
                    banks_written(inc(ins, "pe"), [0, 1, 4, 5])
                TE(_)

                def _(te, gi=gi, cl=cl, cb=cb):
                    wait(te, "tensor", s_upd_tok[0])
                    pe_banks(te, [3])
                    for h in range(16):
                        hb = 32 * (h % 4)
                        hg = h // 4
                        ins = te.matmul(pb(3)[hb:hb + 32, 64 * hg:64 * hg + 64],
                                        bt("r0t_fm")[hb:hb + 32, hg, cb:cb + 32],
                                        bt("S0b")[hb:hb + 32, hg, :],
                                        start=True, stop=True,
                                        tile_position=(hb, hb))
                    banks_written(inc(ins, "pe"), [3])
                TE(_)
                t_osmm = ("pe", count["pe"])
                if bstop == "omm":
                    continue

                def _(v, gi=gi, cl=cl, cb=cb, t_dv=t_dv):
                    read_banks(v, "vector", [0, 1, 3])
                    wait(v, "vector", t_dv)
                    sc2 = sb["scr2"][:].bitcast(BF16)
                    for q in range(4):
                        v.tensor_copy(
                            sc2[cb:cb + 32, :].rearrange("p (j c) -> p j c", j=4)
                            [:, :, 64 * q:64 * q + 64],
                            pb(3)[32 * q:32 * q + 32, 0:256]
                            .rearrange("p (g c) -> p g c", g=4))
                    ot = o_tm_t(gi)
                    for hf in range(2):
                        v.tensor_tensor(ot[cb:cb + 32, 512 * hf:512 * (hf + 1)],
                                        bt("dvv")[cb:cb + 32,
                                                  512 * hf:512 * (hf + 1)],
                                        sc2[cb:cb + 32, 512 * hf:512 * (hf + 1)],
                                        ALU.add)
                    v.tensor_tensor(ot[cb:cb + 32, 0:512],
                                    pb(0, 0, 512)[cb:cb + 32, :],
                                    ot[cb:cb + 32, 0:512], ALU.add)
                    ins = v.tensor_tensor(ot[cb:cb + 32, 512:1024],
                                          pb(1, 0, 512)[cb:cb + 32, :],
                                          ot[cb:cb + 32, 512:1024], ALU.add)
                    tk = inc(ins, "dve")
                    banks_read(tk, [0, 1, 3])
                    o_toks.append(tk)
                VE(_)

                def _(v, gi=gi, cl=cl, ch=ch, cb=cb):
                    read_banks(v, "vector", [4, 5])
                    wait(v, "vector", s_upd_tok[0], t_facts)
                    sdl = ps[:, 512 * 4:512 * 6]
                    for q in range(4):
                        v.tensor_copy(pb(2, 0, 256)[32 * q:32 * q + 32, :]
                                      .rearrange("p (g c) -> p g c", g=4),
                                      sdl[cb:cb + 32, :]
                                      .rearrange("p (j c) -> p j c", j=4)
                                      [:, :, 64 * q:64 * q + 64])
                    v.tensor_tensor(bt("Stmp")[:], bt("S0")[:],
                                    bt("dcols")[:, :, cl:cl + 1]
                                    .broadcast_to([128, 4, 64]), ALU.mult)
                    v.tensor_tensor(bt("S0")[:].rearrange("p a b -> p (a b)"),
                                    bt("Stmp")[:].rearrange("p a b -> p (a b)"),
                                    pb(2, 0, 256), ALU.add)
                    ins = v.tensor_copy(bt("S0b")[:].rearrange("p a b -> p (a b)"),
                                        bt("S0")[:].rearrange("p a b -> p (a b)"))
                    tk = inc(ins, "dve")
                    banks_read(tk, [2, 4, 5])
                    s_upd_tok[0] = tk
                VE(_)
                if seq == 0 and qi == 0 and gi == 0 and cl == 0 and dumps:
                    dump("B1_d", bt("B1")[0:32, :], t_B)
                    dump("S0_d", bt("S0")[:], s_upd_tok[0])
                    dump("o0_d", sb["s"][0:32, 6, :], o_toks)
        bstate["S0_tok"] = s_upd_tok[0]
        bstate["fact_free"] = ("pe", count["pe"])
        if bstop in ("chunks", "fm", "amm", "bevac", "omm"):
            bstate["arena_free"] = [("dve", count["dve"]), ("pe", count["pe"]),
                                    ("act", count["act"])]
            bstate["wsl_free"] = [("pe", count["pe"]), ("pe", count["pe"])]
            return []

        wog_tok = load_rwkv_w(0, W["attn_W_o"], 1024, bstate["wsl_free"][0],
                              scale_col=52)
        wob_tok = load_rwkv_w(1, W["attn_W_o"], 1024, bstate["wsl_free"][1],
                              scale_col=60)
        st_toks = []
        for ii in range(4):
            def _(v, ii=ii):
                wait(v, "vector", o_toks)
                ot = o_tm_t(ii)
                v.tensor_reduce(bt("gnstat")[:, 0:16],
                                ot.rearrange("p (h d) -> p h d", h=16),
                                AXL.X, ALU.add)
                v.tensor_tensor(gn_scr(ii), ot, ot, ALU.mult)
                ins = v.tensor_reduce(bt("gnstat")[:, 16:32],
                                      gn_scr(ii).rearrange("p (h d) -> p h d", h=16),
                                      AXL.X, ALU.add)
                inc(ins, "dve")
            VE(_)
            t_gs = ("dve", count["dve"])

            def _(v, ii=ii):
                wait(v, "vector", t_gs)
                g = bt("gnstat")
                v.tensor_scalar(g[:, 0:16], g[:, 0:16], 1.0 / 64,
                                None, ALU.mult)
                v.tensor_scalar(g[:, 16:32], g[:, 16:32], 1.0 / 64,
                                EPS, ALU.mult, ALU.add)
                v.scalar_tensor_tensor(out=g[:, 32:48],
                                       in0=g[:, 0:16], scalar=-1.0,
                                       in1=g[:, 0:16],
                                       op0=ALU.mult, op1=ALU.mult)
                v.tensor_copy(sb["scr2"][:, 0:1], g[:, 0:1])
                ins = v.tensor_tensor(g[:, 16:32], g[:, 16:32],
                                      g[:, 32:48], ALU.add)
                inc(ins, "dve")
            VE(_)
            t_gs2 = ("dve", count["dve"])

            def _(s, ii=ii):
                wait(s, "scalar", t_gs2)
                ins = s.activation(bt("gnstat")[:, 16:32], bt("gnstat")[:, 16:32],
                                   AF.Sqrt)
                inc(ins, "act")
            SE(_)
            t_gsq = ("act", count["act"])

            def _(v, ii=ii):
                wait(v, "vector", t_gsq)
                g = bt("gnstat")
                v.reciprocal(g[:, 16:32], g[:, 16:32])
                v.tensor_copy(sb["scr2"][:, 0:1], g[:, 0:1])
                ot = o_tm_t(ii)
                v.tensor_tensor(ot.rearrange("p (h d) -> p h d", h=16),
                                ot.rearrange("p (h d) -> p h d", h=16),
                                g[:, 0:16][:, :, None]
                                .broadcast_to([128, 16, 64]), ALU.subtract)
                ins = v.tensor_tensor(ot.rearrange("p (h d) -> p h d", h=16),
                                      ot.rearrange("p (h d) -> p h d", h=16),
                                      g[:, 16:32][:, :, None]
                                      .broadcast_to([128, 16, 64]), ALU.mult)
                inc(ins, "dve")
            VE(_)
            t_on = ("dve", count["dve"])

            def _(s, ii=ii):
                wait(s, "scalar", t_on, proj_toks["g_tm"])
                ins = s.activation(bt("g_tm")[:, ii, :], bt("g_tm")[:, ii, :],
                                   AF.Silu)
                inc(ins, "act")
            SE(_)
            t_sg = ("act", count["act"])

            def _(v, ii=ii):
                wait(v, "vector", t_sg)
                ins = v.tensor_tensor(gn_scr(ii).bitcast(BF16)[:, 0:1024],
                                      o_tm_t(ii), bt("g_tm")[:, ii, :], ALU.mult)
                inc(ins, "dve")
            VE(_)
            t_t1 = ("dve", count["dve"])

            def _(te, ii=ii):
                wait(te, "tensor", t_t1)
                pe_banks(te, [4, 5])
                for c in range(8):
                    pslot = 4 if c < 4 else 5
                    transpose(te, gn_scr(ii).bitcast(BF16)[:, 0:1024]
                              [:, 128 * c:128 * (c + 1)],
                              ps[:, 512 * pslot + 128 * (c % 4):
                                 512 * pslot + 128 * (c % 4 + 1)])
                banks_written(inc(te.nop(), "pe"), [4, 5])
            TE(_)

            def _(s, ii=ii):
                read_banks(s, "scalar", [4, 5])
                ins = s.activation(bt("t1sg_fm")[:, 0, :, :]
                                   .rearrange("p c n -> p (c n)"),
                                   ps[:, 512 * 4:512 * 4 + 1024], AF.Copy)
                tk = inc(ins, "act")
                banks_read(tk, [4, 5])
            SE(_)
            t_t1f = ("act", count["act"])

            def _(te, ii=ii):
                wait(te, "tensor", t_t1f)
                pe_banks(te, [4, 5])
                for c in range(8):
                    pslot = 4 if c < 4 else 5
                    transpose(te, bt("g_tm")[:, ii, 128 * c:128 * (c + 1)],
                              ps[:, 512 * pslot + 128 * (c % 4):
                                 512 * pslot + 128 * (c % 4 + 1)])
                banks_written(inc(te.nop(), "pe"), [4, 5])
            TE(_)

            def _(s, ii=ii):
                read_banks(s, "scalar", [4, 5])
                ins = s.activation(bt("t1sg_fm")[:, 1, :, :]
                                   .rearrange("p c n -> p (c n)"),
                                   ps[:, 512 * 4:512 * 4 + 1024], AF.Copy)
                tk = inc(ins, "act")
                banks_read(tk, [4, 5])
            SE(_)
            t_sgf = ("act", count["act"])

            def _(te, ii=ii):
                wait(te, "tensor", t_t1f, t_sgf, wog_tok, wob_tok)
                pe_banks(te, [6, 7])
                for h2 in range(2):
                    for c in range(8):
                        te.matmul(pb(6 + h2), bt("t1sg_fm")[:, 0, c, :],
                                  bt("wsl")[:, 0, c, 512 * h2:512 * (h2 + 1)],
                                  start=(c == 0), stop=False)
                    for c in range(8):
                        ins = te.matmul(pb(6 + h2), bt("t1sg_fm")[:, 1, c, :],
                                        bt("wsl")[:, 1, c, 512 * h2:512 * (h2 + 1)],
                                        start=False, stop=(c == 7))
                banks_written(inc(ins, "pe"), [6, 7])
            TE(_)

            t = tiles[ii]
            ldt = [None]

            def _(sy, ii=ii, t=t):
                wait(sy, "sync", xp_last[ii % 2])
                ldt[0] = dq(sy, "sync", "qs", sb["xp"][:, ii % 2, :],
                            xsrc.rearrange("(n p) d -> n p d", p=128)[t])
            SY(_)

            def _(v, ii=ii, t=t):
                read_banks(v, "vector", [6, 7])
                wait(v, "vector", ldt[0])
                v.tensor_tensor(sb["xp"][:, ii % 2, 0:512], sb["xp"][:, ii % 2, 0:512],
                                pb(6), ALU.add)
                ins = v.tensor_tensor(sb["xp"][:, ii % 2, 512:1024],
                                      sb["xp"][:, ii % 2, 512:1024], pb(7), ALU.add)
                tk = inc(ins, "dve")
                banks_read(tk, [6, 7])
            VE(_)
            t_x2 = ("dve", count["dve"])

            def _(sy, ii=ii, t=t):
                wait(sy, "sync", t_x2)
                stok = dq(sy, "sync", "qs",
                          xdst.rearrange("(n p) d -> n p d", p=128)[t],
                          sb["xp"][:, ii % 2, :])
                st_toks.append(stok)
                xp_last[ii % 2] = stok
            SY(_)

        bstate["arena_free"] = [("dve", count["dve"]), ("pe", count["pe"]),
                                ("act", count["act"])]
        bstate["wsl_free"] = [("pe", count["pe"]), ("pe", count["pe"])]
        return st_toks

    def w_tm(ii):
        return sb["s"][:, 4 + ii // 2, 512 * (ii % 2):512 * (ii % 2) + 512]

    def w_tm_g(gi):
        return w_tm(gi)

    def o_tm_t(ii):
        return sb["s"][:, 6 + ii, :]

    def gn_scr(ii):
        return sb["s"][:, 10 + (ii % 2), :]

    PH = [p for p in ("A", "B", "C") if p in set(phases)]
    s_free = None
    prev_arena_free = None
    for seq in range(NSEQ):
        xsrc = xin[seq]
        for pi, ph in enumerate(PH):
            last = (pi == len(PH) - 1)
            xdst = xout[seq] if last else xscr
            if ph in ("A", "C"):
                s_toks = ln_from_dram(xsrc, s_free, dump_tag=f"s{ph}{seq}")
                attn_state = {"kv_free": prev_arena_free}
                arena_free = prev_arena_free
                for layer in range(NLAYERS_FFN):
                    wt = load_ffn_attn_weights(layer, arena_free)
                    for w in range(4):
                        s_toks2 = ffn_attn_window(seq, layer, w, wt, s_toks,
                                                  attn_state)
                        for i in range(4):
                            s_toks[4 * w + i] = s_toks2[i]
                    if seq == 0 and layer == 0 and ph == "A":
                        dtk = dump(f"sAttn{seq}", sb["s"][:], s_toks)
                        if dtk is not None:
                            s_toks = [dtk] * NT
                    arena_free = list(s_toks)
                    s_toks = ffn_mlp(seq, layer, wt, s_toks, arena_free)
                    arena_free = list(s_toks)
                    if seq == 0 and ph == "A":
                        dtk = dump(f"sL{layer}", sb["s"][:], s_toks)
                        if dtk is not None:
                            s_toks = [dtk] * NT
                st = add_half_s_writeback(xsrc, xdst, s_toks)
                if last:
                    final_toks.extend(st)
                s_free = list(add_half_s_writeback.s_readers)
                add_half_s_writeback.s_readers = []
                prev_arena_free = (list(s_toks) if isinstance(s_toks, list)
                                   else [s_toks])
                xsrc = xdst
            else:
                if state.get("rwkv_prep_tok") is None:
                    state["rwkv_prep_tok"] = rwkv_prep()
                bstate = state.setdefault("bstate", {})
                bstate["arena_free"] = prev_arena_free
                paf = (list(prev_arena_free) if isinstance(prev_arena_free, list)
                       else ([prev_arena_free] if prev_arena_free else []))
                bstate["wsl_free"] = [list(paf), list(paf)]
                bstate["ww2_free"] = list(paf)
                bstate["S0_tok"] = None
                lctok = [None]

                def _(sy):
                    wait(sy, "sync", prev_arena_free)
                    dq(sy, "sync", "qs", bt("Lc")[:, 0, :], C["c_Lex"][:])
                    dq(sy, "sync", "qs", bt("Lc")[:, 1, :], C["c_Lin"][:])
                    dq(sy, "sync", "qs", bt("Lc")[:, 2, :], C["c_Lend"][:])
                    dq(sy, "sync", "qs", bt("Lc")[:, 3, :], C["c_blko"][:])
                    dq(sy, "sync", "qs", bt("Lc")[:, 4, :], C["c_Lex0"][:])
                    dq(sy, "sync", "qs", bt("masks")[0:32, 0, :], C["c_triu01"][:])
                    lctok[0] = dq(sy, "sync", "qs", bt("masks")[0:32, 1, :],
                                  C["c_diag01"][:])
                SY(_)
                state["rwkv_lc_tok"] = lctok[0]

                def _(v):
                    wait(v, "vector", prev_arena_free)
                    v.memset(bt("S0")[:].rearrange("p a b -> p (a b)"), 0.0)
                    ins = v.memset(bt("S0b")[:].rearrange("p a b -> p (a b)"), 0.0)
                    inc(ins, "dve")
                VE(_)
                bstate["S0_tok"] = ("dve", count["dve"])

                urtok = [None]

                def _(g):
                    wait(g, "gpsimd", prev_arena_free, state["rwkv_prep_tok"])
                    urtok[0] = dq(g, "gpsimd", "qg", sb["lnrow"][0:1, 0:512],
                                  W["attn_u"][:]
                                  .rearrange("a b -> (a b)")[None, :])
                GE(_)

                def _(te):
                    wait(te, "tensor", urtok[0], prev_arena_free)
                    pe_banks(te, [6])
                    ins = te.matmul(pb(6, 0, 512), sb["onesr"][0:1, 0:128],
                                    sb["lnrow"][0:1, 0:512],
                                    start=True, stop=True)
                    banks_written(inc(ins, "pe"), [6])
                TE(_)

                def _(v):
                    read_banks(v, "vector", [6])
                    ins = v.tensor_copy(bt("urep")[:], pb(6, 0, 512))
                    tk = inc(ins, "dve")
                    banks_read(tk, [6])
                    bstate["urep_tok"] = tk
                VE(_)
                allst = []
                for qi in range(bq):
                    stq_ = rwkv_quarter(seq, qi, xsrc, xdst, bstate,
                                        state["rwkv_prep_tok"])
                    allst.extend(stq_)
                if last:
                    final_toks.extend(allst)
                s_free = None
                prev_arena_free = bstate["arena_free"]
                xsrc = xdst

    def _(sy):
        wait(sy, "sync", final_toks)
    SY(_)

    for g in reversed(guards):
        g.__exit__(None, None, None)
    block.__exit__ if False else None

    return nc


def kernel(**inputs):
    nc = build_program()
    consts = CONSTS
    n_cores = 8
    in_maps = []
    for c in range(n_cores):
        m = {"xin": np.ascontiguousarray(inputs["L0_attn_x"][2 * c:2 * c + 2])}
        for k, v in inputs.items():
            if k == "L0_attn_x":
                continue
            name = k.replace("L0_", "")
            m[name] = np.ascontiguousarray(v)
        for k, v in consts.items():
            m[k] = v
        in_maps.append(m)
    res = run_bass_kernel_spmd(nc, in_maps, core_ids=list(range(n_cores)))
    out = np.concatenate([r["xout"] for r in res.results], axis=0)
    return out.astype(np.float32)


# version 11
# speedup vs baseline: 1.0130x; 1.0064x over previous
import numpy as np
import ml_dtypes
import concourse.bass as bass
import concourse.mybir as mybir
from concourse.bass_utils import run_bass_kernel_spmd

F32 = mybir.dt.float32
BF16 = mybir.dt.bfloat16
AF = mybir.ActivationFunctionType
ALU = mybir.AluOpType
AXL = mybir.AxisListType
BF = ml_dtypes.bfloat16

D = 1024
T = 2048
NSEQ = 2
NT = 16
WIN = 512
HID = 2730
EPS = 1e-5
NEG_BIG = -1.0e9
CH = 32
NH_RW = 16


def build_constants():
    c = {}
    c["c_I128"] = np.eye(128, dtype=BF)
    c["c_I128f"] = np.eye(128, dtype=np.float32)
    c["c_onesr"] = np.ones((128, 128), dtype=np.float32)
    i = np.arange(128)
    c["c_tri_q"] = np.where(i[None, :] > i[:, None], NEG_BIG, 0.0).astype(BF)
    c["c_tri_p"] = np.where(i[None, :] < i[:, None], NEG_BIG, 0.0).astype(BF)
    dh = 64
    half_idx = np.arange(0, dh, 2, dtype=np.float64)
    inv_freq = 1.0 / (10000.0 ** (half_idx / dh))
    t = np.arange(T, dtype=np.float64)
    freqs = np.concatenate([t[:, None] * inv_freq[None, :]] * 2, axis=-1)
    base = (half_idx + 0.4 * dh) / (1.4 * dh)
    power = (t - T // 2) / float(WIN // 2)
    scale = np.concatenate([base[None, :] ** power[:, None]] * 2, axis=-1)
    sign = np.concatenate([-np.ones(dh // 2), np.ones(dh // 2)])[None, :]
    QK_SCALE = 8.0
    c["c_rot"] = np.stack([
        QK_SCALE * np.cos(freqs) * scale, QK_SCALE * sign * np.sin(freqs) * scale,
        np.cos(freqs) / scale, sign * np.sin(freqs) / scale,
    ], axis=1).astype(np.float32)
    j = np.arange(CH)
    Lin = (j[None, :] <= j[:, None]).astype(np.float64)
    Lex = (j[None, :] < j[:, None]).astype(np.float64)
    ref = (j[None, :] <= 15).astype(np.float64)

    def blockdiag(blk):
        m = np.zeros((128, 128))
        for q in range(4):
            m[q * CH:(q + 1) * CH, q * CH:(q + 1) * CH] = blk.T
        return m.astype(np.float32)

    c["c_Lin"] = blockdiag(-(Lin - ref))
    c["c_Lex"] = blockdiag(-(Lex - ref))
    c["c_Lend"] = blockdiag(-(Lin - 1.0))
    c["c_Lex0"] = blockdiag(-Lex)
    m = np.zeros((128, 128))
    for q in range(4):
        m[q * CH:(q + 1) * CH, q * CH:(q + 1) * CH] = 1.0
    c["c_blko"] = m.astype(np.float32)
    jj = np.arange(CH)
    c["c_triu01"] = np.tile((jj[:, None] < jj[None, :]).astype(np.float32),
                            (1, NH_RW)).astype(BF)
    c["c_diag01"] = np.tile(np.eye(CH, dtype=np.float32), (1, NH_RW)).astype(BF)
    c["c_scal"] = np.tile(np.array([0.0, 1.0, -1.0, 1e-5, 0.5, -0.5, 8.0, 1e-12],
                           dtype=np.float32)[None, :], (128, 1))
    return c


CONSTS = build_constants()


NLAYERS_FFN = 2


def build_program(dumps=(), phases=("A", "B", "C"), bq=4, bstop="full", bng=4, bnc=4):
    dumps = set(dumps)
    nc = bass.Bass("TRN2", target_bir_lowering=False)
    guards = []

    def keep(g):
        t = g.__enter__()
        guards.append(g)
        return t

    def sbuf(name, shape, dtype):
        return keep(nc.sbuf_tensor(name, list(shape), dtype))

    din, dout = {}, {}

    def dram_in(name, shape, dtype=F32):
        din[name] = nc.dram_tensor(name, list(shape), dtype, kind="ExternalInput").ap()
        return din[name]

    def dram_out(name, shape, dtype=F32):
        dout[name] = nc.dram_tensor(name, list(shape), dtype, kind="ExternalOutput").ap()
        return dout[name]

    xin = dram_in("xin", [NSEQ, T, D])
    xout = dram_out("xout", [NSEQ, T, D])
    xscr = nc.dram_tensor("xscr", [T, D], F32, kind="Internal").ap()

    W = {}
    for name, shape in [
        ("attn_mu_x", [D]), ("attn_W_x1", [D, 160]), ("attn_W_x2", [D, 5, 32]),
        ("attn_x_bias", [5, D]), ("attn_W_r", [D, 512]), ("attn_W_w1", [D, 64]),
        ("attn_W_w2", [64, 512]), ("attn_b_w2", [512]), ("attn_W_k", [D, 512]),
        ("attn_W_v", [D, D]), ("attn_W_g", [D, D]), ("attn_u", [16, 32]),
        ("attn_gn_w", [D]), ("attn_gn_b", [D]), ("attn_W_o", [D, D]),
        ("ffn_wqkv", [2, D, 1536]), ("ffn_q_scale", [2, 64]), ("ffn_k_scale", [2, 64]),
        ("ffn_wo", [2, 512, D]), ("ffn_ln_g", [2, D]), ("ffn_ln_b", [2, D]),
        ("ffn_w1", [2, D, 2 * HID]), ("ffn_b1", [2, 2 * HID]),
        ("ffn_w2", [2, HID, D]), ("ffn_b2", [2, D]),
    ]:
        W[name] = dram_in(name, shape)
    C = {}
    for name, arr in CONSTS.items():
        C[name] = dram_in(name, list(arr.shape), BF16 if arr.dtype == BF else F32)

    sb = {}
    sb["s"] = sbuf("s_res", [128, NT, D], F32)
    ARENA_HW = 57 * 1024
    arena = sbuf("arena", [128, ARENA_HW], BF16)

    def av(off_bytes, shape, dtype=BF16):
        es = 2 if dtype == BF16 else 4
        assert off_bytes % 64 == 0
        n = int(np.prod(shape))
        base = arena[:].bitcast(dtype) if dtype != BF16 else arena[:]
        flat = base[:, off_bytes // es: off_bytes // es + n]
        if len(shape) == 1:
            assert shape[0] == n
            return flat
        pat = " ".join(f"d{i}" for i in range(len(shape)))
        kw = {f"d{i}": shape[i] for i in range(len(shape))}
        return flat.rearrange(f"p ({pat}) -> p {pat}", **kw)

    AT = {}
    o = 0
    def _alloc(tag, shape, dtype=BF16):
        nonlocal o
        es = 2 if dtype == BF16 else 4
        n = int(np.prod(shape)) * es
        AT[tag] = (o, shape, dtype)
        o += (n + 63) // 64 * 64
    _alloc("h_fm", [8, 513]); _alloc("sbf", [4, D])
    _alloc("q_tm", [4, 512]); _alloc("k_tm", [4, 512])
    _alloc("qr_tm", [4, 512]); _alloc("kr_tm", [4, 512])
    _alloc("v_tm", [2, 4, 512]); _alloc("q_fm", [4, 512]); _alloc("k_fm", [2, 4, 512])
    _alloc("pexp", [4, 640]); _alloc("pt", [4, 640])
    _alloc("o_tm", [4, 512]); _alloc("o_fm", [4, 4, 128])
    _alloc("wqkv", [8, 1536]); _alloc("wo", [4, D])
    ATT_END = o
    assert ATT_END <= 2 * ARENA_HW, ATT_END

    MT = {}
    o = 0
    def _malloc(tag, shape, dtype=BF16):
        nonlocal o
        es = 2 if dtype == BF16 else 4
        n = int(np.prod(shape)) * es
        MT[tag] = (o, shape, dtype)
        o += (n + 63) // 64 * 64
    _malloc("y_fm", [8, 1025])
    _malloc("u_fm", [22, 1024])
    _malloc("w1t", [2, 8, 128])
    _malloc("w2res", [22, 1024])
    _malloc("p12", [D]); _malloc("b1blk", [128])
    MLP_END = o
    assert MLP_END <= 2 * ARENA_HW, MLP_END

    def at(tag):
        off, shape, dt = AT[tag]
        return av(off, shape, dt)

    BT = {}
    o = 0
    def _balloc(tag, shape, dtype=BF16):
        nonlocal o
        es = 2 if dtype == BF16 else 4
        n = int(np.prod(shape)) * es
        BT[tag] = (o, shape, dtype)
        o += (n + 63) // 64 * 64
    _balloc("xln_fm", [8, 513])
    _balloc("delta", [8, 512]); _balloc("xi", [8, 512])
    _balloc("low", [2, 512]); _balloc("lora", [512]); _balloc("ww2aug", [512])
    _balloc("r_tm", [4, 512]); _balloc("k_tm2", [4, 512]); _balloc("khat", [4, 512])
    _balloc("Fr", [512]); _balloc("Fk", [512]); _balloc("Fd", [512])
    _balloc("v_tm2", [4, D]); _balloc("g_tm", [4, D])
    _balloc("rt_fm", [4, 128]); _balloc("kt_fm", [4, 128])
    _balloc("r0t_fm", [4, 128]); _balloc("Fr0", [512])
    _balloc("rr_fm", [4, 128]); _balloc("ku_fm", [4, 128])
    _balloc("B1", [512]); _balloc("B2", [512])
    _balloc("dcols", [4, 4], F32); _balloc("S0", [4, 64], F32)
    _balloc("S0b", [4, 64])
    _balloc("Stmp", [4, 64], F32)
    _balloc("dvt", [16], F32)
    _balloc("urep", [512])
    _balloc("gnstat", [48], F32)
    _balloc("dvv", [1024])
    _balloc("t1sg_fm", [2, 8, 128])
    _balloc("wsl", [2, 8, D])
    _balloc("Lc", [5, 128], F32)
    _balloc("masks", [2, 512])
    RWK_END = o
    assert RWK_END <= 2 * ARENA_HW, RWK_END

    def bt(tag):
        if tag == "xln_bf":
            off, _shape, dt = BT["v_tm2"]
            return av(off, [4, D], BF16)
        off, shape, dt = BT[tag]
        return av(off, shape, dt)

    def mt(tag):
        off, shape, dt = MT[tag]
        if tag == "b1blk":
            return arena[0:44, off // 2: off // 2 + 128]
        return av(off, shape, dt)

    def srow_seg(which, j, a, b):
        p = 0 if which == "s" else 64
        return av(MT["u_fm"][0], [8320], F32)[p:p + 1, 2080 * j + a: 2080 * j + b]

    sb["I"] = sbuf("I128", [128, 128], BF16)
    sb["If"] = sbuf("I128f", [128, 128], F32)
    sb["onesr"] = sbuf("onesr", [128, 128], F32)
    sb["tri_pq"] = sbuf("tri_pq", [128, 2, 128], BF16)
    sb["rot"] = sbuf("rot", [128, 4, 4, 64], F32)
    sb["xp"] = sbuf("xpair", [128, 2, D], F32)
    sb["scr"] = sbuf("scr", [128, 512], F32)
    sb["scr2"] = sbuf("scr2", [128, 512], F32)
    sb["st4"] = sbuf("st4", [128, NT, 4], F32)
    sb["mcols"] = sbuf("mcols", [128, NT, 4], F32)
    sb["ssq"] = sbuf("ssq", [128, 4, 8], F32)
    sb["ssk"] = sbuf("ssk", [128, 4, 8], F32)
    sb["qsb"] = sbuf("qsb", [128, 2, 64], F32)
    sb["qsrow"] = sbuf("qsrow", [1, 2, 64], F32)
    sb["rz"] = sbuf("rz", [128, 2, 8], F32)
    sb["mx"] = sbuf("mx", [128, 4], F32)
    sb["halo"] = sbuf("halo", [128, 8, 1], BF16)
    sb["gg"] = sbuf("gg", [128, 512], BF16)
    sb["cpcol"] = sbuf("cpcol", [128, 2, 44], F32)
    sb["lbcol"] = sbuf("lbcol", [128, 8], BF16)
    sb["lgcol"] = sbuf("lgcol", [128, 8], F32)
    sb["hrow"] = sbuf("hrow", [1, 4], F32)
    sb["lnrow"] = sbuf("lnrow", [1, D], F32)
    sb["wx2t"] = sbuf("wx2t", [128, 2, 8, 128], BF16)
    sb["bcols"] = sbuf("bcols", [128, 70], F32)
    sb["halo_b"] = sbuf("halo_b", [128, 8, 1], BF16)
    sb["cpool"] = sbuf("cpool", [128, 8], F32)
    for _ci, _cv in enumerate([0.0, 1.0, -1.0, 1e-5, 0.5, -0.5, 8.0, 1e-12]):
        nc.const_aps.register(F32, _cv, sb["cpool"][:, _ci:_ci + 1])

    ps = keep(nc.psum_tensor("ps", [128, 4096], F32))

    def pb(bi, off=0, width=512):
        return ps[:, 512 * bi + off: 512 * bi + off + width]

    block = keep(nc.Block())

    sems, count = {}, {}
    for nm in (["pe", "dve", "act", "pool"]
               + [f"qs{i}" for i in range(4)] + [f"qg{i}" for i in range(4)]):
        sems[nm] = keep(nc.semaphore("sem_" + nm))
        count[nm] = 0
    pool_idx = {"qs": 0, "qg": 0}
    waited = {e: {s: 0 for s in sems} for e in
              ["tensor", "vector", "scalar", "gpsimd", "sync"]}

    def inc(ins, nm):
        step = 16 if nm[0] == "q" else 1
        count[nm] += step
        if ins is not None:
            ins.then_inc(sems[nm], step)
        return (nm, count[nm])

    def dq(eng, stream, pool, out, in_, **kw):
        nm = f"{pool}{pool_idx[pool] % 4}"
        pool_idx[pool] += 1
        if count[nm] > 0:
            wait(eng, stream, (nm, count[nm]))
        ins = eng.dma_start(out=out, in_=in_, **kw)
        return inc(ins, nm)

    def wait(eng, stream, *toks):
        best = {}
        for t in toks:
            if t is None:
                continue
            ts = t if isinstance(t, list) else [t]
            for u in ts:
                if u is not None:
                    best[u[0]] = max(best.get(u[0], 0), u[1])
        for s, v in best.items():
            if v > waited[stream][s]:
                eng.wait_ge(sems[s], v)
                waited[stream][s] = v

    bank_w = [None] * 8
    bank_r = [[] for _ in range(8)]

    def pe_banks(te, bis):
        toks = []
        for bi in bis:
            toks += bank_r[bi]
        wait(te, "tensor", toks)

    def banks_written(tok, bis):
        for bi in bis:
            bank_w[bi] = tok
            bank_r[bi] = []

    def read_banks(eng, stream, bis):
        wait(eng, stream, [bank_w[bi] for bi in bis])

    def banks_read(tok, bis):
        for bi in bis:
            bank_r[bi].append(tok)

    def TE(fn):
        block.tensor(fn)

    def VE(fn):
        block.vector(fn)

    def SE(fn):
        block.scalar(fn)

    def GE(fn):
        block.gpsimd(fn)

    def SY(fn):
        block.sync(fn)

    final_toks = []
    dumped = set()
    xp_last = {0: None, 1: None}

    def dump(name, ap, tok):
        if name not in dumps or name in dumped:
            return None
        dumped.add(name)
        dd = dram_out("dump_" + name, list(ap.shape),
                      BF16 if ap.dtype == BF16 else F32)
        out = [None]

        def _(sy):
            wait(sy, "sync", tok)
            out[0] = dq(sy, "sync", "qs", dd[:], ap)
            final_toks.append(out[0])
        SY(_)
        return out[0]

    def transpose(te, lhsT, out_ps):
        kk = lhsT.shape[0]
        return te.matmul(out_ps, lhsT, sb["I"][0:kk, 0:kk], start=True, stop=True)

    init_tok = [None]

    def _(sy):
        tks = [dq(sy, "sync", "qs", sb["I"][:], C["c_I128"][:]),
               dq(sy, "sync", "qs", sb["If"][:], C["c_I128f"][:]),
               dq(sy, "sync", "qs", sb["onesr"][:], C["c_onesr"][:]),
               dq(sy, "sync", "qs", sb["tri_pq"][:, 1, :], C["c_tri_q"][:]),
               dq(sy, "sync", "qs", sb["tri_pq"][:, 0, :], C["c_tri_p"][:]),
               dq(sy, "sync", "qs", sb["cpool"][:], C["c_scal"][:])]
        wait(sy, "sync", tks)
        init_tok[0] = tks
    SY(_)

    def _(te):
        wait(te, "tensor", init_tok[0])
    TE(_)

    def _(v):
        wait(v, "vector", init_tok[0])
    VE(_)

    def _(s):
        wait(s, "scalar", init_tok[0])
    SE(_)

    def _(g):
        wait(g, "gpsimd", init_tok[0])
    GE(_)

    zeros_tok = [None]

    def _(v):
        ins = v.memset(sb["halo"][:], 0.0)
        zeros_tok[0] = inc(ins, "dve")
    VE(_)

    def load_rot(seq_w):
        tok = [None]

        def _(sy):
            tok[0] = dq(sy, "sync", "qs", sb["rot"][:],
                        C["c_rot"][:].rearrange("(n p) f d -> n p f d", p=128)[
                            4 * seq_w:4 * seq_w + 4].rearrange("n p f d -> p n f d"))
        SY(_)
        return tok[0]

    def ln_from_dram(src_ap, s_free_toks, dump_tag=None):
        ld = []
        for t in range(NT):
            def _(sy, t=t):
                wait(sy, "sync", xp_last[t % 2])
                ld.append(dq(sy, "sync", "qs", sb["xp"][:, t % 2, :],
                             src_ap.rearrange("(n p) d -> n p d", p=128)[t]))
            SY(_)
            if t % 2 == 0:
                continue
            t0, t1 = t - 1, t

            def _(v, t0=t0, t1=t1):
                wait(v, "vector", ld[t0], ld[t1], s_free_toks)
                for tt in (t0, t1):
                    v.reduce_sum(sb["st4"][:, tt, 0:1], sb["xp"][:, tt % 2, :],
                                 AXL.X)
                for tt in (t0, t1):
                    st = sb["st4"][:, tt, :]
                    v.tensor_scalar(st[:, 0:1], st[:, 0:1], 1.0 / D, None,
                                    ALU.mult)
                for tt in (t0, t1):
                    st = sb["st4"][:, tt, :]
                    v.scalar_tensor_tensor(out=st[:, 3:4], in0=st[:, 0:1],
                                           scalar=-1.0, in1=st[:, 0:1],
                                           op0=ALU.mult, op1=ALU.mult)
                for tt in (t0, t1):
                    st = sb["st4"][:, tt, :]
                    ins = v.tensor_scalar(st[:, 3:4], st[:, 3:4], EPS, None,
                                          ALU.add)
                inc(ins, "dve")
            VE(_)
            t_pre = ("dve", count["dve"])

            def _(s, t0=t0, t1=t1):
                wait(s, "scalar", t_pre)
                for tt in (t0, t1):
                    st = sb["st4"][:, tt, :]
                    s.activation(sb["s"][:, tt, :], sb["xp"][:, tt % 2, :],
                                 AF.Square, accum_out=st[:, 1:2])
                    ins = s.activation(st[:, 1:2], st[:, 1:2], AF.Sqrt,
                                       bias=st[:, 3:4], scale=1.0 / D)
                inc(ins, "act")
            SE(_)
            t_sq = ("act", count["act"])

            def _(v, t0=t0, t1=t1):
                wait(v, "vector", t_sq)
                for tt in (t0, t1):
                    v.reciprocal(sb["st4"][:, tt, 1:2], sb["st4"][:, tt, 1:2])
                v.tensor_copy(sb["scr"][:, 0:1], sb["st4"][:, t0, 0:1])
                for tt in (t0, t1):
                    st = sb["st4"][:, tt, :]
                    ins = v.tensor_scalar(sb["s"][:, tt, :],
                                          sb["xp"][:, tt % 2, :],
                                          st[:, 0:1], st[:, 1:2],
                                          ALU.subtract, ALU.mult)
                    xp_last[tt % 2] = inc(ins, "dve")
                    ln_from_dram.cons[tt] = xp_last[tt % 2]
            VE(_)
        out = [ln_from_dram.cons[t] for t in range(NT)]
        if dump_tag:
            dump(dump_tag, sb["s"][:], out)
            dump(dump_tag + "_st", sb["st4"][:], out)
        return out
    ln_from_dram.cons = {}

    def add_half_s_writeback(src_ap, dst_ap, s_toks, dump_tag=None):
        st_toks = []
        ld = []
        for t in range(NT):
            def _(sy, t=t):
                wait(sy, "sync", xp_last[t % 2])
                ld.append(dq(sy, "sync", "qs", sb["xp"][:, t % 2, :],
                             src_ap.rearrange("(n p) d -> n p d", p=128)[t]))
            SY(_)

            def _(v, t=t):
                wait(v, "vector", ld[t], s_toks[t] if isinstance(s_toks, list) else s_toks)
                ins = v.scalar_tensor_tensor(
                    out=sb["xp"][:, t % 2, :], in0=sb["s"][:, t, :], scalar=0.5,
                    in1=sb["xp"][:, t % 2, :], op0=ALU.mult, op1=ALU.add)
                inc(ins, "dve")
            VE(_)
            tadd = ("dve", count["dve"])
            add_half_s_writeback.s_readers.append(tadd)

            def _(sy, t=t):
                wait(sy, "sync", tadd)
                stok = dq(sy, "sync", "qs",
                          dst_ap.rearrange("(n p) d -> n p d", p=128)[t],
                          sb["xp"][:, t % 2, :])
                st_toks.append(stok)
                xp_last[t % 2] = stok
            SY(_)
        return st_toks
    add_half_s_writeback.s_readers = []

    state = {"cp_done": [False, False], "cp_tok": [None, None]}

    def load_ffn_attn_weights(layer, free_toks):
        toks = {}

        def _(sy):
            wait(sy, "sync", free_toks)
            dq(sy, "sync", "qs", sb["qsrow"][0:1, 0, :],
               W["ffn_q_scale"][layer][None, :])
            toks["qsrow"] = dq(sy, "sync", "qs", sb["qsrow"][0:1, 1, :],
                               W["ffn_k_scale"][layer][None, :])
        SY(_)

        def _(g):
            wait(g, "gpsimd", free_toks)
            wq = W["ffn_wqkv"][layer].rearrange("(c p) n -> p c n", p=128)
            for c in range(8):
                toks[("wqkv", c)] = dq(g, "gpsimd", "qg",
                                       at("wqkv")[:, c, :], wq[:, c, :])
            wor = W["ffn_wo"][layer].rearrange("(c p) n -> p c n", p=128)
            for c in range(4):
                toks[("wo", c)] = dq(g, "gpsimd", "qg",
                                     at("wo")[:, c, :], wor[:, c, :])
            toks["wqkv"] = [toks[("wqkv", c)] for c in range(8)]
            toks["wo"] = [toks[("wo", c)] for c in range(4)]
        GE(_)
        prevc = [None]
        for j, (nm, dst, ddt) in enumerate([("ffn_ln_g", "lgcol", F32),
                                            ("ffn_ln_b", "lbcol", BF16)]):
            rt = [None]

            def _(sy, nm=nm):
                wait(sy, "sync", prevc[0])
                rt[0] = dq(sy, "sync", "qs", sb["lnrow"][0:1, :],
                           W[nm][layer][None, :])
            SY(_)

            def _(te):
                wait(te, "tensor", rt[0])
                pe_banks(te, [7])
                for c in range(8):
                    ins = te.matmul(pb(7, c, 1),
                                    sb["lnrow"][0:1, 128 * c:128 * (c + 1)],
                                    sb["onesr"][0:1, 0:1], start=True, stop=True)
                banks_written(inc(ins, "pe"), [7])
            TE(_)

            def _(v, dst=dst):
                read_banks(v, "vector", [7])
                ins = v.tensor_copy(sb[dst][:], pb(7, 0, 8))
                tk = inc(ins, "dve")
                banks_read(tk, [7])
                prevc[0] = tk
                toks["lncols"] = tk
            VE(_)

        def _(te):
            wait(te, "tensor", toks["qsrow"])
            pe_banks(te, [6])
            te.matmul(pb(6, 0, 64), sb["onesr"][0:1, :], sb["qsrow"][0:1, 0, :],
                      start=True, stop=True)
            ins = te.matmul(pb(6, 64, 64), sb["onesr"][0:1, :], sb["qsrow"][0:1, 1, :],
                            start=True, stop=True)
            banks_written(inc(ins, "pe"), [6])
        TE(_)
        qtok = [None]

        def _(v):
            read_banks(v, "vector", [6])
            ins = v.tensor_copy(sb["qsb"][:].rearrange("p a d -> p (a d)"),
                                pb(6, 0, 128))
            tk = inc(ins, "dve")
            banks_read(tk, [6])
            qtok[0] = tk
        VE(_)
        toks["qsb"] = qtok[0]
        return toks

    def ffn_attn_window(seq, layer, w, wtoks, s_toks, attn_state):
        cur = w % 2
        prv = 1 - cur
        tiles = [4 * w + i for i in range(4)]
        rot_tok = load_rot_win(w, attn_state.get("rot_free"))

        t_sh = [None]

        def _(v):
            wait(v, "vector", [s_toks[t] for t in tiles],
                 attn_state.get("kv_free"), attn_state.get("hfm_done"))
            if w > 0:
                v.tensor_copy(sb["halo"][:], at("h_fm")[:, :, 512:513])
            else:
                v.memset(sb["halo"][:], 0.0)
            for i in range(4):
                ins = v.tensor_copy(at("sbf")[:, i, :], sb["s"][:, tiles[i], :])
            t_sh[0] = inc(ins, "dve")
        VE(_)

        hfm_toks = []
        for i in range(4):
            hb0 = 6 if i % 2 == 0 else 4

            def _(te, i=i, hb0=hb0):
                wait(te, "tensor", t_sh[0])
                pe_banks(te, [hb0, hb0 + 1])
                for c in range(8):
                    ins = transpose(te, at("sbf")[:, i, 128 * c:128 * (c + 1)],
                                    ps[:, 512 * hb0 + 128 * c:
                                       512 * hb0 + 128 * (c + 1)])
                banks_written(inc(ins, "pe"), [hb0, hb0 + 1])
            TE(_)

            def _(s, i=i, hb0=hb0):
                read_banks(s, "scalar", [hb0, hb0 + 1])
                ins = s.activation(at("h_fm")[:, :, 1 + 128 * i:1 + 128 * (i + 1)],
                                   ps[:, 512 * hb0:512 * hb0 + 1024]
                                   .rearrange("p (c n) -> p c n", c=8), AF.Copy)
                tk = inc(ins, "act")
                banks_read(tk, [hb0, hb0 + 1])
                hfm_toks.append(tk)
            SE(_)

        def _(v):
            ins = v.tensor_copy(at("h_fm")[:, :, 0:1], sb["halo"][:])
            hfm_toks.append(inc(ins, "dve"))
        VE(_)
        attn_state["hfm_done"] = list(hfm_toks)

        prep_toks = []
        for i in range(4):
            qb = 0 if i % 2 == 0 else 3

            def _(te, i=i, qb=qb):
                wait(te, "tensor", hfm_toks)
                pe_banks(te, [qb, qb + 1, qb + 2])
                for c in range(8):
                    if i <= 1:
                        wait(te, "tensor", wtoks[("wqkv", c)])
                    off = (1 if c < 4 else 0) + 128 * i
                    for part in range(3):
                        ins = te.matmul(pb(qb + part),
                                        at("h_fm")[:, c, off:off + 128],
                                        at("wqkv")[:, c, 512 * part:512 * (part + 1)],
                                        start=(c == 0), stop=(c == 7))
                banks_written(inc(ins, "pe"), [qb, qb + 1, qb + 2])
            TE(_)

            def _(s, i=i, qb=qb):
                read_banks(s, "scalar", [qb, qb + 1])
                s.activation(at("q_tm")[:, i, :], pb(qb), AF.Copy)
                ins = s.activation(at("k_tm")[:, i, :], pb(qb + 1), AF.Copy)
                banks_read(inc(ins, "act"), [qb, qb + 1])
            SE(_)

            def _(v, i=i, qb=qb):
                read_banks(v, "vector", [qb, qb + 1, qb + 2])
                wait(v, "vector", ("act", count["act"]))
                v.tensor_copy(at("v_tm")[:, cur, i, :], pb(qb + 2))
                v.tensor_tensor(sb["scr"][:], at("q_tm")[:, i, :],
                                at("q_tm")[:, i, :], ALU.mult)
                v.tensor_reduce(sb["ssq"][:, i, :],
                                sb["scr"][:].rearrange("p (h d) -> p h d", h=8),
                                AXL.X, ALU.add)
                v.tensor_tensor(sb["scr"][:], at("k_tm")[:, i, :],
                                at("k_tm")[:, i, :], ALU.mult)
                ins = v.tensor_reduce(sb["ssk"][:, i, :],
                                      sb["scr"][:].rearrange("p (h d) -> p h d", h=8),
                                      AXL.X, ALU.add)
                tk = inc(ins, "dve")
                banks_read(tk, [qb, qb + 1, qb + 2])
                prep_toks.append(tk)
            VE(_)

        if f"qkv_w{w}" in dumps and seq == 0 and layer == 0:
            dump(f"q_raw_w{w}", at("q_tm")[:], prep_toks)
            dump(f"k_raw_w{w}", at("k_tm")[:], prep_toks)
            dump(f"v_raw_w{w}", at("v_tm")[:, cur], prep_toks)

        def _(s):
            wait(s, "scalar", prep_toks)
            for i in range(4):
                s.activation(sb["ssq"][:, i, :], sb["ssq"][:, i, :], AF.Sqrt)
                ins = s.activation(sb["ssk"][:, i, :], sb["ssk"][:, i, :], AF.Sqrt)
            inc(ins, "act")
        SE(_)
        t_sq = ("act", count["act"])

        def _(v):
            wait(v, "vector", t_sq)
            for nm in ("ssq", "ssk"):
                for i in range(4):
                    v.tensor_scalar(sb[nm][:, i, :], sb[nm][:, i, :], 1e-12, None,
                                    ALU.max)
            for nm in ("ssq", "ssk"):
                for i in range(4):
                    ins = v.reciprocal(sb[nm][:, i, :], sb[nm][:, i, :])
            inc(ins, "dve")
        VE(_)
        t_rs = ("dve", count["dve"])

        def qk_prep(e, nm, rn, ssn, qi, ci, si, scrn, iset=(0, 1, 2, 3)):
            for i in iset:
                src3 = at(nm)[:, i, :].rearrange("p (h d) -> p h d", h=8)
                dst = at(rn)[:, i, :]
                dst3 = dst.rearrange("p (h d) -> p h d", h=8)
                e.tensor_tensor(src3, src3,
                                sb[ssn][:, i, :, None].broadcast_to([128, 8, 64]),
                                ALU.mult)
                e.tensor_tensor(src3, src3,
                                sb["qsb"][:, None, qi, :].broadcast_to([128, 8, 64]),
                                ALU.mult)
                e.tensor_tensor(dst3, src3,
                                sb["rot"][:, i, None, ci, :].broadcast_to([128, 8, 64]),
                                ALU.mult)
                rv = at(nm)[:, i, :].rearrange("p (h a d) -> p h a d", h=8,
                                               a=2)[:, :, ::-1, :]
                e.tensor_tensor(sb[scrn][:].rearrange("p (h a d) -> p h a d", h=8, a=2),
                                rv,
                                sb["rot"][:, i, None, si, :]
                                .rearrange("p o (a d) -> p o a d", a=2)
                                .broadcast_to([128, 8, 2, 32]), ALU.mult)
                ins = e.tensor_tensor(dst, dst, sb[scrn][:], ALU.add)
            return ins

        tq = [None]
        tk_ = [None]

        def _(v):
            wait(v, "vector", t_rs, wtoks["qsb"], rot_tok)
            qk_prep(v, "q_tm", "qr_tm", "ssq", 0, 0, 1, "scr")
            tq[0] = inc(qk_prep(v, "k_tm", "kr_tm", "ssk", 1, 2, 3, "scr",
                                iset=(0, 1, 2)), "dve")
        VE(_)

        def _(g):
            wait(g, "gpsimd", t_rs, wtoks["qsb"], rot_tok)
            tk_[0] = inc(qk_prep(g, "k_tm", "kr_tm", "ssk", 1, 2, 3, "scr2",
                                 iset=(3,)), "pool")
        GE(_)
        attn_state["rot_free"] = [tq[0], tk_[0]]

        if f"qkv_w{w}" in dumps and seq == 0 and layer == 0:
            dump(f"qr_w{w}", at("qr_tm")[:], [tq[0]])
            dump(f"kr_w{w}", at("kr_tm")[:], [tk_[0]])

        fm_toks = []
        for p in range(4):
            fb = 6 if p % 2 == 0 else 4

            def _(te, p=p, fb=fb):
                wait(te, "tensor", tq[0], tk_[0])
                pe_banks(te, [fb, fb + 1])
                for i in range(4):
                    transpose(te, at("qr_tm")[:, i, 128 * p:128 * (p + 1)],
                              ps[:, 512 * fb + 128 * i:512 * fb + 128 * (i + 1)])
                for i in range(4):
                    ins = transpose(te, at("kr_tm")[:, i, 128 * p:128 * (p + 1)],
                                    ps[:, 512 * (fb + 1) + 128 * i:
                                       512 * (fb + 1) + 128 * (i + 1)])
                banks_written(inc(ins, "pe"), [fb, fb + 1])
            TE(_)

            def _(s, p=p, fb=fb):
                read_banks(s, "scalar", [fb, fb + 1])
                s.activation(at("q_fm")[:, p, :], pb(fb), AF.Copy)
                ins = s.activation(at("k_fm")[:, cur, p, :], pb(fb + 1), AF.Copy)
                tk2 = inc(ins, "act")
                banks_read(tk2, [fb, fb + 1])
                fm_toks.append(tk2)
            SE(_)

        if w == 1 and seq == 0 and layer == 0:
            dump("qfm_w1", at("q_fm")[:], fm_toks)
            dump("kfm_w1", at("k_fm")[:, cur], fm_toks)
            dump("kfmprev_w1", at("k_fm")[:, prv], fm_toks)
            dump("vtmprev_w1", at("v_tm")[:, prv], fm_toks)
            dump("vtmcur_w1", at("v_tm")[:, cur], fm_toks)

        otile_toks = []
        pv_toks = []
        for i in range(4):
            njt = 5 if w > 0 else i + 1
            msk_tok = {}

            def kf_vt(j, h, i=i):
                p = h // 2
                jt = tiles[i] - 4 + j if w > 0 else j
                jloc = jt - 4 * (w - 1)
                if w == 0:
                    return (at("k_fm")[:, cur, p, 128 * jt:128 * (jt + 1)],
                            at("v_tm")[:, cur, jt, 64 * h:64 * (h + 1)])
                if jloc >= 4:
                    return (at("k_fm")[:, cur, p, 128 * (jloc - 4):128 * (jloc - 3)],
                            at("v_tm")[:, cur, jloc - 4, 64 * h:64 * (h + 1)])
                return (at("k_fm")[:, prv, p, 128 * jloc:128 * (jloc + 1)],
                        at("v_tm")[:, prv, jloc, 64 * h:64 * (h + 1)])

            for h in range(8):
                p = h // 2
                hb = 64 * (h % 2)
                sreg = h % 4
                sbk = [0, 1] if h % 2 == 0 else [2, 3]

                def _(te, i=i, h=h, p=p, hb=hb, sbk=sbk, njt=njt):
                    wait(te, "tensor", fm_toks)
                    pe_banks(te, sbk)
                    base = 512 * sbk[0]
                    q = at("q_fm")[hb:hb + 64, p, 128 * i:128 * (i + 1)]
                    if w == 0:
                        ins = te.matmul(ps[:, base:base + 128 * njt], q,
                                        at("k_fm")[hb:hb + 64, cur, p, 0:128 * njt],
                                        start=True, stop=True)
                    else:
                        s0 = 128 * (4 - i)
                        ins = te.matmul(ps[:, base:base + s0], q,
                                        at("k_fm")[hb:hb + 64, prv, p, 128 * i:512],
                                        start=True, stop=True)
                        if s0 < 512:
                            ins = te.matmul(ps[:, base + s0:base + 512], q,
                                            at("k_fm")[hb:hb + 64, cur, p,
                                                       0:512 - s0],
                                            start=True, stop=True)
                        ins = te.matmul(ps[:, base + 512:base + 640], q,
                                        at("k_fm")[hb:hb + 64, cur, p,
                                                   512 - s0:640 - s0],
                                        start=True, stop=True)
                    banks_written(inc(ins, "pe"), sbk)
                TE(_)

                def _(v, i=i, sbk=sbk, njt=njt, sreg=sreg):
                    read_banks(v, "vector", sbk)
                    base = 512 * sbk[0]
                    if w > 0:
                        v.tensor_tensor(ps[:, base:base + 128], ps[:, base:base + 128],
                                        sb["tri_pq"][:, 0, :], ALU.add)
                    v.tensor_tensor(ps[:, base + 128 * (njt - 1):base + 128 * njt],
                                    ps[:, base + 128 * (njt - 1):base + 128 * njt],
                                    sb["tri_pq"][:, 1, :], ALU.add)
                    ins = v.reduce_max(sb["mx"][:, sreg:sreg + 1],
                                       ps[:, base:base + 128 * njt], AXL.X, negate=True)
                    inc(ins, "dve")
                VE(_)
                msk_tok[h] = ("dve", count["dve"])

                if h % 2 == 0:
                    continue
                h0 = h - 1
                s0, s1 = h0 % 4, h % 4

                def _(s, i=i, h0=h0, h1=h, njt=njt, s0=s0, s1=s1):
                    wait(s, "scalar", msk_tok[h0], msk_tok[h1],
                         attn_state.get(("pexp_free", s0)),
                         attn_state.get(("pexp_free", s1)),
                         attn_state.get(("rz_free", i % 2)) if h0 == 0 else None)
                    s.activation(at("pexp")[:, s0, 0:128 * njt],
                                 ps[:, 0:128 * njt],
                                 AF.Exp, bias=sb["mx"][:, s0:s0 + 1],
                                 scale=1.0,
                                 accum_out=sb["rz"][:, i % 2, h0:h0 + 1])
                    ins = s.activation(at("pexp")[:, s1, 0:128 * njt],
                                       ps[:, 512 * 2:512 * 2 + 128 * njt],
                                       AF.Exp, bias=sb["mx"][:, s1:s1 + 1],
                                       scale=1.0,
                                       accum_out=sb["rz"][:, i % 2, h1:h1 + 1])
                    tk3 = inc(ins, "act")
                    banks_read(tk3, [0, 1, 2, 3])
                SE(_)
                t_exp = ("act", count["act"])
                if w == 1 and i == 0 and h == 1 and seq == 0 and layer == 0:
                    dump("pexp_w1i0h0", at("pexp")[:, 0, :], t_exp)

                for hh in (h0, h):
                    sregh = hh % 4

                    def _(te, sregh=sregh, njt=njt):
                        wait(te, "tensor", t_exp)
                        pe_banks(te, [5, 6] if njt == 5 else [5])
                        for j in range(njt):
                            dst = (ps[:, 512 * 5 + 128 * j:512 * 5 + 128 * (j + 1)]
                                   if j < 4 else pb(6, 0, 128))
                            ins = transpose(te,
                                            at("pexp")[:, sregh, 128 * j:128 * (j + 1)],
                                            dst)
                        tk = inc(ins, "pe")
                        banks_written(tk, [5, 6] if njt == 5 else [5])
                        attn_state[("pexp_free", sregh)] = tk
                    TE(_)

                    def _(v, sregh=sregh, njt=njt):
                        read_banks(v, "vector", [5, 6] if njt == 5 else [5])
                        wait(v, "vector", attn_state.get(("pt_free", sregh)))
                        if njt == 5:
                            v.tensor_copy(at("pt")[:, sregh, 0:512], pb(5))
                            ins = v.tensor_copy(at("pt")[:, sregh, 512:640],
                                                pb(6, 0, 128))
                        else:
                            ins = v.tensor_copy(at("pt")[:, sregh, 0:128 * njt],
                                                pb(5, 0, 128 * njt))
                        tk4 = inc(ins, "dve")
                        banks_read(tk4, [5, 6] if njt == 5 else [5])
                    VE(_)
                    t_pt = ("dve", count["dve"])
                    if w == 1 and i == 0 and hh == 0 and seq == 0 and layer == 0:
                        dump("pt_w1i0h0", at("pt")[:, sregh, :], t_pt)

                    def _(te, i=i, hh=hh, sregh=sregh, njt=njt, t_pt=t_pt):
                        wait(te, "tensor", t_pt)
                        if hh == 0:
                            pe_banks(te, [4])
                        for j in range(njt):
                            _kf, vt = kf_vt(j, hh)
                            ins = te.matmul(pb(4, 64 * hh, 64),
                                            at("pt")[:, sregh, 128 * j:128 * (j + 1)],
                                            vt, start=(j == 0), stop=(j == njt - 1))
                        tk = inc(ins, "pe")
                        attn_state[("pt_free", sregh)] = tk
                        pv_toks.append(tk)
                        if hh == 7:
                            banks_written(tk, [4])
                    TE(_)

            def _(v, i=i):
                read_banks(v, "vector", [4])
                v.reciprocal(sb["rz"][:, i % 2, :], sb["rz"][:, i % 2, :])
                v.tensor_copy(sb["scr"][:, 0:1], sb["rz"][:, i % 2, 0:1])
                ins = v.tensor_tensor(
                    at("o_tm")[:, i, :].rearrange("p (h d) -> p h d", h=8),
                    pb(4).rearrange("p (h d) -> p h d", h=8),
                    sb["rz"][:, i % 2, :, None].broadcast_to([128, 8, 64]), ALU.mult)
                tk5 = inc(ins, "dve")
                banks_read(tk5, [4])
                otile_toks.append(tk5)
                attn_state[("rz_free", i % 2)] = tk5
            VE(_)

        if f"qkv_w{w}" in dumps and seq == 0 and layer == 0:
            dump(f"o_w{w}", at("o_tm")[:], otile_toks)
        attn_state["kv_free"] = list(pv_toks[-8:])

        s_upd = []
        for i in range(4):
            ob = 5 if i % 2 == 0 else 6
            wb = 2 if i % 2 == 0 else 0

            def _(te, i=i, ob=ob):
                wait(te, "tensor", otile_toks[i])
                pe_banks(te, [ob])
                for c in range(4):
                    ins = transpose(te, at("o_tm")[:, i, 128 * c:128 * (c + 1)],
                                    ps[:, 512 * ob + 128 * c:512 * ob + 128 * (c + 1)])
                banks_written(inc(ins, "pe"), [ob])
            TE(_)

            def _(s, i=i, ob=ob):
                read_banks(s, "scalar", [ob])
                ins = s.activation(at("o_fm")[:, i % 2],
                                   pb(ob).rearrange("p (c n) -> p c n", c=4), AF.Copy)
                tk6 = inc(ins, "act")
                banks_read(tk6, [ob])
            SE(_)
            t_ofm = ("act", count["act"])

            def _(te, i=i, wb=wb):
                wait(te, "tensor", t_ofm)
                pe_banks(te, [wb, wb + 1])
                for c in range(4):
                    if i == 0 and w == 0:
                        wait(te, "tensor", wtoks[("wo", c)])
                    for half in range(2):
                        ins = te.matmul(pb(wb + half), at("o_fm")[:, i % 2, c, :],
                                        at("wo")[:, c, 512 * half:512 * (half + 1)],
                                        start=(c == 0), stop=(c == 3))
                banks_written(inc(ins, "pe"), [wb, wb + 1])
            TE(_)

            def _(v, i=i, wb=wb):
                read_banks(v, "vector", [wb, wb + 1])
                t = tiles[i]
                v.tensor_tensor(sb["s"][:, t, 0:512], sb["s"][:, t, 0:512], pb(wb),
                                ALU.add)
                ins = v.tensor_tensor(sb["s"][:, t, 512:1024], sb["s"][:, t, 512:1024],
                                      pb(wb + 1), ALU.add)
                tk7 = inc(ins, "dve")
                banks_read(tk7, [wb, wb + 1])
                s_upd.append(tk7)
            VE(_)

        return s_upd

    def load_rot_win(w, rot_free):
        tok = [None]

        def _(sy):
            wait(sy, "sync", rot_free)
            tok[0] = dq(sy, "sync", "qs", sb["rot"][:],
                        C["c_rot"][:].rearrange("(n p) f d -> n p f d", p=128)[
                            4 * w:4 * w + 4].rearrange("n p f d -> p n f d"))
        SY(_)
        return tok[0]

    def load_mlp_weights(layer, arena_free, need_b1):
        toks = {}

        def _(v):
            wait(v, "vector", arena_free)
            v.memset(mt("w2res")[32:64, 21, :], 0.0)
            ins = v.memset(mt("w2res")[64:128, 21, :], 0.0)
            toks["w2z"] = inc(ins, "dve")
        VE(_)

        if need_b1:
            def _(v):
                wait(v, "vector", arena_free)
                ins = v.memset(mt("b1blk"), 0.0)
                toks["b1z"] = inc(ins, "dve")
            VE(_)

        def _(g):
            wait(g, "gpsimd", arena_free, toks["w2z"], toks.get("b1z"))
            w2 = W["ffn_w2"][layer]
            toks["w2a"] = dq(g, "gpsimd", "qg", mt("w2res")[:, 0:21, :],
                             w2[0:2688].rearrange("(c p) n -> p c n", p=128))
            toks["w2b"] = dq(g, "gpsimd", "qg", mt("w2res")[0:42, 21, :],
                             w2[2688:2730])
            toks["b2"] = dq(g, "gpsimd", "qg", mt("w2res")[64:65, 21, :],
                            W["ffn_b2"][layer][None, :])
            if need_b1:
                b1 = W["ffn_b1"][layer]
                dq(g, "gpsimd", "qg", mt("b1blk")[0:21, :],
                   b1[0:2688].rearrange("(m n) -> m n", n=128))
                dq(g, "gpsimd", "qg", mt("b1blk")[21:22, 0:42], b1[2688:2730][None, :])
                dq(g, "gpsimd", "qg", mt("b1blk")[22:43, :],
                   b1[2730:5418].rearrange("(m n) -> m n", n=128))
                toks["b1"] = dq(g, "gpsimd", "qg", mt("b1blk")[43:44, 0:42],
                                b1[5418:5460][None, :])
        GE(_)
        return toks

    def ffn_mlp(seq, layer, wtoks, s_toks, arena_free):
        cp_ready = state["cp_done"][layer]
        mw = load_mlp_weights(layer, arena_free, need_b1=not cp_ready)

        if not cp_ready:
            def _(te):
                wait(te, "tensor", mw["b1"])
                pe_banks(te, [5])
                ins = te.matmul(pb(5, 0, 44), mt("b1blk"),
                                sb["I"][0:44, 0:44], start=True, stop=True)
                banks_written(inc(ins, "pe"), [5])
            TE(_)

            def _(v):
                read_banks(v, "vector", [5])
                ins = v.tensor_copy(sb["cpcol"][:, layer, :], pb(5, 0, 44))
                tk = inc(ins, "dve")
                banks_read(tk, [5])
                state["cp_tok"][layer] = tk
            VE(_)

        def _(v):
            wait(v, "vector", s_toks, zeros_tok[0])
            for t in range(NT):
                stt = sb["st4"][:, t, :]
                v.reduce_sum(stt[:, 0:1], sb["s"][:, t, 0:512], AXL.X)
                ins = v.reduce_sum(stt[:, 1:2], sb["s"][:, t, 512:1024], AXL.X)
            inc(ins, "dve")
        VE(_)
        t_sums = ("dve", count["dve"])

        def _(s):
            wait(s, "scalar", s_toks)
            for t in range(NT):
                s.activation(sb["scr"][:], sb["s"][:, t, 0:512], AF.Square,
                             accum_out=sb["st4"][:, t, 2:3])
                ins = s.activation(sb["scr2"][:], sb["s"][:, t, 512:1024],
                                   AF.Square, accum_out=sb["st4"][:, t, 3:4])
            inc(ins, "act")
        SE(_)
        t_st = [t_sums, ("act", count["act"])]

        def _(v):
            wait(v, "vector", s_toks, arena_free)
            for j in range(4):
                ins = v.memset(srow_seg("s", j, 0, 1), 0.0)
            inc(ins, "dve")
        VE(_)
        z0 = ("dve", count["dve"])
        srow_toks = [z0]
        for tg in range(4):
            def _(te, tg=tg):
                wait(te, "tensor", t_st)
                pe_banks(te, [4, 5, 6, 7])
                for tt in range(4):
                    t = 4 * tg + tt
                    for j in range(4):
                        ins = te.matmul(ps[0:1, 512 * (4 + j) + 128 * tt:
                                           512 * (4 + j) + 128 * (tt + 1)],
                                        sb["st4"][:, t, j:j + 1], sb["If"][:],
                                        start=True, stop=True)
                banks_written(inc(ins, "pe"), [4, 5, 6, 7])
            TE(_)

            def _(v, tg=tg):
                read_banks(v, "vector", [4, 5, 6, 7])
                for j in range(4):
                    ins = v.tensor_copy(
                        srow_seg("s", j, 1 + 512 * tg, 1 + 512 * (tg + 1)),
                        ps[0:1, 512 * (4 + j):512 * (4 + j) + 512])
                tk = inc(ins, "dve")
                banks_read(tk, [4, 5, 6, 7])
                srow_toks.append(tk)
            VE(_)

        def _(v):
            wait(v, "vector", srow_toks)
            v.tensor_tensor(srow_seg("m", 0, 1, 2049), srow_seg("s", 0, 1, 2049),
                            srow_seg("s", 1, 0, 2048), ALU.add)
            v.tensor_tensor(srow_seg("m", 1, 1, 2049), srow_seg("s", 2, 1, 2049),
                            srow_seg("s", 3, 0, 2048), ALU.add)
            v.tensor_scalar(srow_seg("m", 0, 1, 2049), srow_seg("m", 0, 1, 2049),
                            1.0 / D, None, ALU.mult)
            v.tensor_scalar(srow_seg("m", 1, 1, 2049), srow_seg("m", 1, 1, 2049),
                            1.0 / D, EPS, ALU.mult, ALU.add)
            v.scalar_tensor_tensor(out=srow_seg("m", 2, 1, 2049),
                                   in0=srow_seg("m", 0, 1, 2049), scalar=-1.0,
                                   in1=srow_seg("m", 0, 1, 2049),
                                   op0=ALU.mult, op1=ALU.mult)
            ins = v.tensor_tensor(srow_seg("m", 1, 1, 2049), srow_seg("m", 1, 1, 2049),
                                  srow_seg("m", 2, 1, 2049), ALU.add)
            inc(ins, "dve")
        VE(_)
        t_mr = ("dve", count["dve"])

        def _(s):
            wait(s, "scalar", t_mr)
            s.activation(srow_seg("m", 1, 1, 2049), srow_seg("m", 1, 1, 2049),
                         AF.Ln)
            ins = s.activation(srow_seg("m", 1, 1, 2049), srow_seg("m", 1, 1, 2049),
                               AF.Exp, scale=-0.5)
            inc(ins, "act")
        SE(_)
        t_sqr = ("act", count["act"])

        def _(v):
            wait(v, "vector", t_sqr)
            ins = v.tensor_tensor(srow_seg("m", 3, 1, 2049), srow_seg("m", 0, 1, 2049),
                                  srow_seg("m", 1, 1, 2049), ALU.mult)
            inc(ins, "dve")
        VE(_)
        t_rstd = ("dve", count["dve"])

        mc_toks = []
        for tg in range(4):
            def _(te, tg=tg):
                wait(te, "tensor", t_rstd)
                pe_banks(te, [4])
                for tt in range(4):
                    t = 4 * tg + tt
                    srcs = [srow_seg("m", 0, 1 + 128 * t, 1 + 128 * (t + 1)),
                            srow_seg("m", 1, 1 + 128 * t, 1 + 128 * (t + 1)),
                            srow_seg("m", 0, 2 + 128 * t, 2 + 128 * (t + 1)),
                            srow_seg("m", 1, 2 + 128 * t, 2 + 128 * (t + 1))]
                    for j, srcap in enumerate(srcs):
                        ins = te.matmul(pb(4, 128 * tt + 32 * j, 1), srcap,
                                        sb["onesr"][64:65, 0:1],
                                        start=True, stop=True)
                banks_written(inc(ins, "pe"), [4])
            TE(_)

            def _(v, tg=tg):
                read_banks(v, "vector", [4])
                for tt in range(4):
                    t = 4 * tg + tt
                    ins = v.tensor_copy(sb["mcols"][:, t, :],
                                        pb(4)[:, 128 * tt:128 * tt + 128:32])
                tk = inc(ins, "dve")
                banks_read(tk, [4])
                mc_toks.append(tk)
            VE(_)

        if seq == 0 and layer == 0:
            dump("mcols_d", sb["mcols"][:], mc_toks)
            dump("st4_d", sb["st4"][:], t_st)

        def _(te):
            wait(te, "tensor", t_rstd)
            pe_banks(te, [5])
            ins = te.matmul(pb(5, 0, 1), sb["onesr"][64:65, :],
                            srow_seg("m", 3, 1, 2), start=True, stop=True)
            banks_written(inc(ins, "pe"), [5])
        TE(_)
        halo0_bank_tok = ("pe", count["pe"])

        s_out = list(s_toks) if isinstance(s_toks, list) else [s_toks] * NT
        u_free = [arena_free]
        y_free = [arena_free]
        gg_free = [None]
        w1_free = {0: None, 1: None}
        cp_tok = state["cp_tok"][layer]
        p12_T = [None]
        for half in range(2):
            yfm_toks = []
            for tt in range(8):
                t = 8 * half + tt

                def _(v, t=t, tt=tt):
                    wait(v, "vector", mc_toks, arena_free, p12_T[0])
                    v.tensor_scalar(mt("p12")[:, 0:512], sb["s"][:, t, 0:512],
                                    sb["mcols"][:, t, 0:1], sb["mcols"][:, t, 1:2],
                                    ALU.subtract, ALU.mult)
                    ins = v.tensor_scalar(mt("p12")[:, 512:1024],
                                          sb["s"][:, t, 512:1024],
                                          sb["mcols"][:, t, 2:3],
                                          sb["mcols"][:, t, 3:4],
                                          ALU.subtract, ALU.mult)
                    inc(ins, "dve")
                VE(_)
                t_p12 = ("dve", count["dve"])

                def _(te, t=t, tt=tt):
                    wait(te, "tensor", t_p12, y_free[0])
                    pe_banks(te, [6, 7])
                    for c in range(8):
                        ins = transpose(te, mt("p12")[:, 128 * c:128 * (c + 1)],
                                        ps[:, 512 * 6 + 128 * c:512 * 6 + 128 * (c + 1)])
                    tk = inc(ins, "pe")
                    banks_written(tk, [6, 7])
                    p12_T[0] = tk
                TE(_)

                def _(s, t=t, tt=tt):
                    read_banks(s, "scalar", [6, 7])
                    ins = s.activation(mt("y_fm")[:, :, 1 + 128 * tt:1 + 128 * (tt + 1)],
                                       ps[:, 512 * 6:512 * 6 + 1024]
                                       .rearrange("p (c n) -> p c n", c=8), AF.Copy)
                    tk = inc(ins, "act")
                    banks_read(tk, [6, 7])
                    yfm_toks.append(tk)
                SE(_)

            def _(v, half=half):
                wait(v, "vector", yfm_toks)
                if half == 0:
                    read_banks(v, "vector", [5])
                    wait(v, "vector", halo0_bank_tok)
                    ins = v.tensor_scalar(mt("y_fm")[:, 4:8, 0:1],
                                          pb(5, 0, 1)[:, None, :]
                                          .broadcast_to([128, 4, 1]),
                                          -1.0, None, ALU.mult)
                    banks_read(inc(ins, "dve"), [5])
                    yfm_toks.append(("dve", count["dve"]))
                else:
                    ins = v.tensor_copy(mt("y_fm")[:, 4:8, 0:1],
                                        sb["mx"][:, 0:4, None])
                    yfm_toks.append(inc(ins, "dve"))
            VE(_)

            if half == 0:
                def _(v):
                    wait(v, "vector", yfm_toks)
                    ins = v.tensor_copy(sb["mx"][:, 0:4],
                                        mt("y_fm")[:, 4:8, 1024])
                    yfm_toks.append(inc(ins, "dve"))
                VE(_)

            def _(v, half=half):
                wait(v, "vector", yfm_toks, wtoks["lncols"])
                ins = v.tensor_tensor(mt("y_fm")[:], mt("y_fm")[:],
                                      sb["lgcol"][:, :, None]
                                      .broadcast_to([128, 8, 1025]), ALU.mult)
                inc(ins, "dve")
            VE(_)
            t_y = ("dve", count["dve"])
            if seq == 0 and layer == 0 and half == 0:
                dump("yfm_h0", mt("y_fm")[:], t_y)
                dump("cpcol_d", sb["cpcol"][:, 0, :], state["cp_tok"][layer])
            y_free[0] = None

            def _(v):
                wait(v, "vector", u_free[0], arena_free)
                v.memset(mt("u_fm")[32:64, 21, :], 0.0)
                v.memset(mt("u_fm")[64:128, 21, :], 0.0)
                v.memset(sb["scr"][:, 0:1], 0.0)
                ins = v.memset(mt("u_fm")[64:65, 21, :], 1.0)
                inc(ins, "dve")
            VE(_)
            uz_tok = ("dve", count["dve"])

            glu_toks = []
            for mp in range(22):
                mrows = 42 if mp == 21 else 128
                ld2 = {}
                for ga in range(2):
                    col0 = (2730 if ga == 0 else 0) + 128 * mp
                    slot = ga

                    def _(g, col0=col0, slot=slot, mrows=mrows, ga=ga):
                        wait(g, "gpsimd", w1_free[slot])
                        ld2[ga] = dq(g, "gpsimd", "qg",
                                     mt("w1t")[:, slot, :, 0:mrows],
                                     W["ffn_w1"][layer][:, col0:col0 + mrows]
                                     .rearrange("(c p) n -> p c n", p=128))
                    GE(_)

                    if not cp_ready and half == 0:
                        mcol = (1 - ga) * 22 + mp

                        def _(te, slot=slot, mrows=mrows, w1_tok=ld2[ga]):
                            wait(te, "tensor", w1_tok, wtoks["lncols"],
                                 state["cp_tok"][layer])
                            pe_banks(te, [5])
                            for c in range(8):
                                ins = te.matmul(ps[0:mrows, 512 * 5:512 * 5 + 1],
                                                mt("w1t")[:, slot, c, 0:mrows],
                                                sb["lbcol"][:, c:c + 1],
                                                start=(c == 0), stop=(c == 7))
                            banks_written(inc(ins, "pe"), [5])
                        TE(_)

                        def _(v, mcol=mcol, mrows=mrows):
                            read_banks(v, "vector", [5])
                            wait(v, "vector", state["cp_tok"][layer])
                            ins = v.tensor_tensor(
                                sb["cpcol"][0:mrows, layer, mcol:mcol + 1],
                                sb["cpcol"][0:mrows, layer, mcol:mcol + 1],
                                ps[0:mrows, 512 * 5:512 * 5 + 1], ALU.add)
                            tk = inc(ins, "dve")
                            banks_read(tk, [5])
                            state["cp_tok"][layer] = tk
                        VE(_)

                for grp in range(2):
                    for ga in range(2):
                        bsel = (0 if ga == 0 else 1) + 2 * grp

                        def _(te, mrows=mrows, grp=grp, bsel=bsel, ga=ga,
                              w1_tok=ld2[ga]):
                            wait(te, "tensor", w1_tok, t_y)
                            pe_banks(te, [bsel])
                            for c in range(8):
                                off = (1 if c < 4 else 0) + 512 * grp
                                ins = te.matmul(pb(bsel, 0, 512)[0:mrows, :],
                                                mt("w1t")[:, ga, c, 0:mrows],
                                                mt("y_fm")[:, c, off:off + 512],
                                                start=(c == 0), stop=(c == 7))
                            tk = inc(ins, "pe")
                            banks_written(tk, [bsel])
                            w1_free[ga] = tk
                            if ga == 1 and mp == 21 and grp == 1:
                                y_free[0] = tk
                        TE(_)

                def _(s, mp=mp, mrows=mrows):
                    read_banks(s, "scalar", [0, 2])
                    wait(s, "scalar", state["cp_tok"][layer], u_free[0], uz_tok)
                    for grp in range(2):
                        ins = s.activation(
                            mt("u_fm")[0:mrows, mp, 512 * grp:512 * (grp + 1)],
                            pb(2 * grp, 0, 512)[0:mrows, :], AF.Gelu,
                            bias=sb["cpcol"][0:mrows, layer, 22 + mp:23 + mp],
                            scale=1.0)
                    tk = inc(ins, "act")
                    banks_read(tk, [0, 2])
                SE(_)
                gel_tok = ("act", count["act"])

                for grp in range(2):
                    def _(v, mp=mp, grp=grp, mrows=mrows, gel_tok=gel_tok):
                        bsel = 1 + 2 * grp
                        read_banks(v, "vector", [bsel])
                        wait(v, "vector", gel_tok)
                        ins = v.scalar_tensor_tensor(
                            out=mt("u_fm")[0:mrows, mp, 512 * grp:512 * (grp + 1)],
                            in0=pb(bsel, 0, 512)[0:mrows, :],
                            scalar=sb["cpcol"][0:mrows, layer, mp:mp + 1],
                            in1=mt("u_fm")[0:mrows, mp, 512 * grp:512 * (grp + 1)],
                            op0=ALU.add, op1=ALU.mult)
                        tk = inc(ins, "dve")
                        banks_read(tk, [bsel])
                        glu_toks.append(tk)
                    VE(_)

            if seq == 0 and layer == 0:
                dump(f"ufm_h{half}", mt("u_fm")[:], glu_toks)
                dump("w2res_d", mt("w2res")[:], [mw["w2a"], mw["w2b"], mw["b2"], mw["w2z"]])
                dump(f"yfm_hh{half}", mt("y_fm")[:], t_y)

            for tt in range(8):
                t = 8 * half + tt
                obk = [4, 5] if tt % 2 == 0 else [6, 7]

                def _(te, t=t, tt=tt, obk=obk):
                    wait(te, "tensor", glu_toks, uz_tok, mw["w2a"], mw["w2b"], mw["b2"], mw["w2z"])
                    pe_banks(te, obk)
                    DBG1 = (half == 1 and tt == 1 and seq == 0 and layer == 0
                            and ("w2psum_h1t1" in dumps))
                    if DBG1:
                        def _(v):
                            wait(v, "vector", glu_toks, uz_tok)
                            ins = v.tensor_copy(sb["scr2"][:, 0:128],
                                                mt("u_fm")[:, 0, 128:256])
                            inc(ins, "dve")
                        VE(_)
                        dump("u_at_mm", sb["scr2"][:], ("dve", count["dve"]))
                    for h2 in range(2):
                        kl = [0] if DBG1 else list(range(22))
                        for k in kl:
                            kr = 65 if k == 21 else 128
                            lhs = mt("u_fm")[0:kr, k, 128 * tt:128 * (tt + 1)]
                            ins = te.matmul(pb(obk[h2], 0, 512),
                                            lhs,
                                            mt("w2res")[0:kr, k,
                                                        512 * h2:512 * (h2 + 1)],
                                            start=(k == kl[0]), stop=(k == kl[-1]))
                    tk = inc(ins, "pe")
                    banks_written(tk, obk)
                    u_free[0] = tk
                TE(_)

                def _(v, t=t, obk=obk, half=half, tt=tt):
                    read_banks(v, "vector", obk)
                    if half == 1 and tt == 1 and seq == 0 and layer == 0:
                        v.tensor_copy(sb["scr"][:], pb(obk[0], 0, 512))
                    v.tensor_tensor(sb["s"][:, t, 0:512], sb["s"][:, t, 0:512],
                                    pb(obk[0], 0, 512), ALU.add)
                    ins = v.tensor_tensor(sb["s"][:, t, 512:1024],
                                          sb["s"][:, t, 512:1024],
                                          pb(obk[1], 0, 512), ALU.add)
                    tk = inc(ins, "dve")
                    banks_read(tk, obk)
                    s_out[t] = tk
                    if half == 1 and tt == 1 and seq == 0 and layer == 0:
                        dump("w2psum_h1t1", sb["scr"][:], tk)
                VE(_)

        if not cp_ready:
            state["cp_done"][layer] = True
        return s_out

    sb["wx2stg"] = keep(nc.sbuf_tensor("wx2stg", [128, 32], BF16))

    def rwkv_prep():

        wx2_toks = []
        last_evac = [None]
        for n in range(5):
            for c in range(8):
                stg_tok = [None]

                def _(g, n=n, c=c):
                    wait(g, "gpsimd", last_evac[0])
                    stg_tok[0] = dq(g, "gpsimd", "qg", sb["wx2stg"][:],
                                    W["attn_W_x2"][128 * c:128 * (c + 1), n, :])
                GE(_)

                def _(te, n=n, c=c):
                    wait(te, "tensor", stg_tok[0])
                    pe_banks(te, [6])
                    ins = transpose(te, sb["wx2stg"][:],
                                    ps[0:32, 512 * 6:512 * 6 + 128])
                    banks_written(inc(ins, "pe"), [6])
                TE(_)

                def _(v, n=n, c=c):
                    read_banks(v, "vector", [6])
                    ins = v.tensor_copy(
                        sb["wx2t"][32 * (n % 4):32 * (n % 4) + 32, n // 4, c, :],
                        pb(6, 0, 128)[0:32, :])
                    tk = inc(ins, "dve")
                    banks_read(tk, [6])
                    last_evac[0] = tk
                    wx2_toks.append(tk)
                VE(_)

        row_jobs = [("attn_mu_x", W["attn_mu_x"][None, :], 0, 8),
                    ("xb0", W["attn_x_bias"][0][None, :], 8, 8),
                    ("xb1", W["attn_x_bias"][1][None, :], 16, 8),
                    ("xb2", W["attn_x_bias"][2][None, :], 24, 8),
                    ("xb3", W["attn_x_bias"][3][None, :], 32, 8),
                    ("xb4", W["attn_x_bias"][4][None, :], 40, 8),
                    ("u", W["attn_u"][:].rearrange("a b -> (a b)")[None, :], 48, 4),
                    ("gn_w", W["attn_gn_w"][None, :], 52, 8),
                    ("gn_b", W["attn_gn_b"][None, :], 60, 8)]
        prev_cons = [None]
        for nm, ap_, col0, nch in row_jobs:
            rtok = [None]

            def _(g, ap_=ap_):
                wait(g, "gpsimd", prev_cons[0])
                rtok[0] = dq(g, "gpsimd", "qg", sb["lnrow"][0:1, 0:128 * nch], ap_)
            GE(_)

            def _(te, col0=col0, nch=nch):
                wait(te, "tensor", rtok[0])
                pe_banks(te, [6])
                for c in range(nch):
                    ins = te.matmul(pb(6, c, 1),
                                    sb["lnrow"][0:1, 128 * c:128 * (c + 1)],
                                    sb["onesr"][0:1, 0:1], start=True, stop=True)
                banks_written(inc(ins, "pe"), [6])
            TE(_)

            def _(v, col0=col0, nch=nch):
                read_banks(v, "vector", [6])
                ins = v.tensor_copy(sb["bcols"][:, col0:col0 + nch], pb(6, 0, nch))
                tk = inc(ins, "dve")
                banks_read(tk, [6])
                prev_cons[0] = tk
                wx2_toks.append(tk)
            VE(_)
        return wx2_toks

    def bc(col, n=1):
        return sb["bcols"][:, col:col + n]

    def load_rwkv_w(slot, dram_ap, ncols, free_tok, scale_col=None):
        tok = [None]

        def _(g):
            wait(g, "gpsimd", free_tok)
            tok[0] = dq(g, "gpsimd", "qg", bt("wsl")[:, slot, :, 0:ncols],
                        dram_ap.rearrange("(c p) n -> p c n", p=128))
        GE(_)
        if scale_col is not None:
            def _(v):
                wait(v, "vector", tok[0])
                for c in range(8):
                    ins = v.tensor_scalar(bt("wsl")[:, slot, c, 0:ncols],
                                          bt("wsl")[:, slot, c, 0:ncols],
                                          bc(scale_col + c), None, ALU.mult)
                tok[0] = inc(ins, "dve")
            VE(_)
        return tok[0]

    def rwkv_quarter(seq, qi, xsrc, xdst, bstate, prep_tok):
        tiles = [4 * qi + i for i in range(4)]
        NCH = 16

        ln_toks = []
        xp_cons = []
        ldts = {}
        for ii, t in enumerate(tiles):
            def _(sy, ii=ii, t=t):
                wait(sy, "sync", xp_last[ii % 2])
                ldts[ii] = dq(sy, "sync", "qs", sb["xp"][:, ii % 2, :],
                              xsrc.rearrange("(n p) d -> n p d", p=128)[t])
            SY(_)
            if ii % 2 == 0:
                continue
            i0_, i1_ = ii - 1, ii

            def _(v, i0_=i0_, i1_=i1_):
                wait(v, "vector", ldts[i0_], ldts[i1_], bstate.get("arena_free"))
                for tt in (i0_, i1_):
                    v.reduce_sum(sb["st4"][:, tt, 0:1], sb["xp"][:, tt % 2, :],
                                 AXL.X)
                for tt in (i0_, i1_):
                    st = sb["st4"][:, tt, :]
                    v.tensor_scalar(st[:, 0:1], st[:, 0:1], 1.0 / D, None,
                                    ALU.mult)
                for tt in (i0_, i1_):
                    st = sb["st4"][:, tt, :]
                    v.scalar_tensor_tensor(out=st[:, 3:4], in0=st[:, 0:1],
                                           scalar=-1.0, in1=st[:, 0:1],
                                           op0=ALU.mult, op1=ALU.mult)
                for tt in (i0_, i1_):
                    st = sb["st4"][:, tt, :]
                    ins = v.tensor_scalar(st[:, 3:4], st[:, 3:4], EPS, None,
                                          ALU.add)
                inc(ins, "dve")
            VE(_)
            t_pre = ("dve", count["dve"])

            def _(s, i0_=i0_, i1_=i1_):
                wait(s, "scalar", t_pre)
                for tt in (i0_, i1_):
                    st = sb["st4"][:, tt, :]
                    s.activation(bt("xln_bf")[:, tt, :], sb["xp"][:, tt % 2, :],
                                 AF.Square, accum_out=st[:, 1:2])
                    ins = s.activation(st[:, 1:2], st[:, 1:2], AF.Sqrt,
                                       bias=st[:, 3:4], scale=1.0 / D)
                inc(ins, "act")
            SE(_)
            t_sq = ("act", count["act"])

            def _(v, i0_=i0_, i1_=i1_):
                wait(v, "vector", t_sq)
                for tt in (i0_, i1_):
                    v.reciprocal(sb["st4"][:, tt, 1:2], sb["st4"][:, tt, 1:2])
                v.tensor_copy(sb["scr"][:, 0:1], sb["st4"][:, i0_, 0:1])
                for tt in (i0_, i1_):
                    st = sb["st4"][:, tt, :]
                    ins = v.tensor_scalar(bt("xln_bf")[:, tt, :],
                                          sb["xp"][:, tt % 2, :],
                                          st[:, 0:1], st[:, 1:2],
                                          ALU.subtract, ALU.mult)
                    tk = inc(ins, "dve")
                    xp_last[tt % 2] = tk
                    xp_cons.append(tk)
                    ln_toks.append(tk)
            VE(_)

        fm_toks = []
        for ii in range(4):
            def _(te, ii=ii):
                wait(te, "tensor", ln_toks[ii], bstate.get("arena_free"))
                pe_banks(te, [6, 7])
                for c in range(8):
                    ins = transpose(te, bt("xln_bf")[:, ii, 128 * c:128 * (c + 1)],
                                    ps[:, 512 * 6 + 128 * c:512 * 6 + 128 * (c + 1)])
                banks_written(inc(ins, "pe"), [6, 7])
            TE(_)

            def _(s, ii=ii):
                read_banks(s, "scalar", [6, 7])
                ins = s.activation(bt("xln_fm")[:, :, 1 + 128 * ii:1 + 128 * (ii + 1)],
                                   ps[:, 512 * 6:512 * 6 + 1024]
                                   .rearrange("p (c n) -> p c n", c=8), AF.Copy)
                tk = inc(ins, "act")
                banks_read(tk, [6, 7])
                fm_toks.append(tk)
            SE(_)

        def _(v):
            wait(v, "vector", bstate.get("arena_free"))
            if qi == 0:
                ins = v.memset(bt("xln_fm")[:, :, 0:1], 0.0)
            else:
                ins = v.tensor_copy(bt("xln_fm")[:, :, 0:1], sb["halo_b"][:])
            fm_toks.append(inc(ins, "dve"))
        VE(_)

        def _(v):
            wait(v, "vector", fm_toks)
            ins = v.tensor_copy(sb["halo_b"][:], bt("xln_fm")[:, :, 512:513])
            fm_toks.append(inc(ins, "dve"))
        VE(_)

        def _(v):
            wait(v, "vector", fm_toks, prep_tok)
            ins = v.tensor_tensor(bt("delta")[:], bt("xln_fm")[:, :, 0:512],
                                  bt("xln_fm")[:, :, 1:513], ALU.subtract)
            inc(ins, "dve")
        VE(_)
        t_delta = ("dve", count["dve"])

        def _(v):
            wait(v, "vector", t_delta)
            for c in range(8):
                ins = v.scalar_tensor_tensor(
                    out=bt("xi")[:, c, :], in0=bt("delta")[:, c, :],
                    scalar=bc(c), in1=bt("xln_fm")[:, c, 1:513],
                    op0=ALU.mult, op1=ALU.add)
            inc(ins, "dve")
        VE(_)
        t_xmu = ("dve", count["dve"])

        wx1_tok = load_rwkv_w(0, W["attn_W_x1"], 160, bstate.get("wsl_free", [None, None])[0])
        for m2 in range(2):
            mr = 128 if m2 == 0 else 32

            def _(te, m2=m2, mr=mr):
                wait(te, "tensor", t_xmu, wx1_tok)
                pe_banks(te, [m2])
                for c in range(8):
                    ins = te.matmul(pb(m2, 0, 512)[0:mr, :],
                                    bt("wsl")[:, 0, c, 128 * m2:128 * m2 + mr],
                                    bt("xi")[:, c, :], start=(c == 0), stop=(c == 7))
                banks_written(inc(ins, "pe"), [m2])
            TE(_)

            def _(s, m2=m2, mr=mr):
                read_banks(s, "scalar", [m2])
                ins = s.activation(bt("low")[0:mr, m2, :], pb(m2, 0, 512)[0:mr, :],
                                   AF.Tanh)
                tk = inc(ins, "act")
                banks_read(tk, [m2])
            SE(_)
        t_low = ("act", count["act"])

        projs = [("r", W["attn_W_r"], 512, "r_tm", None),
                 ("w", None, 0, None, None),
                 ("k", W["attn_W_k"], 512, "k_tm2", None),
                 ("v", W["attn_W_v"], 1024, "v_tm2", None),
                 ("g", W["attn_W_g"], 1024, "g_tm", None)]
        xi_free = [None]
        wsl_free = bstate.get("wsl_free", [None, None])
        proj_toks = {}
        for bi, (bname, wdram, dout, obuf, _sc) in enumerate(projs):
            lowrow = 32 * (bi % 4)
            lowcol = bi // 4
            xi_toks = []
            for c in range(8):
                bk = c % 2

                def _(te, bi=bi, c=c, bk=bk, lowrow=lowrow, lowcol=lowcol):
                    wait(te, "tensor", t_low, prep_tok, xi_free[0])
                    pe_banks(te, [bk])
                    tp = (lowrow, 0) if lowrow else None
                    ins = te.matmul(pb(bk, 0, 512),
                                    sb["wx2t"][lowrow:lowrow + 32, bi // 4, c, :],
                                    bt("low")[lowrow:lowrow + 32, lowcol, :],
                                    start=True, stop=True, tile_position=tp)
                    banks_written(inc(ins, "pe"), [bk])
                TE(_)

                def _(v, bi=bi, c=c, bk=bk):
                    read_banks(v, "vector", [bk])
                    wait(v, "vector", t_xmu, xi_free[0])
                    v.scalar_tensor_tensor(
                        out=sb["scr2"][:].bitcast(BF16)[:, 0:512],
                        in0=pb(bk, 0, 512), scalar=bc(8 + 8 * bi + c),
                        in1=bt("delta")[:, c, :], op0=ALU.add, op1=ALU.mult)
                    ins = v.tensor_tensor(bt("xi")[:, c, :],
                                          sb["scr2"][:].bitcast(BF16)[:, 0:512],
                                          bt("xln_fm")[:, c, 1:513], ALU.add)
                    tk = inc(ins, "dve")
                    banks_read(tk, [bk])
                    xi_toks.append(tk)
                VE(_)
            t_xi = list(xi_toks)

            if bname == "w":
                ww1_tok = load_rwkv_w(1, W["attn_W_w1"], 64, wsl_free[1])
                aug_tok = [None]

                def _(g):
                    wait(g, "gpsimd", bstate.get("ww2_free"))
                    dq(g, "gpsimd", "qg", bt("ww2aug")[0:64, :], W["attn_W_w2"][:])
                    aug_tok[0] = dq(g, "gpsimd", "qg", bt("ww2aug")[64:65, :],
                                    W["attn_b_w2"][None, :])
                GE(_)

                def _(te):
                    wait(te, "tensor", t_xi, ww1_tok)
                    pe_banks(te, [2])
                    for c in range(8):
                        ins = te.matmul(pb(2, 0, 512)[0:64, :],
                                        bt("wsl")[:, 1, c, 0:64],
                                        bt("xi")[:, c, :], start=(c == 0),
                                        stop=(c == 7))
                    banks_written(inc(ins, "pe"), [2])
                TE(_)

                def _(s):
                    read_banks(s, "scalar", [2])
                    ins = s.activation(bt("lora")[0:64, :], pb(2, 0, 512)[0:64, :],
                                       AF.Tanh)
                    tk = inc(ins, "act")
                    banks_read(tk, [2])
                SE(_)
                t_lora = ("act", count["act"])

                def _(v):
                    wait(v, "vector", bstate.get("arena_free"))
                    ins = v.memset(bt("lora")[64:65, :], 1.0)
                    inc(ins, "dve")
                VE(_)
                t_ones = ("dve", count["dve"])

                for ii in range(4):
                    def _(te, ii=ii):
                        wait(te, "tensor", t_lora, t_ones, aug_tok[0])
                        pe_banks(te, [3])
                        ins = te.matmul(pb(3, 0, 512),
                                        bt("lora")[0:65, 128 * ii:128 * (ii + 1)],
                                        bt("ww2aug")[0:65, :], start=True, stop=True)
                        banks_written(inc(ins, "pe"), [3])
                    TE(_)

                    def _(s, ii=ii):
                        read_banks(s, "scalar", [3])
                        ins = s.activation(w_tm(ii), pb(3, 0, 512), AF.Exp)
                        tk = inc(ins, "act")
                        banks_read(tk, [3])
                        proj_toks.setdefault("w", []).append(tk)
                    SE(_)
                xi_free[0] = ("pe", count["pe"])
                bstate["ww2_free"] = ("pe", count["pe"])
                wsl_free[1] = ("pe", count["pe"])
                if seq == 0 and qi == 0:
                    dump("w_e", sb["s"][:, 4:6, :], proj_toks["w"])
                    dump("lora_d", bt("lora")[0:64, :], proj_toks["w"])
                    dump("xi_w", bt("xi")[:], proj_toks["w"])
            else:
                slot = (1 + bi) % 2
                wtok = load_rwkv_w(slot, wdram, dout, wsl_free[slot])
                nb = dout // 512
                for ii in range(4):
                    bks = [2, 3] if nb == 1 else [2, 3]

                    def _(te, ii=ii, bi=bi, slot=slot, dout=dout, nb=nb):
                        wait(te, "tensor", t_xi, wtok)
                        pe_banks(te, [2, 3][:nb] if ii % 2 == 0 else [4, 5][:nb])
                        bset = [2, 3][:nb] if ii % 2 == 0 else [4, 5][:nb]
                        for h2 in range(nb):
                            for c in range(8):
                                ins = te.matmul(
                                    pb(bset[h2], 0, 512),
                                    bt("xi")[:, c, 128 * ii:128 * (ii + 1)],
                                    bt("wsl")[:, slot, c, 512 * h2:512 * (h2 + 1)],
                                    start=(c == 0), stop=(c == 7))
                        banks_written(inc(ins, "pe"), bset)
                    TE(_)
                    mm_tok = ("pe", count["pe"])

                    def _(v, ii=ii, bi=bi, obuf=obuf, nb=nb):
                        bset = [2, 3][:nb] if ii % 2 == 0 else [4, 5][:nb]
                        read_banks(v, "vector", bset)
                        for h2 in range(nb):
                            ins = v.tensor_copy(
                                bt(obuf)[:, ii, 512 * h2:512 * (h2 + 1)],
                                pb(bset[h2], 0, 512))
                        tk = inc(ins, "dve")
                        banks_read(tk, bset)
                        proj_toks.setdefault(obuf, []).append(tk)
                    VE(_)
                xi_free[0] = ("pe", count["pe"])
                wsl_free[slot] = ("pe", count["pe"])
        bstate["wsl_free"] = [("pe", count["pe"]), ("pe", count["pe"])]

        if bstop == "proj":
            bstate["arena_free"] = [("dve", count["dve"]), ("pe", count["pe"]),
                                    ("act", count["act"])]
            bstate["wsl_free"] = [("pe", count["pe"]), ("pe", count["pe"])]
            return []
        if seq == 0 and qi == 0 and dumps:
            dump("low_d", bt("low")[:, 0, :], ("act", count["act"]))
            dump("low_d2", bt("low")[0:32, 1, :], ("act", count["act"]))
            for nmx in ["r_tm", "k_tm2", "v_tm2", "g_tm", "khat"]:
                dump(nmx + "_d", bt(nmx)[:],
                     [("dve", count["dve"]), ("act", count["act"])])

        o_toks = []
        s_upd_tok = [bstate.get("S0_tok")]
        for gi in range(bng):
            def _(te, gi=gi):
                wait(te, "tensor", proj_toks["w"],
                     bstate.get("fact_free"), state.get("rwkv_lc_tok"))
                pe_banks(te, [0, 1, 2, 3, 4])
                te.matmul(pb(0), bt("Lc")[:, 0, :], w_tm_g(gi), start=True, stop=True)
                te.matmul(pb(1), bt("Lc")[:, 1, :], w_tm_g(gi), start=True, stop=True)
                te.matmul(pb(2), bt("Lc")[:, 2, :], w_tm_g(gi), start=True, stop=True)
                te.matmul(pb(4), bt("Lc")[:, 4, :], w_tm_g(gi), start=True, stop=True)
                for g4 in range(4):
                    ins = te.matmul(pb(3, 4 * g4, 4),
                                    w_tm_g(gi)[:, 128 * g4:128 * (g4 + 1)],
                                    bt("Lc")[:, 3, 0:128:32], start=True, stop=True)
                banks_written(inc(ins, "pe"), [0, 1, 2, 3, 4])
            TE(_)

            def _(s, gi=gi):
                read_banks(s, "scalar", [0, 1, 2, 3, 4])
                s.activation(bt("Fr")[:], pb(0), AF.Exp)
                s.activation(bt("Fk")[:], pb(1), AF.Exp, scale=-1.0)
                s.activation(bt("Fd")[:], pb(2), AF.Exp, scale=-1.0)
                s.activation(bt("Fr0")[:], pb(4), AF.Exp)
                ins = s.activation(bt("dcols")[:].rearrange("p a b -> p (a b)"),
                                   pb(3, 0, 16), AF.Exp, scale=-1.0)
                tk = inc(ins, "act")
                banks_read(tk, [0, 1, 2, 3, 4])
            SE(_)
            t_facts = ("act", count["act"])

            def _(v, gi=gi):
                wait(v, "vector", t_facts, proj_toks["r_tm"], proj_toks["k_tm2"])
                v.tensor_tensor(bt("r_tm")[:, gi, :].bitcast(BF16) if False else
                                sb["scr"][:].bitcast(BF16)[:, 0:512],
                                bt("r_tm")[:, gi, :], bt("Fr")[:], ALU.mult)
                v.tensor_tensor(sb["scr2"][:].bitcast(BF16)[:, 0:512],
                                bt("k_tm2")[:, gi, :], bt("Fk")[:], ALU.mult)
                v.tensor_tensor(bt("khat")[:, gi, :], bt("k_tm2")[:, gi, :],
                                bt("Fd")[:], ALU.mult)
                ins = v.tensor_tensor(sb["gg"][:], bt("r_tm")[:, gi, :],
                                      bt("Fr0")[:], ALU.mult)
                inc(ins, "dve")
            VE(_)
            t_fm0 = ("dve", count["dve"])

            def _(te, gi=gi):
                wait(te, "tensor", t_fm0, bstate.get("bfm_free"))
                pe_banks(te, [4, 5])
                for g4 in range(4):
                    transpose(te, sb["scr"][:].bitcast(BF16)[:, 0:512]
                              [:, 128 * g4:128 * (g4 + 1)],
                              ps[:, 512 * 4 + 128 * g4:512 * 4 + 128 * (g4 + 1)])
                for g4 in range(4):
                    ins = transpose(te, sb["scr2"][:].bitcast(BF16)[:, 0:512]
                                    [:, 128 * g4:128 * (g4 + 1)],
                                    ps[:, 512 * 5 + 128 * g4:512 * 5 + 128 * (g4 + 1)])
                banks_written(inc(ins, "pe"), [4, 5])
            TE(_)

            def _(v, gi=gi):
                read_banks(v, "vector", [4, 5])
                v.tensor_copy(bt("rt_fm")[:].rearrange("p a b -> p (a b)"), pb(4))
                ins = v.tensor_copy(bt("kt_fm")[:].rearrange("p a b -> p (a b)"),
                                    pb(5))
                tk = inc(ins, "dve")
                banks_read(tk, [4, 5])
            VE(_)
            t_fm1 = ("dve", count["dve"])

            def _(te, gi=gi):
                wait(te, "tensor", t_fm1)
                pe_banks(te, [3, 4, 5])
                for g4 in range(4):
                    transpose(te, bt("r_tm")[:, gi, 128 * g4:128 * (g4 + 1)],
                              ps[:, 512 * 4 + 128 * g4:512 * 4 + 128 * (g4 + 1)])
                for g4 in range(4):
                    transpose(te, bt("k_tm2")[:, gi, 128 * g4:128 * (g4 + 1)],
                              ps[:, 512 * 5 + 128 * g4:512 * 5 + 128 * (g4 + 1)])
                for g4 in range(4):
                    ins = transpose(te, sb["gg"][:, 128 * g4:128 * (g4 + 1)],
                                    ps[:, 512 * 3 + 128 * g4:512 * 3 + 128 * (g4 + 1)])
                banks_written(inc(ins, "pe"), [3, 4, 5])
            TE(_)

            def _(v, gi=gi):
                read_banks(v, "vector", [3, 4, 5])
                v.tensor_copy(bt("rr_fm")[:].rearrange("p a b -> p (a b)"), pb(4))
                v.tensor_copy(bt("ku_fm")[:].rearrange("p a b -> p (a b)"), pb(5))
                v.tensor_copy(bt("r0t_fm")[:].rearrange("p a b -> p (a b)"), pb(3))
                ins = v.tensor_tensor(bt("ku_fm")[:], bt("ku_fm")[:],
                                      bc(48, 4)[:, :, None]
                                      .broadcast_to([128, 4, 128]), ALU.mult)
                tk = inc(ins, "dve")
                banks_read(tk, [3, 4, 5])
            VE(_)
            t_fm2 = ("dve", count["dve"])
            bstate["bfm_free"] = None

            def _(v, gi=gi):
                wait(v, "vector", proj_toks["r_tm"], proj_toks["k_tm2"],
                     prep_tok, bstate.get("urep_tok"))
                v.tensor_tensor(sb["scr"][:], bt("r_tm")[:, gi, :],
                                bt("k_tm2")[:, gi, :], ALU.mult)
                v.tensor_tensor(sb["scr"][:], sb["scr"][:], bt("urep")[:],
                                ALU.mult)
                v.tensor_reduce(bt("dvt")[:],
                                sb["scr"][:].rearrange("p (h d) -> p h d",
                                                       h=16),
                                AXL.X, ALU.add)
                v.tensor_copy(sb["scr2"][:, 0:1], sb["scr"][:, 0:1])
                ins = v.tensor_tensor(bt("dvv")[:].rearrange("p (h d) -> p h d",
                                                             h=16),
                                      bt("v_tm2")[:, gi, :]
                                      .rearrange("p (h d) -> p h d", h=16),
                                      bt("dvt")[:, :, None]
                                      .broadcast_to([128, 16, 64]), ALU.mult)
                inc(ins, "dve")
            VE(_)
            t_dv = ("dve", count["dve"])
            if seq == 0 and qi == 0 and gi == 0 and dumps:
                dump("dvt_d", bt("dvt")[:], t_dv)
                dump("urep_d", bt("urep")[:], t_dv)
            if bstop == "fm":
                continue

            if seq == 0 and qi == 0 and gi == 1 and dumps:
                dump("otile0_d", sb["s"][:, 6, :], o_toks)
            if seq == 0 and qi == 0 and gi == 2 and dumps:
                dump("otile1_d", sb["s"][:, 7, :], o_toks)
            if seq == 0 and qi == 0 and gi == 0 and dumps:
                dump("Fr_d", bt("Fr")[:], t_facts)
                dump("Fk_d", bt("Fk")[:], t_facts)
                dump("dcols_d", bt("dcols")[:], t_facts)
                dump("rtfm_d", bt("rt_fm")[:], t_fm2)
                dump("ktfm_d", bt("kt_fm")[:], t_fm2)

            for cl in range(bnc):
                ch = 4 * gi + cl
                cb = 32 * cl

                def _(te, gi=gi, cl=cl, cb=cb):
                    wait(te, "tensor", t_fm1, t_fm2)
                    pe_banks(te, [6])
                    for h in range(16):
                        hb = 32 * (h % 4)
                        hg = h // 4
                        ins = te.matmul(ps[hb:hb + 32,
                                           512 * 6 + 32 * h:512 * 6 + 32 * h + 32],
                                        bt("kt_fm")[hb:hb + 32, hg, cb:cb + 32],
                                        bt("rt_fm")[hb:hb + 32, hg, cb:cb + 32],
                                        start=True, stop=True,
                                        tile_position=(hb, hb))
                    banks_written(inc(ins, "pe"), [6])
                TE(_)
                if bstop == "amm":
                    continue

                def _(v, cl=cl, cb=cb):
                    read_banks(v, "vector", [6])
                    wait(v, "vector", s_upd_tok[0], state.get("rwkv_lc_tok"))
                    tri = bt("masks")[0:32, 0, 0:128].rearrange(
                        "p (j c) -> p j c", j=4)
                    for q in range(4):
                        ins = v.scalar_tensor_tensor(
                            out=bt("B1")[cb:cb + 32, :]
                            .rearrange("p (j c) -> p j c", j=4)
                            [:, :, 32 * q:32 * q + 32],
                            in0=pb(6)[32 * q:32 * q + 32, :]
                            .rearrange("p (j c) -> p j c", j=4)
                            [:, :, 32 * q:32 * q + 32],
                            scalar=1.0, in1=tri, op0=ALU.mult, op1=ALU.mult)
                    tk = inc(ins, "dve")
                    banks_read(tk, [6])
                VE(_)
                t_B = ("dve", count["dve"])
                if bstop == "bevac":
                    continue

                def _(te, gi=gi, cl=cl, cb=cb):
                    wait(te, "tensor", t_B, proj_toks["v_tm2"])
                    pe_banks(te, [0, 1, 4, 5])
                    for h in range(16):
                        hb = 32 * (h % 4)
                        hg = h // 4
                        ot = pb(0, 0, 512) if h < 8 else pb(1, 0, 512)
                        oc = 64 * (h % 8)
                        ins = te.matmul(ot[:, oc:oc + 64][cb:cb + 32, :],
                                  bt("B1")[cb:cb + 32, 32 * h:32 * h + 32],
                                  bt("v_tm2")[cb:cb + 32, gi, 64 * h:64 * h + 64],
                                  start=True, stop=True, tile_position=(cb, cb))
                    if bstop != "omm":
                        for h in range(16):
                            ins = te.matmul(
                                ps[cb:cb + 32,
                                   512 * 4 + 64 * h:512 * 4 + 64 * h + 64],
                                bt("khat")[cb:cb + 32, gi, 32 * h:32 * h + 32],
                                bt("v_tm2")[cb:cb + 32, gi, 64 * h:64 * h + 64],
                                start=True, stop=True, tile_position=(cb, cb))
                    banks_written(inc(ins, "pe"), [0, 1, 4, 5])
                TE(_)

                def _(te, gi=gi, cl=cl, cb=cb):
                    wait(te, "tensor", s_upd_tok[0])
                    pe_banks(te, [3])
                    for h in range(16):
                        hb = 32 * (h % 4)
                        hg = h // 4
                        ins = te.matmul(pb(3)[hb:hb + 32, 64 * hg:64 * hg + 64],
                                        bt("r0t_fm")[hb:hb + 32, hg, cb:cb + 32],
                                        bt("S0b")[hb:hb + 32, hg, :],
                                        start=True, stop=True,
                                        tile_position=(hb, hb))
                    banks_written(inc(ins, "pe"), [3])
                TE(_)
                t_osmm = ("pe", count["pe"])
                if bstop == "omm":
                    continue

                def _(v, gi=gi, cl=cl, cb=cb, t_dv=t_dv):
                    read_banks(v, "vector", [0, 1, 3])
                    wait(v, "vector", t_dv)
                    sc2 = sb["scr2"][:].bitcast(BF16)
                    for q in range(4):
                        v.tensor_copy(
                            sc2[cb:cb + 32, :].rearrange("p (j c) -> p j c", j=4)
                            [:, :, 64 * q:64 * q + 64],
                            pb(3)[32 * q:32 * q + 32, 0:256]
                            .rearrange("p (g c) -> p g c", g=4))
                    ot = o_tm_t(gi)
                    for hf in range(2):
                        v.tensor_tensor(ot[cb:cb + 32, 512 * hf:512 * (hf + 1)],
                                        bt("dvv")[cb:cb + 32,
                                                  512 * hf:512 * (hf + 1)],
                                        sc2[cb:cb + 32, 512 * hf:512 * (hf + 1)],
                                        ALU.add)
                    v.tensor_tensor(ot[cb:cb + 32, 0:512],
                                    pb(0, 0, 512)[cb:cb + 32, :],
                                    ot[cb:cb + 32, 0:512], ALU.add)
                    ins = v.tensor_tensor(ot[cb:cb + 32, 512:1024],
                                          pb(1, 0, 512)[cb:cb + 32, :],
                                          ot[cb:cb + 32, 512:1024], ALU.add)
                    tk = inc(ins, "dve")
                    banks_read(tk, [0, 1, 3])
                    o_toks.append(tk)
                VE(_)

                def _(v, gi=gi, cl=cl, ch=ch, cb=cb):
                    read_banks(v, "vector", [4, 5])
                    wait(v, "vector", s_upd_tok[0], t_facts)
                    sdl = ps[:, 512 * 4:512 * 6]
                    for q in range(4):
                        v.tensor_copy(pb(2, 0, 256)[32 * q:32 * q + 32, :]
                                      .rearrange("p (g c) -> p g c", g=4),
                                      sdl[cb:cb + 32, :]
                                      .rearrange("p (j c) -> p j c", j=4)
                                      [:, :, 64 * q:64 * q + 64])
                    v.tensor_tensor(bt("Stmp")[:], bt("S0")[:],
                                    bt("dcols")[:, :, cl:cl + 1]
                                    .broadcast_to([128, 4, 64]), ALU.mult)
                    v.tensor_tensor(bt("S0")[:].rearrange("p a b -> p (a b)"),
                                    bt("Stmp")[:].rearrange("p a b -> p (a b)"),
                                    pb(2, 0, 256), ALU.add)
                    ins = v.tensor_copy(bt("S0b")[:].rearrange("p a b -> p (a b)"),
                                        bt("S0")[:].rearrange("p a b -> p (a b)"))
                    tk = inc(ins, "dve")
                    banks_read(tk, [2, 4, 5])
                    s_upd_tok[0] = tk
                VE(_)
                if seq == 0 and qi == 0 and gi == 0 and cl == 0 and dumps:
                    dump("B1_d", bt("B1")[0:32, :], t_B)
                    dump("S0_d", bt("S0")[:], s_upd_tok[0])
                    dump("o0_d", sb["s"][0:32, 6, :], o_toks)
        bstate["S0_tok"] = s_upd_tok[0]
        bstate["fact_free"] = ("pe", count["pe"])
        if bstop in ("chunks", "fm", "amm", "bevac", "omm"):
            bstate["arena_free"] = [("dve", count["dve"]), ("pe", count["pe"]),
                                    ("act", count["act"])]
            bstate["wsl_free"] = [("pe", count["pe"]), ("pe", count["pe"])]
            return []

        wog_tok = load_rwkv_w(0, W["attn_W_o"], 1024, bstate["wsl_free"][0],
                              scale_col=52)
        wob_tok = load_rwkv_w(1, W["attn_W_o"], 1024, bstate["wsl_free"][1],
                              scale_col=60)
        st_toks = []
        for ii in range(4):
            def _(v, ii=ii):
                wait(v, "vector", o_toks)
                ot = o_tm_t(ii)
                v.tensor_reduce(bt("gnstat")[:, 0:16],
                                ot.rearrange("p (h d) -> p h d", h=16),
                                AXL.X, ALU.add)
                v.tensor_tensor(gn_scr(ii), ot, ot, ALU.mult)
                ins = v.tensor_reduce(bt("gnstat")[:, 16:32],
                                      gn_scr(ii).rearrange("p (h d) -> p h d", h=16),
                                      AXL.X, ALU.add)
                inc(ins, "dve")
            VE(_)
            t_gs = ("dve", count["dve"])

            def _(v, ii=ii):
                wait(v, "vector", t_gs)
                g = bt("gnstat")
                v.tensor_scalar(g[:, 0:16], g[:, 0:16], 1.0 / 64,
                                None, ALU.mult)
                v.tensor_scalar(g[:, 16:32], g[:, 16:32], 1.0 / 64,
                                EPS, ALU.mult, ALU.add)
                v.scalar_tensor_tensor(out=g[:, 32:48],
                                       in0=g[:, 0:16], scalar=-1.0,
                                       in1=g[:, 0:16],
                                       op0=ALU.mult, op1=ALU.mult)
                v.tensor_copy(sb["scr2"][:, 0:1], g[:, 0:1])
                ins = v.tensor_tensor(g[:, 16:32], g[:, 16:32],
                                      g[:, 32:48], ALU.add)
                inc(ins, "dve")
            VE(_)
            t_gs2 = ("dve", count["dve"])

            def _(s, ii=ii):
                wait(s, "scalar", t_gs2)
                ins = s.activation(bt("gnstat")[:, 16:32], bt("gnstat")[:, 16:32],
                                   AF.Sqrt)
                inc(ins, "act")
            SE(_)
            t_gsq = ("act", count["act"])

            def _(v, ii=ii):
                wait(v, "vector", t_gsq)
                g = bt("gnstat")
                v.reciprocal(g[:, 16:32], g[:, 16:32])
                v.tensor_copy(sb["scr2"][:, 0:1], g[:, 0:1])
                ot = o_tm_t(ii)
                v.tensor_tensor(ot.rearrange("p (h d) -> p h d", h=16),
                                ot.rearrange("p (h d) -> p h d", h=16),
                                g[:, 0:16][:, :, None]
                                .broadcast_to([128, 16, 64]), ALU.subtract)
                ins = v.tensor_tensor(ot.rearrange("p (h d) -> p h d", h=16),
                                      ot.rearrange("p (h d) -> p h d", h=16),
                                      g[:, 16:32][:, :, None]
                                      .broadcast_to([128, 16, 64]), ALU.mult)
                inc(ins, "dve")
            VE(_)
            t_on = ("dve", count["dve"])

            def _(s, ii=ii):
                wait(s, "scalar", t_on, proj_toks["g_tm"])
                ins = s.activation(bt("g_tm")[:, ii, :], bt("g_tm")[:, ii, :],
                                   AF.Silu)
                inc(ins, "act")
            SE(_)
            t_sg = ("act", count["act"])

            def _(v, ii=ii):
                wait(v, "vector", t_sg)
                ins = v.tensor_tensor(gn_scr(ii).bitcast(BF16)[:, 0:1024],
                                      o_tm_t(ii), bt("g_tm")[:, ii, :], ALU.mult)
                inc(ins, "dve")
            VE(_)
            t_t1 = ("dve", count["dve"])

            def _(te, ii=ii):
                wait(te, "tensor", t_t1)
                pe_banks(te, [4, 5])
                for c in range(8):
                    pslot = 4 if c < 4 else 5
                    transpose(te, gn_scr(ii).bitcast(BF16)[:, 0:1024]
                              [:, 128 * c:128 * (c + 1)],
                              ps[:, 512 * pslot + 128 * (c % 4):
                                 512 * pslot + 128 * (c % 4 + 1)])
                banks_written(inc(te.nop(), "pe"), [4, 5])
            TE(_)

            def _(s, ii=ii):
                read_banks(s, "scalar", [4, 5])
                ins = s.activation(bt("t1sg_fm")[:, 0, :, :]
                                   .rearrange("p c n -> p (c n)"),
                                   ps[:, 512 * 4:512 * 4 + 1024], AF.Copy)
                tk = inc(ins, "act")
                banks_read(tk, [4, 5])
            SE(_)
            t_t1f = ("act", count["act"])

            def _(te, ii=ii):
                wait(te, "tensor", t_t1f)
                pe_banks(te, [4, 5])
                for c in range(8):
                    pslot = 4 if c < 4 else 5
                    transpose(te, bt("g_tm")[:, ii, 128 * c:128 * (c + 1)],
                              ps[:, 512 * pslot + 128 * (c % 4):
                                 512 * pslot + 128 * (c % 4 + 1)])
                banks_written(inc(te.nop(), "pe"), [4, 5])
            TE(_)

            def _(s, ii=ii):
                read_banks(s, "scalar", [4, 5])
                ins = s.activation(bt("t1sg_fm")[:, 1, :, :]
                                   .rearrange("p c n -> p (c n)"),
                                   ps[:, 512 * 4:512 * 4 + 1024], AF.Copy)
                tk = inc(ins, "act")
                banks_read(tk, [4, 5])
            SE(_)
            t_sgf = ("act", count["act"])

            def _(te, ii=ii):
                wait(te, "tensor", t_t1f, t_sgf, wog_tok, wob_tok)
                pe_banks(te, [6, 7])
                for h2 in range(2):
                    for c in range(8):
                        te.matmul(pb(6 + h2), bt("t1sg_fm")[:, 0, c, :],
                                  bt("wsl")[:, 0, c, 512 * h2:512 * (h2 + 1)],
                                  start=(c == 0), stop=False)
                    for c in range(8):
                        ins = te.matmul(pb(6 + h2), bt("t1sg_fm")[:, 1, c, :],
                                        bt("wsl")[:, 1, c, 512 * h2:512 * (h2 + 1)],
                                        start=False, stop=(c == 7))
                banks_written(inc(ins, "pe"), [6, 7])
            TE(_)

            t = tiles[ii]
            ldt = [None]

            def _(sy, ii=ii, t=t):
                wait(sy, "sync", xp_last[ii % 2])
                ldt[0] = dq(sy, "sync", "qs", sb["xp"][:, ii % 2, :],
                            xsrc.rearrange("(n p) d -> n p d", p=128)[t])
            SY(_)

            def _(v, ii=ii, t=t):
                read_banks(v, "vector", [6, 7])
                wait(v, "vector", ldt[0])
                v.tensor_tensor(sb["xp"][:, ii % 2, 0:512], sb["xp"][:, ii % 2, 0:512],
                                pb(6), ALU.add)
                ins = v.tensor_tensor(sb["xp"][:, ii % 2, 512:1024],
                                      sb["xp"][:, ii % 2, 512:1024], pb(7), ALU.add)
                tk = inc(ins, "dve")
                banks_read(tk, [6, 7])
            VE(_)
            t_x2 = ("dve", count["dve"])

            def _(sy, ii=ii, t=t):
                wait(sy, "sync", t_x2)
                stok = dq(sy, "sync", "qs",
                          xdst.rearrange("(n p) d -> n p d", p=128)[t],
                          sb["xp"][:, ii % 2, :])
                st_toks.append(stok)
                xp_last[ii % 2] = stok
            SY(_)

        bstate["arena_free"] = [("dve", count["dve"]), ("pe", count["pe"]),
                                ("act", count["act"])]
        bstate["wsl_free"] = [("pe", count["pe"]), ("pe", count["pe"])]
        return st_toks

    def w_tm(ii):
        return sb["s"][:, 4 + ii // 2, 512 * (ii % 2):512 * (ii % 2) + 512]

    def w_tm_g(gi):
        return w_tm(gi)

    def o_tm_t(ii):
        return sb["s"][:, 6 + ii, :]

    def gn_scr(ii):
        return sb["s"][:, 10 + (ii % 2), :]

    PH = [p for p in ("A", "B", "C") if p in set(phases)]
    s_free = None
    prev_arena_free = None
    for seq in range(NSEQ):
        xsrc = xin[seq]
        for pi, ph in enumerate(PH):
            last = (pi == len(PH) - 1)
            xdst = xout[seq] if last else xscr
            if ph in ("A", "C"):
                s_toks = ln_from_dram(xsrc, s_free, dump_tag=f"s{ph}{seq}")
                attn_state = {"kv_free": prev_arena_free}
                arena_free = prev_arena_free
                for layer in range(NLAYERS_FFN):
                    wt = load_ffn_attn_weights(layer, arena_free)
                    for w in range(4):
                        s_toks2 = ffn_attn_window(seq, layer, w, wt, s_toks,
                                                  attn_state)
                        for i in range(4):
                            s_toks[4 * w + i] = s_toks2[i]
                    if seq == 0 and layer == 0 and ph == "A":
                        dtk = dump(f"sAttn{seq}", sb["s"][:], s_toks)
                        if dtk is not None:
                            s_toks = [dtk] * NT
                    arena_free = list(s_toks)
                    s_toks = ffn_mlp(seq, layer, wt, s_toks, arena_free)
                    arena_free = list(s_toks)
                    if seq == 0 and ph == "A":
                        dtk = dump(f"sL{layer}", sb["s"][:], s_toks)
                        if dtk is not None:
                            s_toks = [dtk] * NT
                st = add_half_s_writeback(xsrc, xdst, s_toks)
                if last:
                    final_toks.extend(st)
                s_free = list(add_half_s_writeback.s_readers)
                add_half_s_writeback.s_readers = []
                prev_arena_free = (list(s_toks) if isinstance(s_toks, list)
                                   else [s_toks])
                xsrc = xdst
            else:
                if state.get("rwkv_prep_tok") is None:
                    state["rwkv_prep_tok"] = rwkv_prep()
                bstate = state.setdefault("bstate", {})
                bstate["arena_free"] = prev_arena_free
                paf = (list(prev_arena_free) if isinstance(prev_arena_free, list)
                       else ([prev_arena_free] if prev_arena_free else []))
                bstate["wsl_free"] = [list(paf), list(paf)]
                bstate["ww2_free"] = list(paf)
                bstate["S0_tok"] = None
                lctok = [None]

                def _(sy):
                    wait(sy, "sync", prev_arena_free)
                    dq(sy, "sync", "qs", bt("Lc")[:, 0, :], C["c_Lex"][:])
                    dq(sy, "sync", "qs", bt("Lc")[:, 1, :], C["c_Lin"][:])
                    dq(sy, "sync", "qs", bt("Lc")[:, 2, :], C["c_Lend"][:])
                    dq(sy, "sync", "qs", bt("Lc")[:, 3, :], C["c_blko"][:])
                    dq(sy, "sync", "qs", bt("Lc")[:, 4, :], C["c_Lex0"][:])
                    dq(sy, "sync", "qs", bt("masks")[0:32, 0, :], C["c_triu01"][:])
                    lctok[0] = dq(sy, "sync", "qs", bt("masks")[0:32, 1, :],
                                  C["c_diag01"][:])
                SY(_)
                state["rwkv_lc_tok"] = lctok[0]

                def _(v):
                    wait(v, "vector", prev_arena_free)
                    v.memset(bt("S0")[:].rearrange("p a b -> p (a b)"), 0.0)
                    ins = v.memset(bt("S0b")[:].rearrange("p a b -> p (a b)"), 0.0)
                    inc(ins, "dve")
                VE(_)
                bstate["S0_tok"] = ("dve", count["dve"])

                urtok = [None]

                def _(g):
                    wait(g, "gpsimd", prev_arena_free, state["rwkv_prep_tok"])
                    urtok[0] = dq(g, "gpsimd", "qg", sb["lnrow"][0:1, 0:512],
                                  W["attn_u"][:]
                                  .rearrange("a b -> (a b)")[None, :])
                GE(_)

                def _(te):
                    wait(te, "tensor", urtok[0], prev_arena_free)
                    pe_banks(te, [6])
                    ins = te.matmul(pb(6, 0, 512), sb["onesr"][0:1, 0:128],
                                    sb["lnrow"][0:1, 0:512],
                                    start=True, stop=True)
                    banks_written(inc(ins, "pe"), [6])
                TE(_)

                def _(v):
                    read_banks(v, "vector", [6])
                    ins = v.tensor_copy(bt("urep")[:], pb(6, 0, 512))
                    tk = inc(ins, "dve")
                    banks_read(tk, [6])
                    bstate["urep_tok"] = tk
                VE(_)
                allst = []
                for qi in range(bq):
                    stq_ = rwkv_quarter(seq, qi, xsrc, xdst, bstate,
                                        state["rwkv_prep_tok"])
                    allst.extend(stq_)
                if last:
                    final_toks.extend(allst)
                s_free = None
                prev_arena_free = bstate["arena_free"]
                xsrc = xdst

    def _(sy):
        wait(sy, "sync", final_toks)
    SY(_)

    for g in reversed(guards):
        g.__exit__(None, None, None)
    block.__exit__ if False else None

    return nc


def kernel(**inputs):
    nc = build_program()
    consts = CONSTS
    n_cores = 8
    in_maps = []
    for c in range(n_cores):
        m = {"xin": np.ascontiguousarray(inputs["L0_attn_x"][2 * c:2 * c + 2])}
        for k, v in inputs.items():
            if k == "L0_attn_x":
                continue
            name = k.replace("L0_", "")
            m[name] = np.ascontiguousarray(v)
        for k, v in consts.items():
            m[k] = v
        in_maps.append(m)
    res = run_bass_kernel_spmd(nc, in_maps, core_ids=list(range(n_cores)))
    out = np.concatenate([r["xout"] for r in res.results], axis=0)
    return out.astype(np.float32)
